# version 05
# baseline (speedup 1.0000x reference)
import contextlib
import numpy as np
import ml_dtypes
import concourse.bass as bass
import concourse.mybir as mybir
from concourse.bass_utils import run_bass_kernel_spmd

F32 = mybir.dt.float32
BF16 = mybir.dt.bfloat16
I32 = mybir.dt.int32
AF = mybir.ActivationFunctionType
ALU = mybir.AluOpType
AX = mybir.AxisListType

NCORES = 8
R = 2048
D = 2048
NT = R // 128
KC = D // 128
FFN_I = 5632
NIC = FFN_I // 128
LN_EPS = 1e-5

ENGINES = ("sync", "gpsimd", "tensor", "vector", "scalar")
DMA_SEM = {"sync": "lq", "gpsimd": "gq", "scalar": "wq", "vector": "vq"}


class P:

    def __init__(self):
        self.nc = bass.Bass("TRN2", target_bir_lowering=False, debug=False)
        self.stk = contextlib.ExitStack()
        self.q = {e: [] for e in ENGINES}
        self.sems = {}
        self.cnt = {}
        self.rel = {}
        self.uid = 0
        self.consts = {}

    def name(self, base):
        self.uid += 1
        return f"{base}_{self.uid}"

    def sem(self, name):
        if name not in self.sems:
            self.sems[name] = self.stk.enter_context(self.nc.semaphore(name))
            self.cnt[name] = 0
        return self.sems[name]

    def sb(self, base, shape, dtype):
        stk = self._scopes[-1] if getattr(self, "_scopes", None) else self.stk
        return stk.enter_context(self.nc.sbuf_tensor(self.name(base), shape, dtype))

    def ps(self, base, shape, dtype=F32):
        stk = self._scopes[-1] if getattr(self, "_scopes", None) else self.stk
        return stk.enter_context(self.nc.psum_tensor(self.name(base), shape, dtype))

    @contextlib.contextmanager
    def block_scope(self):
        if not hasattr(self, "_scopes"):
            self._scopes = []
        blk = contextlib.ExitStack()
        self._scopes.append(blk)
        try:
            yield
        except BaseException:
            self._scopes.pop()
            raise
        else:
            self._scopes.pop()
            blk.close()
            self.barrier(sems=tuple(s for s in self.cnt))

    def const2sb_blk(self, ap, shape, dtype, base, sem):
        t = self.sb(base, shape, dtype)
        self.dma("sync", sem, t[:], ap)
        return t

    def wait_blk_consts(self, sem):
        for eng in ("tensor", "vector", "scalar"):
            self.wait(eng, sem, self.cnt[sem])

    def dram(self, base, shape, dtype, kind="Internal"):
        if kind == "Internal" and base in getattr(self, "dbg_out", ()):
            ap = self.nc.dram_tensor("dbg_" + base, shape, dtype, kind="ExternalOutput").ap()
            return ap
        return self.nc.dram_tensor(self.name(base), shape, dtype, kind=kind).ap()

    def on(self, eng, fn):
        self.q[eng].append(fn)

    def phase(self, name):
        for e in ENGINES:
            self.q[e].append(("phase", name))

    def inc(self, eng, fn, sem, n=1):
        h = self.sem(sem)
        self.cnt[sem] += n
        self.q[eng].append(lambda e, fn=fn, h=h, n=n: fn(e).then_inc(h, n))
        return self.cnt[sem]

    def wait(self, eng, sem, val=None):
        h = self.sem(sem)
        if val is None:
            val = self.cnt[sem]
        if val > 0:
            self.q[eng].append(lambda e, h=h, v=val: e.wait_ge(h, v))

    def dma(self, eng, sem, out, in_, **kw):
        return self.inc(eng, lambda e, o=out, i=in_, kw=kw: e.dma_start(out=o, in_=i, **kw), sem, 16)

    def release(self, key, sem):
        self.rel[key] = [(sem, self.cnt[sem])]

    def release_add(self, key, sem):
        self.rel.setdefault(key, []).append((sem, self.cnt[sem]))

    def acquire(self, eng, key):
        for sem, val in self.rel.get(key, ()):
            self.wait(eng, sem, val)

    def barrier(self, engines=ENGINES, sems=("te", "ve", "se")):
        for eng in engines:
            for s in sems:
                if s in self.cnt and self.cnt[s] > 0:
                    self.wait(eng, s, self.cnt[s])

    def mm_group(self, psum, pairs, sem="te"):
        n = len(pairs)
        for j, (l, r) in enumerate(pairs):
            s, t = j == 0, j == n - 1
            if t:
                v = self.inc("tensor", lambda e, o=psum, l=l, r=r, s=s: e.matmul(o, l, r, start=s, stop=True), sem)
            else:
                self.on("tensor", lambda e, o=psum, l=l, r=r, s=s: e.matmul(o, l, r, start=s, stop=False))
        return v

    def const2sb(self, ap, shape, dtype, base="const"):
        t = self.sb(base, shape, dtype)
        self.dma("sync", "dcn", t[:], ap)
        return t

    def wait_consts(self, eng):
        self.wait(eng, "dcn", self.cnt["dcn"])

    def build(self):
        with self.stk:
            with self.nc.Block() as block:
                for eng in ENGINES:
                    dec = getattr(block, eng)
                    fns = self.q[eng]

                    def body(e, fns=fns):
                        cur = contextlib.ExitStack()
                        try:
                            for f in fns:
                                if isinstance(f, tuple) and f[0] == "phase":
                                    cur.close()
                                    cur = contextlib.ExitStack()
                                    cur.enter_context(self.nc.named_scope(f[1]))
                                else:
                                    f(e)
                        finally:
                            cur.close()
                    dec(body)
        return self.nc



NCONST = 8
C_LNEPS, C_L2EPS, C_ONE, C_SCALE = 0, 1, 2, 3


def emit_setup(p, cvec, identc, onesc, pswapc=None, onesfc=None):
    p.cv = p.const2sb(cvec, [128, NCONST], F32, "cvec")
    p.ident = p.const2sb(identc, [128, 128], BF16, "ident")
    p.ones = p.const2sb(onesc, [128, 128], BF16, "ones")
    if pswapc is not None:
        p.pswap = p.const2sb(pswapc, [128, 128], BF16, "pswap")
    if onesfc is not None:
        p.onesf = p.const2sb(onesfc, [1, 128], F32, "onesf")


def emit_ln(p, xin, h_tm_d, n_tiles=NT):
    ph = contextlib.ExitStack()
    S = lambda b, s, d: ph.enter_context(p.nc.sbuf_tensor(p.name(b), s, d))
    xt = [S("ln_x", [128, D], F32) for _ in range(2)]
    ht = [S("ln_h", [128, D], BF16) for _ in range(2)]
    dump1 = S("ln_dump1", [128, D], BF16)
    dump2 = S("ln_dump2", [128, D], BF16)
    ssum = S("ln_ssum", [128, 1], F32)
    ssq = S("ln_ssq", [128, 1], F32)
    mean = S("ln_mean", [128, 1], F32)
    var = S("ln_var", [128, 1], F32)
    msq = S("ln_msq", [128, 1], F32)
    rstd = S("ln_rstd", [128, 1], F32)

    for tt in range(n_tiles):
        b = tt % 2
        xs = "da" if b == 0 else "db"
        p.acquire("sync", f"ln_x{b}")
        lv = p.dma("sync", xs, xt[b][:], xin[128 * tt:128 * (tt + 1), :])
        p.wait("vector", xs, lv)
        p.wait("scalar", xs, lv)
        sv = p.inc("vector", lambda e, o=dump1, i=xt[b], s=ssum: e.tensor_scalar(
            o[:], i[:], 1.0, 0.0, op0=ALU.mult, op1=ALU.add, accum_out=s[:]), "ve")
        sq = p.inc("scalar", lambda e, o=dump2, i=xt[b], s=ssq: e.activation(
            o[:], i[:], AF.Square, accum_out=s[:]), "se")
        p.wait("vector", "se", sq)
        p.wait("vector", "ve", sv)
        mv = p.inc("vector", lambda e, m=mean, s=ssum: e.tensor_scalar(
            m[:], s[:], 1.0 / D, None, op0=ALU.mult), "ve")
        p.inc("vector", lambda e, v=var, s=ssq: e.tensor_scalar(
            v[:], s[:], 1.0 / D, None, op0=ALU.mult), "ve")
        p.wait("vector", "ve", mv)
        mqv = p.inc("vector", lambda e, m=msq, a=mean: e.tensor_tensor(
            m[:], a[:], a[:], op=ALU.mult), "ve")
        p.wait("vector", "ve", mqv)
        vv = p.inc("vector", lambda e, v=var, m=msq: e.tensor_tensor(
            v[:], v[:], m[:], op=ALU.subtract), "ve")
        p.wait("scalar", "ve", vv)
        sqv = p.inc("scalar", lambda e, v=var, r=rstd, c=p.cv: e.activation(
            r[:], v[:], AF.Sqrt, bias=c[:, C_LNEPS:C_LNEPS + 1]), "se")
        p.wait("vector", "se", sqv)
        rv = p.inc("vector", lambda e, r=rstd: e.reciprocal(r[:], r[:]), "ve")
        p.acquire("vector", f"ln_h{b}")
        p.wait("vector", "ve", rv)
        av = p.inc("vector", lambda e, h=ht[b], x=xt[b], m=mean, r=rstd: e.tensor_scalar(
            h[:], x[:], m[:], r[:], op0=ALU.subtract, op1=ALU.mult), "ve")
        p.release(f"ln_x{b}", "ve")
        p.wait("scalar", "ve", av)
        ws = "dwa" if b == 0 else "dwb"
        wv = p.dma("scalar", ws, h_tm_d[128 * tt:128 * (tt + 1), :], ht[b][:])
        p.release(f"ln_h{b}", ws)
    ph.close()


def emit_load_h_fm(p, h_tm_d, hf, t0, tcols):
    for kc in range(KC):
        p.dma("sync", "dtot", hf[kc][:, 0:tcols],
              h_tm_d[t0:t0 + tcols, 128 * kc:128 * (kc + 1)], transpose=True)
    return p.cnt["dtot"]



def _emit_ffn_impl(p, xin, xout, gwT, uwT, dwT, gsc, usc, dsc, zero_point=False):
    h_tm_d = p.dram("hL0f", [R, D], BF16)
    hdn_d = p.dram("hdnL0f", [NIC, 128, R], BF16)
    for eng in ("tensor", "vector", "scalar"):
        p.wait_consts(eng)

    p.phase("ffn_ln")
    emit_ln(p, xin, h_tm_d)

    p.phase("ffn_gateup")
    dtm_all = p.sb("dtm", [128, NT * D], BF16)
    dtm = [dtm_all[:, D * t:D * (t + 1)] for t in range(NT)]
    ident = p.ident

    with contextlib.ExitStack() as ph:
        hf_all = ph.enter_context(p.nc.sbuf_tensor(p.name("hf"), [128, KC * R], BF16))
        hf = [hf_all[:, R * kc:R * (kc + 1)] for kc in range(KC)]
        NWB = 4
        gw = [ph.enter_context(p.nc.sbuf_tensor(p.name("gw"), [128, KC, 128], BF16)) for _ in range(NWB)]
        uw = [ph.enter_context(p.nc.sbuf_tensor(p.name("uw"), [128, KC, 128], BF16)) for _ in range(NWB)]
        sg = [ph.enter_context(p.nc.sbuf_tensor(p.name("sg"), [128, 512], BF16)) for _ in range(2)]
        hdn_t = [ph.enter_context(p.nc.sbuf_tensor(p.name("hdn"), [128, R], BF16)) for _ in range(2)]
        pg = [ph.enter_context(p.nc.psum_tensor(p.name("pg"), [128, 512], F32)) for _ in range(2)]
        pu = [ph.enter_context(p.nc.psum_tensor(p.name("pu"), [128, 512], F32)) for _ in range(2)]

        p.wait("sync", "dwa", p.cnt["dwa"])
        p.wait("sync", "dwb", p.cnt["dwb"])
        hv = emit_load_h_fm(p, h_tm_d, hf, 0, R)
        p.wait("tensor", "dtot", p.cnt["dtot"])

        sbc = None
        if zero_point:
            sbc_all = ph.enter_context(p.nc.sbuf_tensor(p.name("sbc"), [128, R], BF16))
            sbc = [sbc_all[:, 512 * q:512 * (q + 1)] for q in range(4)]
            psb = ph.enter_context(p.nc.psum_tensor(p.name("psb"), [128, 512], F32))
            for q in range(4):
                tv = p.mm_group(psb[:], [(p.ones[:], hf[kc][:, 512 * q:512 * (q + 1)])
                                         for kc in range(KC)])
                p.wait("scalar", "te", tv)
                sv = p.inc("scalar", lambda e, o=sbc[q], i_=psb: e.activation(
                    o, i_[:], AF.Copy, scale=-1.0), "se")
                p.wait("tensor", "se", sv)

        WSEM = ["dc", "dd", "dc2", "dd2"]

        def gu_load(i):
            bi = i % NWB
            ws = WSEM[bi]
            p.acquire("gpsimd", f"gw{bi}")
            p.dma("gpsimd", ws, gw[bi][:],
                  gwT[:, 128 * i:128 * (i + 1)].rearrange("(kc p) j -> p kc j", p=128))
            return p.dma("gpsimd", ws, uw[bi][:],
                         uwT[:, 128 * i:128 * (i + 1)].rearrange("(kc p) j -> p kc j", p=128))

        pw_ = [gu_load(i) for i in range(min(3, NIC))]
        for i in range(NIC):
            bi = i % NWB
            ws = WSEM[bi]
            gv = pw_[i]
            if i + 3 < NIC:
                pw_.append(gu_load(i + 3))
            p.wait("tensor", ws, gv)
            p.acquire("vector", f"hdn{i % 2}")
            for q in range(4):
                bq = q % 2
                zp = [(p.ones[:], sbc[q])] if zero_point else []
                p.acquire("tensor", f"pg{bq}")
                tg = p.mm_group(pg[bq][:], [(gw[bi][:, kc, :], hf[kc][:, 512 * q:512 * (q + 1)])
                                            for kc in range(KC)] + zp)
                p.acquire("tensor", f"pu{bq}")
                tu = p.mm_group(pu[bq][:], [(uw[bi][:, kc, :], hf[kc][:, 512 * q:512 * (q + 1)])
                                            for kc in range(KC)] + zp)
                p.wait("scalar", "te", tg)
                p.acquire("scalar", f"sg{bq}")
                ev = p.inc("scalar", lambda e, o=sg[bq], i_=pg[bq], s=gsc[:, i:i + 1]:
                           e.activation(o[:], i_[:], AF.Silu, scale=s), "se")
                p.release(f"pg{bq}", "se")
                p.wait("vector", "te", tu)
                p.wait("vector", "se", ev)
                p.inc("vector", lambda e, o=hdn_t[i % 2][:, 512 * q:512 * (q + 1)], i_=pu[bq],
                      s=usc[:, i:i + 1], g=sg[bq]: e.scalar_tensor_tensor(
                          o, i_[:], s, g[:], op0=ALU.mult, op1=ALU.mult), "ve")
                p.release(f"pu{bq}", "ve")
                p.release(f"sg{bq}", "ve")
            p.release(f"gw{bi}", "te")
            p.wait("scalar", "ve", p.cnt["ve"])
            hs = "dwa" if i % 2 == 0 else "dwb"
            p.dma("scalar", hs, hdn_d[i, :, :], hdn_t[i % 2][:])
            p.release(f"hdn{i % 2}", hs)

    p.phase("ffn_down")
    with contextlib.ExitStack() as ph:
        hh_all = ph.enter_context(p.nc.sbuf_tensor(p.name("hh"), [128, NIC * 1024], BF16))
        hh = [hh_all[:, 1024 * i:1024 * (i + 1)] for i in range(NIC)]
        dw = [ph.enter_context(p.nc.sbuf_tensor(p.name("dw"), [128, NIC, 128], BF16)) for _ in range(4)]
        dd = [ph.enter_context(p.nc.sbuf_tensor(p.name("dd"), [128, 512], BF16)) for _ in range(2)]
        pd = [ph.enter_context(p.nc.psum_tensor(p.name("pd"), [128, 512], F32)) for _ in range(2)]
        pt = [ph.enter_context(p.nc.psum_tensor(p.name("pt"), [128, 128], BF16)) for _ in range(2)]

        sbh_all = ph.enter_context(p.nc.sbuf_tensor(p.name("sbh"), [128, 1024], BF16)) if zero_point else None
        psb2 = ph.enter_context(p.nc.psum_tensor(p.name("psb2"), [128, 512], F32)) if zero_point else None
        for half in range(2):
            p.wait("gpsimd", "dwa", p.cnt["dwa"])
            p.wait("gpsimd", "dwb", p.cnt["dwb"])
            p.acquire("gpsimd", "hh")
            for i in range(NIC):
                p.dma("gpsimd", "dgt", hh[i][:], hdn_d[i, :, 1024 * half:1024 * (half + 1)])
            hv = p.cnt["dgt"]
            p.wait("tensor", "dgt", hv)
            if zero_point:
                sbh = [sbh_all[:, 512 * q:512 * (q + 1)] for q in range(2)]
                p.wait("tensor", "se", p.cnt["se"])
                for q in range(2):
                    tv = p.mm_group(psb2[:], [(p.ones[:], hh[i][:, 512 * q:512 * (q + 1)])
                                              for i in range(NIC)])
                    p.wait("scalar", "te", tv)
                    sv = p.inc("scalar", lambda e, o=sbh[q], i_=psb2: e.activation(
                        o, i_[:], AF.Copy, scale=-1.0), "se")
                    p.wait("tensor", "se", sv)
            def dw_load(d):
                bd = d % 4
                ws = WSEM[bd]
                p.acquire("gpsimd", f"dwb{bd}")
                return p.dma("gpsimd", ws, dw[bd][:],
                             dwT[:, 128 * d:128 * (d + 1)].rearrange("(ic p) j -> p ic j", p=128))

            pdw = [dw_load(d) for d in range(3)]
            for d in range(KC):
                bd = d % 4
                ws = WSEM[bd]
                wv = pdw[d]
                if d + 3 < KC:
                    pdw.append(dw_load(d + 3))
                p.wait("tensor", ws, wv)
                for q in range(2):
                    bq = q % 2
                    zp = [(p.ones[:], sbh[q])] if zero_point else []
                    p.acquire("tensor", f"pd{bq}")
                    tv = p.mm_group(pd[bq][:], [(dw[bd][:, i, :], hh[i][:, 512 * q:512 * (q + 1)])
                                                for i in range(NIC)] + zp)
                    p.wait("scalar", "te", tv)
                    p.acquire("scalar", f"dd{bq}")
                    sv = p.inc("scalar", lambda e, o=dd[bq], i_=pd[bq], s=dsc[:, d:d + 1]:
                               e.activation(o[:], i_[:], AF.Copy, scale=s), "se")
                    p.release(f"pd{bq}", "se")
                    p.wait("tensor", "se", sv)
                    for j in range(4):
                        bj = j % 2
                        tt_i = half * 8 + q * 4 + j
                        p.acquire("tensor", f"pt{bj}")
                        tpv = p.inc("tensor", lambda e, o=pt[bj], i_=dd[bq][:, 128 * j:128 * (j + 1)],
                                    id_=ident: e.transpose(o[:], i_, id_[:]), "te")
                        p.wait("vector", "te", tpv)
                        p.inc("vector", lambda e, o=dtm[tt_i][:, 128 * d:128 * (d + 1)], i_=pt[bj]:
                              e.tensor_copy(o, i_[:]), "ve")
                        p.release(f"pt{bj}", "ve")
                    p.release(f"dd{bq}", "te")
                p.release(f"dwb{bd}", "te")
            p.release("hh", "te")

    rph = contextlib.ExitStack()
    xr = [rph.enter_context(p.nc.sbuf_tensor(p.name("xr"), [128, D], F32)) for _ in range(2)]
    xo = [rph.enter_context(p.nc.sbuf_tensor(p.name("xo"), [128, D], F32)) for _ in range(2)]
    p.wait("sync", "ve", p.cnt["ve"])
    p.wait("vector", "ve", p.cnt["ve"])
    for tt in range(NT):
        b = tt % 2
        xs = "da" if b == 0 else "db"
        ws = "dwa" if b == 0 else "dwb"
        p.acquire("sync", f"res_x{b}")
        lv = p.dma("sync", xs, xr[b][:], xin[128 * tt:128 * (tt + 1), :])
        p.wait("vector", xs, lv)
        p.acquire("vector", f"res_o{b}")
        p.inc("vector", lambda e, o=xo[b], a=xr[b], dl=dtm[tt]: e.tensor_tensor(
            o[:], a[:], dl[:], op=ALU.add), "ve")
        p.release(f"res_x{b}", "ve")
        p.wait("scalar", "ve", p.cnt["ve"])
        p.dma("scalar", ws, xout[128 * tt:128 * (tt + 1), :], xo[b][:])
        p.release(f"res_o{b}", ws)
    rph.close()



def emit_ln2_l1(p, xin, X_d, h2_d, lnwb, lnbb):
    ph = contextlib.ExitStack()
    S = lambda b, s, d: ph.enter_context(p.nc.sbuf_tensor(p.name(b), s, d))
    xt = [S("l1x", [128, D], F32) for _ in range(2)]
    Xt = [S("l1X", [128, D], F32) for _ in range(2)]
    ht = [S("l1h", [128, D], BF16) for _ in range(2)]
    tmp = S("l1tmp", [128, D], F32)
    dump1 = S("l1d1", [128, D], BF16)
    dump2 = S("l1d2", [128, D], BF16)
    ssum = S("l1ssum", [128, 1], F32)
    ssq = S("l1ssq", [128, 1], F32)
    mean = S("l1mean", [128, 1], F32)
    var = S("l1var", [128, 1], F32)
    msq = S("l1msq", [128, 1], F32)
    rstd = S("l1rstd", [128, 1], F32)

    def stats(src_tile):
        sv = p.inc("vector", lambda e, o=dump1, i=src_tile, s=ssum: e.tensor_scalar(
            o[:], i[:], 1.0, 0.0, op0=ALU.mult, op1=ALU.add, accum_out=s[:]), "ve")
        sq = p.inc("scalar", lambda e, o=dump2, i=src_tile, s=ssq: e.activation(
            o[:], i[:], AF.Square, accum_out=s[:]), "se")
        p.wait("vector", "se", sq)
        p.wait("vector", "ve", sv)
        mv = p.inc("vector", lambda e, m=mean, s=ssum: e.tensor_scalar(
            m[:], s[:], 1.0 / D, None, op0=ALU.mult), "ve")
        p.inc("vector", lambda e, v=var, s=ssq: e.tensor_scalar(
            v[:], s[:], 1.0 / D, None, op0=ALU.mult), "ve")
        p.wait("vector", "ve", mv)
        mqv = p.inc("vector", lambda e, m=msq, a=mean: e.tensor_tensor(
            m[:], a[:], a[:], op=ALU.mult), "ve")
        p.wait("vector", "ve", mqv)
        vv = p.inc("vector", lambda e, v=var, m=msq: e.tensor_tensor(
            v[:], v[:], m[:], op=ALU.subtract), "ve")
        p.wait("scalar", "ve", vv)
        sqv = p.inc("scalar", lambda e, v=var, r=rstd, c=p.cv: e.activation(
            r[:], v[:], AF.Sqrt, bias=c[:, C_LNEPS:C_LNEPS + 1]), "se")
        p.wait("vector", "se", sqv)
        rv = p.inc("vector", lambda e, r=rstd: e.reciprocal(r[:], r[:]), "ve")
        p.wait("vector", "ve", rv)

    for tt in range(NT):
        b = tt % 2
        xs = "da" if b == 0 else "db"
        ws = "dwa" if b == 0 else "dwb"
        p.acquire("sync", f"l1x{b}")
        lv = p.dma("sync", xs, xt[b][:], xin[128 * tt:128 * (tt + 1), :])
        p.wait("vector", xs, lv)
        p.wait("scalar", xs, lv)
        stats(xt[b])
        p.acquire("vector", f"l1X{b}")
        xv = p.inc("vector", lambda e, o=Xt[b], x=xt[b], m=mean, r=rstd: e.tensor_scalar(
            o[:], x[:], m[:], r[:], op0=ALU.subtract, op1=ALU.mult), "ve")
        p.release(f"l1x{b}", "ve")
        p.wait("scalar", "ve", xv)
        p.dma("scalar", ws, X_d[128 * tt:128 * (tt + 1), :], Xt[b][:])
        p.wait("vector", "ve", xv)
        stats(Xt[b])
        nv = p.inc("vector", lambda e, o=tmp, x=Xt[b], m=mean, r=rstd: e.tensor_scalar(
            o[:], x[:], m[:], r[:], op0=ALU.subtract, op1=ALU.mult), "ve")
        p.release(f"l1X{b}", "ve")
        p.release_add(f"l1X{b}", ws)
        p.wait("vector", "ve", nv)
        wv_ = p.inc("vector", lambda e, o=tmp, w=lnwb: e.tensor_tensor(
            o[:], o[:], w[:], op=ALU.mult), "ve")
        p.wait("vector", "ve", wv_)
        p.acquire("vector", f"l1h{b}")
        hv = p.inc("vector", lambda e, o=ht[b], t=tmp, bb=lnbb: e.tensor_tensor(
            o[:], t[:], bb[:], op=ALU.add), "ve")
        p.wait("scalar", "ve", hv)
        ws2 = "dwc" if b == 0 else "dwd"
        p.dma("scalar", ws2, h2_d[128 * tt:128 * (tt + 1), :], ht[b][:])
        p.release(f"l1h{b}", ws2)
    ph.close()


def _emit_l1_attn_impl(p, xin, xout, WinAp, WoutAp, binc, boutc, lnwb, lnbb,
                 costab, sintab, onesf, replica_groups):
    X_d = p.dram("Xl1", [R, D], F32)
    h2_d = p.dram("h2l1", [R, D], BF16)
    Q_d = p.dram("Ql1", [KC, 128, R], BF16)
    ccin = p.dram("ccin", [128, KC], F32)
    ccout = p.dram("ccout", [128, KC], F32)
    for eng in ("tensor", "vector", "scalar"):
        p.wait_consts(eng)

    p.phase("l1a_ln")
    emit_ln2_l1(p, xin, X_d, h2_d, lnwb, lnbb)

    dtm_all = p.sb("l1dtm", [128, NT * D], BF16)
    dtm = [dtm_all[:, D * t:D * (t + 1)] for t in range(NT)]
    accA = p.sb("accA", [128, KC], F32)
    zv = p.inc("vector", lambda e, o=accA: e.memset(o[:], 0.0), "ve")

    p.phase("l1a_p1")
    with contextlib.ExitStack() as ph:
        S = lambda b, s, d: ph.enter_context(p.nc.sbuf_tensor(p.name(b), s, d))
        hf2 = [S("hf2", [128, 512], BF16) for _ in range(KC)]
        qsb = [S("qsb", [128, 512], BF16) for _ in range(KC)]
        ksb = [S("ksb", [128, 512], BF16) for _ in range(KC)]
        vsb = [S("vsb", [128, 512], BF16) for _ in range(KC)]
        ctb = [S("ctb", [128, 512], BF16) for _ in range(8)]
        stb = [S("stb", [128, 512], BF16) for _ in range(8)]
        wcol = [S("wcol", [128, KC, 128], BF16) for _ in range(2)]
        rsc1 = S("rsc1", [128, 512], F32)
        rsc2 = S("rsc2", [128, 512], F32)
        ksq = [S("ksq", [128, 512], BF16) for _ in range(2)]
        knrm = S("knrm", [128, 512], BF16)
        n2row = S("n2row", [1, 512], F32)
        rdump = S("rdump", [128, 512], F32)
        rcol = S("rcol", [128, 1], F32)
        pqkv = [ph.enter_context(p.nc.psum_tensor(p.name("pqkv"), [128, 512], F32)) for _ in range(2)]
        prh = ph.enter_context(p.nc.psum_tensor(p.name("prh"), [128, 512], F32))
        pn2 = ph.enter_context(p.nc.psum_tensor(p.name("pn2"), [1, 512], F32))
        prb = ph.enter_context(p.nc.psum_tensor(p.name("prb"), [128, 512], F32))

        for q in range(4):
            t0 = 512 * q
            p.wait("sync", "dwa", p.cnt["dwa"])
            p.wait("sync", "dwb", p.cnt["dwb"])
            p.wait("sync", "dwc", p.cnt["dwc"])
            p.wait("sync", "dwd", p.cnt["dwd"])
            for kc in range(KC):
                p.acquire("sync", f"hf2_{kc}")
                p.dma("sync", "dtot", hf2[kc][:], h2_d[t0:t0 + 512, 128 * kc:128 * (kc + 1)],
                      transpose=True)
            for c in range(8):
                p.acquire("sync", f"rt{c}")
                p.dma("sync", "dtot", ctb[c][:], costab[128 * c:128 * (c + 1), t0:t0 + 512])
                p.dma("sync", "dtot", stb[c][:], sintab[128 * c:128 * (c + 1), t0:t0 + 512])
            hv = p.cnt["dtot"]
            p.wait("tensor", "dtot", hv)
            p.wait("vector", "dtot", hv)
            for oc in range(3 * KC):
                bw = oc % 2
                ws = "dc" if bw == 0 else "dd"
                p.acquire("gpsimd", f"wcol{bw}")
                wv = p.dma("gpsimd", ws, wcol[bw][:],
                           WinAp[:, 128 * oc:128 * (oc + 1)].rearrange("(kc p) j -> p kc j", p=128))
                p.wait("tensor", ws, wv)
                p.acquire("tensor", f"pqkv{bw}")
                tv = p.mm_group(pqkv[bw][:], [(wcol[bw][:, kc, :], hf2[kc][:]) for kc in range(KC)])
                p.release(f"wcol{bw}", "te")
                dst = (qsb if oc < KC else ksb if oc < 2 * KC else vsb)[oc % KC]
                p.wait("scalar", "te", tv)
                p.acquire("scalar", f"l1t_{oc % KC}_{oc // KC}")
                sv = p.inc("scalar", lambda e, o=dst, i_=pqkv[bw], bc=binc[:, oc:oc + 1]:
                           e.activation(o[:], i_[:], AF.Identity, bias=bc), "se")
                p.release(f"pqkv{bw}", "se")
            sev = p.cnt["se"]
            v3 = 0
            for tiles in (qsb, ksb):
                for c in range(8):
                    p.wait("tensor", "se", sev)
                    p.acquire("tensor", "prh")
                    rv = p.inc("tensor", lambda e, o=prh, t=tiles[c], pw=p.pswap:
                               e.matmul(o[:], pw[:], t[:], start=True, stop=True), "te")
                    p.wait("vector", "te", rv)
                    p.wait("vector", "ve", v3)
                    v1 = p.inc("vector", lambda e, o=rsc1, i_=prh, s=stb[c]: e.tensor_tensor(
                        o[:], i_[:], s[:], op=ALU.mult), "ve")
                    p.release("prh", "ve")
                    v2 = p.inc("vector", lambda e, o=rsc2, t=tiles[c], ct=ctb[c]: e.tensor_tensor(
                        o[:], t[:], ct[:], op=ALU.mult), "ve")
                    p.wait("vector", "ve", v2)
                    v3 = p.inc("vector", lambda e, o=tiles[c], a=rsc1, bb=rsc2: e.tensor_tensor(
                        o[:], a[:], bb[:], op=ALU.add), "ve")
            p.wait("scalar", "ve", p.cnt["ve"])
            for kc in range(KC):
                p.dma("scalar", "dwa", Q_d[kc, :, t0:t0 + 512], qsb[kc][:])
            qspill = p.cnt["dwa"]
            vev = p.cnt["ve"]
            p.acquire("tensor", "pn2")
            for c in range(KC):
                bw = c % 2
                p.wait("scalar", "ve", vev)
                p.acquire("scalar", f"ksq{bw}")
                qv = p.inc("scalar", lambda e, o=ksq[bw], i_=ksb[c]: e.activation(
                    o[:], i_[:], AF.Square), "se")
                p.wait("tensor", "se", qv)
                fn = lambda e, o=pn2, l=p.ones[:, 0:1], r=ksq[bw], s=(c == 0), t=(c == KC - 1): \
                    e.matmul(o[:], l, r[:], start=s, stop=t)
                tv = p.inc("tensor", fn, "te")
                p.release(f"ksq{bw}", "te")
            p.wait("vector", "te", p.cnt["te"])
            nv = p.inc("vector", lambda e, o=n2row, i_=pn2: e.tensor_copy(o[:], i_[:]), "ve")
            p.wait("scalar", "ve", nv)
            sv = p.inc("scalar", lambda e, o=n2row: e.activation(o[:], o[:], AF.Sqrt), "se")
            p.wait("vector", "se", sv)
            rv = p.inc("vector", lambda e, o=n2row: e.reciprocal(o[:], o[:]), "ve")
            p.wait("tensor", "ve", rv)
            bv = p.inc("tensor", lambda e, o=prb, l=onesf[:], r=n2row: e.matmul(
                o[:], l, r[:], start=True, stop=True), "te")
            p.wait("vector", "te", bv)
            for c in range(KC):
                k1 = p.inc("vector", lambda e, o=knrm, k=ksb[c], r=prb: e.tensor_tensor(
                    o[:], k[:], r[:], op=ALU.mult), "ve")
                p.wait("vector", "ve", k1)
                k2 = p.inc("vector", lambda e, o=rdump, k=knrm, v=vsb[c]: e.tensor_tensor(
                    o[:], k[:], v[:], op=ALU.mult), "ve")
                p.wait("vector", "ve", k2)
                k2 = p.inc("vector", lambda e, o=rcol, i_=rdump: e.tensor_reduce(
                    o[:], i_[:], axis=AX.X, op=ALU.add), "ve")
                p.wait("vector", "ve", k2)
                k3 = p.inc("vector", lambda e, o=accA[:, c:c + 1], rc=rcol: e.tensor_tensor(
                    o, o, rc[:], op=ALU.add), "ve")
            vv_all = p.cnt["ve"]
            p.release("pn2", "ve")
            for kc in range(KC):
                p.rel[f"hf2_{kc}"] = [("te", p.cnt["te"])]
                p.rel[f"l1t_{kc}_0"] = [("dwa", qspill)]
                p.rel[f"l1t_{kc}_1"] = [("ve", vv_all)]
                p.rel[f"l1t_{kc}_2"] = [("ve", vv_all)]
            for c in range(8):
                p.rel[f"rt{c}"] = [("ve", vv_all)]

    p.wait("gpsimd", "ve", p.cnt["ve"])
    p.dma("gpsimd", "dcc", ccin, accA[:])
    p.wait("gpsimd", "dcc", p.cnt["dcc"])
    p.inc("gpsimd", lambda e, i_=ccin, o_=ccout, rg=replica_groups: e.collective_compute(
        "AllReduce", ALU.add, replica_groups=rg, ins=[i_.opt()], outs=[o_.opt()]), "dcl", 1)
    p.wait("gpsimd", "dcl", p.cnt["dcl"])
    p.dma("gpsimd", "dcc", accA[:], ccout)
    ccv = p.cnt["dcc"]
    p.wait("vector", "dcc", ccv)

    p.phase("l1a_p2")
    p.barrier(sems=("te", "ve", "se", "dwa", "dwb", "dwc", "dwd"))
    with contextlib.ExitStack() as ph:
        S = lambda b, s, d: ph.enter_context(p.nc.sbuf_tensor(p.name(b), s, d))
        ysb = [S("ysb", [128, 512], BF16) for _ in range(KC)]
        wcol = [S("wcol2", [128, KC, 128], BF16) for _ in range(2)]
        dd = [S("l1dd", [128, 512], BF16) for _ in range(2)]
        pw = [ph.enter_context(p.nc.psum_tensor(p.name("pw"), [128, 512], F32)) for _ in range(2)]
        pt = [ph.enter_context(p.nc.psum_tensor(p.name("pt2"), [128, 128], BF16)) for _ in range(2)]
        for q in range(4):
            t0 = 512 * q
            p.wait("sync", "dwa", p.cnt["dwa"])
            p.wait("sync", "dwb", p.cnt["dwb"])
            for kc in range(KC):
                p.acquire("sync", f"ysb{kc}")
                p.dma("sync", "dtot", ysb[kc][:], Q_d[kc, :, t0:t0 + 512])
            hv = p.cnt["dtot"]
            p.wait("vector", "dtot", hv)
            for kc in range(KC):
                p.inc("vector", lambda e, o=ysb[kc], a=accA[:, kc:kc + 1]: e.tensor_scalar(
                    o[:], o[:], a, None, op0=ALU.mult), "ve")
            yv = p.cnt["ve"]
            p.wait("tensor", "ve", yv)
            for oc in range(KC):
                bw = oc % 2
                ws = "dc" if bw == 0 else "dd"
                p.acquire("gpsimd", f"wc2{bw}")
                wv = p.dma("gpsimd", ws, wcol[bw][:],
                           WoutAp[:, 128 * oc:128 * (oc + 1)].rearrange("(kc p) j -> p kc j", p=128))
                p.wait("tensor", ws, wv)
                p.acquire("tensor", f"pw{bw}")
                tv = p.mm_group(pw[bw][:], [(wcol[bw][:, kc, :], ysb[kc][:]) for kc in range(KC)])
                p.release(f"wc2{bw}", "te")
                p.wait("scalar", "te", tv)
                p.acquire("scalar", f"l1dd{bw}")
                sv = p.inc("scalar", lambda e, o=dd[bw], i_=pw[bw], bc=boutc[:, oc:oc + 1]:
                           e.activation(o[:], i_[:], AF.Identity, bias=bc), "se")
                p.release(f"pw{bw}", "se")
                p.wait("tensor", "se", sv)
                for j in range(4):
                    bj = j % 2
                    tt_i = q * 4 + j
                    p.acquire("tensor", f"pt2{bj}")
                    tpv = p.inc("tensor", lambda e, o=pt[bj], i_=dd[bw][:, 128 * j:128 * (j + 1)],
                                id_=p.ident: e.transpose(o[:], i_, id_[:]), "te")
                    p.wait("vector", "te", tpv)
                    p.inc("vector", lambda e, o=dtm[tt_i][:, 128 * oc:128 * (oc + 1)], i_=pt[bj]:
                          e.tensor_copy(o, i_[:]), "ve")
                    p.release(f"pt2{bj}", "ve")
                p.release(f"l1dd{bw}", "te")
            for kc in range(KC):
                p.release(f"ysb{kc}", "te")

    rph = contextlib.ExitStack()
    xr = [rph.enter_context(p.nc.sbuf_tensor(p.name("xr"), [128, D], F32)) for _ in range(2)]
    Xr = [rph.enter_context(p.nc.sbuf_tensor(p.name("Xr"), [128, D], F32)) for _ in range(2)]
    xo = [rph.enter_context(p.nc.sbuf_tensor(p.name("xo"), [128, D], F32)) for _ in range(2)]
    p.wait("sync", "ve", p.cnt["ve"])
    p.wait("vector", "ve", p.cnt["ve"])
    p.wait("gpsimd", "ve", p.cnt["ve"])
    for tt in range(NT):
        b = tt % 2
        xs = "da" if b == 0 else "db"
        Xs = "dc" if b == 0 else "dd"
        ws = "dwa" if b == 0 else "dwb"
        p.acquire("sync", f"r2x{b}")
        lv = p.dma("sync", xs, xr[b][:], xin[128 * tt:128 * (tt + 1), :])
        p.acquire("gpsimd", f"r2X{b}")
        Xv = p.dma("gpsimd", Xs, Xr[b][:], X_d[128 * tt:128 * (tt + 1), :])
        p.wait("vector", xs, lv)
        p.wait("vector", Xs, Xv)
        p.acquire("vector", f"r2o{b}")
        a1 = p.inc("vector", lambda e, o=xo[b], a=xr[b], x2=Xr[b]: e.tensor_tensor(
            o[:], a[:], x2[:], op=ALU.add), "ve")
        p.release(f"r2x{b}", "ve")
        p.release(f"r2X{b}", "ve")
        p.wait("vector", "ve", a1)
        p.inc("vector", lambda e, o=xo[b], dl=dtm[tt]: e.tensor_tensor(
            o[:], o[:], dl[:], op=ALU.add), "ve")
        p.wait("scalar", "ve", p.cnt["ve"])
        p.dma("scalar", ws, xout[128 * tt:128 * (tt + 1), :], xo[b][:])
        p.release(f"r2o{b}", ws)
    rph.close()



C64 = 64
NCH = R // C64
NG = 4
SCL = 128.0 ** -0.5


def emit_ln_l0(p, xin, xhalo, h_d):
    emit_ln(p, xin, h_d[3:3 + R, :])
    p.barrier(engines=("sync", "vector", "scalar"), sems=("ve", "se", "dwa", "dwb"))
    ph = contextlib.ExitStack()
    S = lambda b, s, d: ph.enter_context(p.nc.sbuf_tensor(p.name(b), s, d))
    hx = S("hlx", [3, D], F32)
    hh = S("hlh", [3, D], BF16)
    hd1 = S("hld1", [3, D], BF16)
    hd2 = S("hld2", [3, D], BF16)
    hsum = S("hlsum", [3, 1], F32)
    hsq = S("hlsq", [3, 1], F32)
    hmean = S("hlmean", [3, 1], F32)
    hvar = S("hlvar", [3, 1], F32)
    hmsq = S("hlmsq", [3, 1], F32)
    hrstd = S("hlrstd", [3, 1], F32)
    lv = p.dma("sync", "da", hx[:], xhalo)
    p.wait("vector", "da", lv)
    p.wait("scalar", "da", lv)
    sv = p.inc("vector", lambda e, o=hd1, i=hx, s=hsum: e.tensor_scalar(
        o[:], i[:], 1.0, 0.0, op0=ALU.mult, op1=ALU.add, accum_out=s[:]), "ve")
    sq = p.inc("scalar", lambda e, o=hd2, i=hx, s=hsq: e.activation(
        o[:], i[:], AF.Square, accum_out=s[:]), "se")
    p.wait("vector", "se", sq)
    p.wait("vector", "ve", sv)
    mv = p.inc("vector", lambda e, m=hmean, s=hsum: e.tensor_scalar(
        m[:], s[:], 1.0 / D, None, op0=ALU.mult), "ve")
    p.inc("vector", lambda e, v=hvar, s=hsq: e.tensor_scalar(
        v[:], s[:], 1.0 / D, None, op0=ALU.mult), "ve")
    p.wait("vector", "ve", mv)
    mqv = p.inc("vector", lambda e, m=hmsq, a=hmean: e.tensor_tensor(
        m[:], a[:], a[:], op=ALU.mult), "ve")
    p.wait("vector", "ve", mqv)
    vv = p.inc("vector", lambda e, v=hvar, m=hmsq: e.tensor_tensor(
        v[:], v[:], m[:], op=ALU.subtract), "ve")
    p.wait("scalar", "ve", vv)
    sqv = p.inc("scalar", lambda e, v=hvar, r=hrstd, c=p.cv: e.activation(
        r[:], v[:], AF.Sqrt, bias=c[0:3, C_LNEPS:C_LNEPS + 1]), "se")
    p.wait("vector", "se", sqv)
    rv = p.inc("vector", lambda e, r=hrstd: e.reciprocal(r[:], r[:]), "ve")
    p.wait("vector", "ve", rv)
    av = p.inc("vector", lambda e, h=hh, x=hx, m=hmean, r=hrstd: e.tensor_scalar(
        h[:], x[:], m[:], r[:], op0=ALU.subtract, op1=ALU.mult), "ve")
    p.wait("scalar", "ve", av)
    p.dma("scalar", "dwa", h_d[0:3, :], hh[:])
    ph.close()


def _emit_l0_attn_impl(p, xin, xhalo, xout, Wq, Wk, Wv, Ww, Wf1, Wf2, Wb, Wg1, Wg2, Wo,
                 cq2d, ck2d, cv2d, bg2b, normwb, trix, masksu, cmask2d, onesf2,
                 replica_groups):
    h_d = p.dram("hL0a", [3 + R, D], BF16)
    qd = p.dram("qdL0", [KC, 128, R], BF16)
    kd = p.dram("kdL0", [KC, 128, R], BF16)
    vd = p.dram("vdL0", [KC, 128, R], BF16)
    wd = p.dram("wdL0", [KC, 128, R], BF16)
    gnd = p.dram("gnd", [R, D], F32)
    gsd = p.dram("gsd", [R, D], BF16)
    Btd = p.dram("Btd", [R, D], BF16)
    Bhd = p.dram("Bhd", [R, D], BF16)
    Kpd = p.dram("Kpd", [R, D], BF16)
    Vpd = p.dram("Vpd", [R, D], BF16)
    Dtd = p.dram("Dtd", [R, D], BF16)
    ecd = p.dram("ecd", [NCH, D], F32)
    ecc = p.dram("ecc", [NCH, 128, 16], F32)
    otd = p.dram("otd", [R, D], BF16)
    nworld = len(replica_groups[0])
    agin = p.dram("agin", [128, 2 * D], BF16)
    agout = p.dram("agout", [nworld * 128, 2 * D], BF16)
    ctin = p.dram("ctin", [1, D], F32)
    ctout = p.dram("ctout", [nworld, D], F32)
    for eng in ("tensor", "vector", "scalar"):
        p.wait_consts(eng)

    p.phase("l0a_ln")
    emit_ln_l0(p, xin, xhalo, h_d)

    beta_sb = p.sb("beta", [128, NT * 16], F32)
    betat = [beta_sb[:, 16 * t:16 * (t + 1)] for t in range(NT)]
    Skb = p.sb("Skb", [128, D], BF16)
    Svb = p.sb("Svb", [128, D], BF16)
    ctot = p.sb("ctot", [1, D], F32)
    hfm_all2 = [p.sb("hfm", [128, KC * 512], BF16) for _ in range(2)]
    hfm2 = [[t[:, 512 * kc:512 * (kc + 1)] for kc in range(KC)] for t in hfm_all2]
    hhalo_all = p.sb("hhalo", [128, KC * 3], BF16)
    hhalo = [hhalo_all[:, 3 * kc:3 * (kc + 1)] for kc in range(KC)]

    with contextlib.ExitStack() as ph:
        S = lambda b, s, d: ph.enter_context(p.nc.sbuf_tensor(p.name(b), s, d))
        htm3 = S("htm3", [3, D], BF16)
        ptp = ph.enter_context(p.nc.psum_tensor(p.name("ptp"), [128, 3], BF16))
        p.wait("sync", "dwa", p.cnt["dwa"])
        p.wait("sync", "dwb", p.cnt["dwb"])
        lv = p.dma("sync", "dtot", htm3[:], h_d[0:3, :])
        p.wait("tensor", "dtot", lv)
        for kc in range(KC):
            p.acquire("tensor", "ptp")
            tv = p.inc("tensor", lambda e, o=ptp, i_=htm3[:, 128 * kc:128 * (kc + 1)],
                       id_=p.ident: e.transpose(o[:], i_, id_[0:3, 0:3]), "te")
            p.wait("vector", "te", tv)
            vv = p.inc("vector", lambda e, o=hhalo[kc], i_=ptp: e.tensor_copy(o, i_[:]), "ve")
            p.release("ptp", "ve")
        p.wait("tensor", "ve", p.cnt["ve"])

    p.phase("l0a_proj")
    vzero = p.inc("vector", lambda e, o=ctot: e.memset(o[:], 0.0), "ve")
    with contextlib.ExitStack() as ph:
        S = lambda b, s, d: ph.enter_context(p.nc.sbuf_tensor(p.name(b), s, d))
        wfull = S("wfull", [128, KC, KC, 128], BF16)
        tails = S("tails", [128, KC * 3], F32)
        acc = [S("acc", [128, 512], F32) for _ in range(2)]
        qf = [S("qf", [128, 512], BF16) for _ in range(2)]
        pp = [ph.enter_context(p.nc.psum_tensor(p.name("pp"), [128, 512], F32)) for _ in range(2)]
        pmini = ph.enter_context(p.nc.psum_tensor(p.name("pmini"), [128, 3], F32))

        for (Wt, convt, dstd) in ((Wq, cq2d, qd), (Wk, ck2d, kd), (Wv, cv2d, vd), (Ww, cv2d, wd)):
            p.wait("gpsimd", "te", p.cnt["te"])
            for oc in range(KC):
                p.dma("gpsimd", "dgt", wfull[:, oc, :, :],
                      Wt[:, 128 * oc:128 * (oc + 1)].rearrange("(kc p) j -> p kc j", p=128))
            wv_ = p.cnt["dgt"]
            p.wait("tensor", "dgt", wv_)
            for oc in range(KC):
                p.acquire("tensor", "pmini")
                tv = p.mm_group(pmini[:], [(wfull[:, oc, kc, :], hhalo[kc][:]) for kc in range(KC)])
                p.wait("vector", "te", tv)
                vv = p.inc("vector", lambda e, o=tails[:, 3 * oc:3 * (oc + 1)], i_=pmini:
                           e.tensor_copy(o, i_[:]), "ve")
                p.release("pmini", "ve")
            def p1_loads(tile):
                bh = tile % 2
                semh = "dta" if bh == 0 else "dtb"
                t0 = 512 * tile
                p.acquire("sync", f"hfmb{bh}")
                for kc in range(KC):
                    p.dma("sync", semh, hfm2[bh][kc][:], h_d[3 + t0:3 + t0 + 512,
                                                             128 * kc:128 * (kc + 1)], transpose=True)
                return p.cnt[semh]

            pendh = p1_loads(0)
            for tile in range(NG):
                t0 = 512 * tile
                bh = tile % 2
                hfm = hfm2[bh]
                semh = "dta" if bh == 0 else "dtb"
                hv = pendh
                if tile + 1 < NG:
                    pendh = p1_loads(tile + 1)
                p.wait("tensor", semh, hv)
                for oc in range(KC):
                    bq = (tile * KC + oc) % 2
                    p.acquire("tensor", f"pp{bq}")
                    tv = p.mm_group(pp[bq][:], [(wfull[:, oc, kc, :], hfm[kc][:]) for kc in range(KC)])
                    p.wait("scalar", "te", tv)
                    p.acquire("scalar", f"accb{bq}")
                    c3 = p.inc("scalar", lambda e, o=acc[bq], i_=pp[bq], s=convt[:, 4 * oc + 3:4 * oc + 4]:
                               e.activation(o[:], i_[:], AF.Identity, scale=s), "se")
                    p.wait("vector", "se", c3)
                    for i in (2, 1, 0):
                        a0 = 3 - i
                        cv_ = p.inc("vector", lambda e, o=acc[bq][:, a0:512], i_=pp[bq][:, 0:512 - a0],
                              s=convt[:, 4 * oc + i:4 * oc + i + 1], ac=acc[bq][:, a0:512]:
                              e.scalar_tensor_tensor(o, i_, s, ac, op0=ALU.mult, op1=ALU.add), "ve")
                        p.wait("vector", "ve", cv_)
                        cv_ = p.inc("vector", lambda e, o=acc[bq][:, 0:3 - i], t_=tails[:, 3 * oc + i:3 * oc + 3],
                              s=convt[:, 4 * oc + i:4 * oc + i + 1]:
                              e.scalar_tensor_tensor(o, t_, s, o, op0=ALU.mult, op1=ALU.add), "ve")
                        p.wait("vector", "ve", cv_)
                    tnv = p.inc("vector", lambda e, o=tails[:, 3 * oc:3 * (oc + 1)], i_=pp[bq][:, 509:512]:
                                e.tensor_copy(o, i_), "ve")
                    p.release(f"pp{bq}", "ve")
                    p.wait("scalar", "ve", tnv)
                    p.acquire("scalar", f"qf{bq}")
                    fv = p.inc("scalar", lambda e, o=qf[bq], i_=acc[bq]: e.activation(
                        o[:], i_[:], AF.Silu), "se")
                    p.rel[f"accb{bq}"] = [("se", fv)]
                    ws = "dwa" if bq == 0 else "dwb"
                    p.wait("scalar", "se", fv)
                    p.dma("scalar", ws, dstd[oc, :, t0:t0 + 512], qf[bq][:])
                    p.release(f"qf{bq}", ws)
                    if oc == KC - 1:
                        p.release(f"hfmb{bh}", "te")

    p.phase("l0a_gates")
    p.barrier(sems=("te", "ve", "se"))
    with contextlib.ExitStack() as ph:
        S = lambda b, s, d: ph.enter_context(p.nc.sbuf_tensor(p.name(b), s, d))
        wf1s = S("wf1s", [128, KC, 128], BF16)
        wg1s = S("wg1s", [128, KC, 128], BF16)
        wbs = S("wbs", [128, KC, 16], BF16)
        wf2s = S("wf2s", [128, D], BF16)
        wg2s = S("wg2s", [128, D], BF16)
        f1 = S("f1", [128, 512], BF16)
        g1 = S("g1", [128, 512], BF16)
        sc1 = S("sc1", [128, 512], F32)
        sc2 = S("sc2", [128, 512], F32)
        gnt = S("gnt", [128, D], F32)
        gst = S("gst", [128, D], BF16)
        pj = [ph.enter_context(p.nc.psum_tensor(p.name("pj"), [128, 512], F32)) for _ in range(2)]
        pb = ph.enter_context(p.nc.psum_tensor(p.name("pb"), [128, 16], F32))

        p.wait("gpsimd", "te", p.cnt["te"])
        p.dma("gpsimd", "dgt", wf1s[:], Wf1.rearrange("(kc p) j -> p kc j", p=128))
        p.dma("gpsimd", "dgt", wg1s[:], Wg1.rearrange("(kc p) j -> p kc j", p=128))
        p.dma("gpsimd", "dgt", wbs[:], Wb.rearrange("(kc p) j -> p kc j", p=128))
        p.dma("gpsimd", "dgt", wf2s[:], Wf2)
        wv_ = p.dma("gpsimd", "dgt", wg2s[:], Wg2)
        p.wait("tensor", "dgt", wv_)
        def p2_loads(tile):
            bh = tile % 2
            semh = "dta" if bh == 0 else "dtb"
            t0 = 512 * tile
            p.acquire("sync", f"hfmb{bh}")
            for kc in range(KC):
                p.dma("sync", semh, hfm2[bh][kc][:], h_d[3 + t0:3 + t0 + 512,
                                                         128 * kc:128 * (kc + 1)], transpose=True)
            return p.cnt[semh]

        pendh = p2_loads(0)
        for tile in range(NG):
            t0 = 512 * tile
            bh = tile % 2
            hfm = hfm2[bh]
            semh = "dta" if bh == 0 else "dtb"
            hv = pendh
            if tile + 1 < NG:
                pendh = p2_loads(tile + 1)
            p.wait("tensor", semh, hv)
            p.acquire("tensor", "pj0")
            tv = p.mm_group(pj[0][:], [(wf1s[:, kc, :], hfm[kc][:]) for kc in range(KC)])
            p.wait("scalar", "te", tv)
            p.acquire("scalar", "f1")
            fv = p.inc("scalar", lambda e, o=f1, i_=pj[0]: e.activation(o[:], i_[:], AF.Copy), "se")
            p.release("pj0", "se")
            p.acquire("tensor", "pj1")
            tv = p.mm_group(pj[1][:], [(wg1s[:, kc, :], hfm[kc][:]) for kc in range(KC)])
            p.wait("scalar", "te", tv)
            p.acquire("scalar", "g1")
            gv = p.inc("scalar", lambda e, o=g1, i_=pj[1]: e.activation(o[:], i_[:], AF.Copy), "se")
            p.release("pj1", "se")
            for j in range(4):
                tt_i = tile * 4 + j
                p.acquire("scalar", "gn_sp")
                p.acquire("vector", "gn_sp")
                p.acquire("tensor", "pb")
                tv = p.mm_group(pb[:], [(hfm[kc][:, 128 * j:128 * (j + 1)], wbs[:, kc, :])
                                        for kc in range(KC)])
                p.wait("scalar", "te", tv)
                bv = p.inc("scalar", lambda e, o=betat[tt_i], i_=pb: e.activation(
                    o, i_[:], AF.Sigmoid), "se")
                p.release("pb", "se")
                p.wait("tensor", "se", max(fv, gv))
                for q2 in range(4):
                    cs = slice(512 * q2, 512 * (q2 + 1))
                    p.acquire("tensor", "pj0")
                    tv = p.inc("tensor", lambda e, o=pj[0], l=f1[:, 128 * j:128 * (j + 1)],
                               r=wf2s[:, cs]: e.matmul(o[:], l, r, start=True, stop=True), "te")
                    p.wait("scalar", "te", tv)
                    p.acquire("scalar", "sc1")
                    e1 = p.inc("scalar", lambda e, o=sc1, i_=pj[0]: e.activation(
                        o[:], i_[:], AF.Exp, scale=-1.0), "se")
                    p.release("pj0", "se")
                    p.wait("scalar", "se", e1)
                    e2 = p.inc("scalar", lambda e, o=gnt[:, cs], i_=sc1: e.activation(
                        o, i_[:], AF.Ln, bias=1.0), "se")
                    p.release("sc1", "se")
                    p.acquire("tensor", "pj1")
                    tv = p.inc("tensor", lambda e, o=pj[1], l=g1[:, 128 * j:128 * (j + 1)],
                               r=wg2s[:, cs]: e.matmul(o[:], l, r, start=True, stop=True), "te")
                    p.wait("vector", "te", tv)
                    p.acquire("vector", "sc2")
                    v1 = p.inc("vector", lambda e, o=sc2, i_=pj[1], b=bg2b[:, cs]: e.tensor_tensor(
                        o[:], i_[:], b, op=ALU.add), "ve")
                    p.release("pj1", "ve")
                    p.wait("scalar", "ve", v1)
                    s1 = p.inc("scalar", lambda e, o=sc2: e.activation(o[:], o[:], AF.Sigmoid), "se")
                    p.wait("vector", "se", s1)
                    v2 = p.inc("vector", lambda e, o=gst[:, cs], i_=sc2, n=normwb[:, cs]:
                               e.tensor_tensor(o, i_[:], n, op=ALU.mult), "ve")
                    p.release("sc2", "ve")
                sev = p.cnt["se"]
                vev = p.cnt["ve"]
                p.wait("scalar", "se", sev)
                p.wait("scalar", "ve", vev)
                ws = "dwa" if tt_i % 2 == 0 else "dwb"
                p.dma("scalar", ws, gnd[128 * tt_i:128 * (tt_i + 1), :], gnt[:])
                p.dma("scalar", ws, gsd[128 * tt_i:128 * (tt_i + 1), :], gst[:])
                p.release("gn_sp", ws)
                p.rel["f1"] = [("te", p.cnt["te"])]
                p.rel["g1"] = [("te", p.cnt["te"])]
                p.rel["sc1"] = [("se", p.cnt["se"])]
                p.rel["sc2"] = [("ve", p.cnt["ve"])]
                if j == 3:
                    p.release(f"hfmb{bh}", "te")

    p.phase("l0a_sweep1")
    p.barrier(sems=("te", "ve", "se", "dwa", "dwb"))
    zk = p.inc("vector", lambda e, o=Skb: e.memset(o[:], 0.0), "ve")
    zv2 = p.inc("vector", lambda e, o=Svb: e.memset(o[:], 0.0), "ve")
    with contextlib.ExitStack() as ph:
        S = lambda b, s, d: ph.enter_context(p.nc.sbuf_tensor(p.name(b), s, d))
        ktm2 = [S("ktm", [128, D], BF16) for _ in range(2)]
        vtm2 = [S("vtm", [128, D], BF16) for _ in range(2)]
        wtm2 = [S("wtm", [128, D], BF16) for _ in range(2)]
        gn22 = [S("gn2", [128, D], F32) for _ in range(2)]
        wsq = S("wsq", [128, D], F32)
        sq16 = S("sq16", [128, 16], F32)
        wn = S("wn", [128, D], BF16)
        kp = S("kp", [128, D], BF16)
        vp = S("vp", [128, D], BF16)
        Bt = S("Bt", [128, D], BF16)
        Bh = S("Bh", [128, D], BF16)
        Dt = S("Dt", [128, D], BF16)
        en = S("en", [128, 512], BF16)
        ecrow = S("ecrow", [128, D], F32)
        eccol = S("eccol", [128, 16], F32)
        ctot2 = S("ctot2", [128, D], F32)
        pc_ps = ph.enter_context(p.nc.psum_tensor(p.name("pc"), [128, 512], F32))
        pl_ps = ph.enter_context(p.nc.psum_tensor(p.name("pl"), [128, 512], F32))
        PB = ph.enter_context(p.nc.psum_tensor(p.name("PB"), [128, 512], F32))
        pcol = ph.enter_context(p.nc.psum_tensor(p.name("pcol"), [128, 16], F32))
        psk = ph.enter_context(p.nc.psum_tensor(p.name("psk"), [128, 512], F32))
        psv = ph.enter_context(p.nc.psum_tensor(p.name("psv"), [128, 512], F32))
        zc = p.inc("vector", lambda e, o=ctot2: e.memset(o[:], 0.0), "ve")

        def s1_loads(pr):
            b = pr % 2
            sem = "dta" if b == 0 else "dtb"
            gsem = "dga" if b == 0 else "dgb"
            p.acquire("sync", f"s1buf{b}")
            for kc in range(KC):
                p.dma("sync", sem, ktm2[b][:, 128 * kc:128 * (kc + 1)],
                      kd[kc, :, 128 * pr:128 * pr + 128], transpose=True)
                p.dma("sync", sem, vtm2[b][:, 128 * kc:128 * (kc + 1)],
                      vd[kc, :, 128 * pr:128 * pr + 128], transpose=True)
                p.dma("sync", sem, wtm2[b][:, 128 * kc:128 * (kc + 1)],
                      wd[kc, :, 128 * pr:128 * pr + 128], transpose=True)
            p.acquire("gpsimd", f"s1buf{b}")
            p.dma("gpsimd", gsem, gn22[b][:], gnd[128 * pr:128 * pr + 128, :])
            return p.cnt[sem], p.cnt[gsem]

        pend = s1_loads(0)
        for pr in range(NT):
            tp0 = 128 * pr
            b = pr % 2
            ktm, vtm, wtm, gn2 = ktm2[b], vtm2[b], wtm2[b], gn22[b]
            lv, gv_ = pend
            if pr + 1 < NT:
                pend = s1_loads(pr + 1)
            sem = "dta" if b == 0 else "dtb"
            gsem = "dga" if b == 0 else "dgb"
            for eng in ("tensor", "vector", "scalar"):
                p.wait(eng, sem, lv)
                p.wait(eng, gsem, gv_)
            for hc in range(2):
                ci = 2 * pr + hc
                rs_ = slice(64 * hc, 64 * hc + 64)
                h64 = 64 * hc
                tt_i = pr
                p.acquire("vector", f"sw1sp{hc}")
                p.acquire("scalar", f"sw1sp{hc}")
                wv1 = p.inc("scalar", lambda e, o=wsq[rs_, :], i_=wtm[rs_, :]: e.activation(
                    o, i_, AF.Square), "se")
                p.wait("vector", "se", wv1)
                wv2 = p.inc("vector", lambda e, o=sq16[rs_, :], i_=wsq[rs_, :]: e.tensor_reduce(
                    o, i_.rearrange("p (h j) -> p h j", h=16), axis=AX.X, op=ALU.add), "ve")
                p.wait("scalar", "ve", wv2)
                wv3 = p.inc("scalar", lambda e, o=sq16[rs_, :], c=p.cv[rs_, C_L2EPS:C_L2EPS + 1]:
                    e.activation(o, o, AF.Sqrt, bias=c), "se")
                p.wait("vector", "se", wv3)
                wv4 = p.inc("vector", lambda e, o=sq16[rs_, :]: e.reciprocal(o, o), "ve")
                p.wait("vector", "ve", wv4)
                for h in range(16):
                    hs = slice(128 * h, 128 * (h + 1))
                    p.inc("vector", lambda e, o=wn[rs_, hs], i_=wtm[rs_, hs], s=sq16[rs_, h:h + 1]:
                          e.tensor_scalar(o, i_, s, None, op0=ALU.mult), "ve")
                    p.inc("vector", lambda e, o=kp[rs_, hs], i_=ktm[rs_, hs],
                          s=betat[tt_i][rs_, h:h + 1]: e.tensor_scalar(o, i_, s, None, op0=ALU.mult), "ve")
                    p.inc("vector", lambda e, o=vp[rs_, hs], i_=vtm[rs_, hs],
                          s=betat[tt_i][rs_, h:h + 1]: e.tensor_scalar(o, i_, s, None, op0=ALU.mult), "ve")
                wv5 = p.cnt["ve"]
                p.wait("tensor", "se", p.cnt["se"])
                p.acquire("tensor", "pcol")
                for h in range(16):
                    fn = lambda e, o=pcol[:, h:h + 1], l=gn2[rs_, 128 * h:128 * (h + 1)], \
                        r=p.cv[rs_, C_ONE:C_ONE + 1]: e.matmul(o, l, r, start=True, stop=True)
                    cv_ = p.inc("tensor", fn, "te")
                p.wait("scalar", "te", p.cnt["te"])
                p.acquire("scalar", "eccolbuf")
                ecv = p.inc("scalar", lambda e, o=eccol, i_=pcol: e.activation(
                    o[:], i_[:], AF.Exp, scale=-1.0), "se")
                p.release("pcol", "se")
                p.wait("scalar", "se", ecv)
                p.dma("scalar", "dwa" if ci % 2 == 0 else "dwb", ecc[ci, :, :], eccol[:])
                for q2 in range(4):
                    cs = slice(512 * q2, 512 * (q2 + 1))
                    p.acquire("tensor", "pc_ps")
                    t1 = p.inc("tensor", lambda e, o=pc_ps[rs_, :], l=trix[rs_, 0:64], r=gn2[rs_, cs]:
                               e.matmul(o, l, r, start=True, stop=True), "te")
                    p.acquire("tensor", "pl_ps")
                    t2 = p.inc("tensor", lambda e, o=pl_ps[h64:h64 + 1, :], l=p.cv[rs_, C_ONE:C_ONE + 1],
                               r=gn2[rs_, cs]: e.matmul(o, l, r, start=True, stop=True), "te")
                    p.wait("scalar", "te", t2)
                    s1 = p.inc("scalar", lambda e, o=Dt[rs_, cs], i_=pc_ps[rs_, :]: e.activation(
                        o, i_, AF.Exp, scale=-1.0), "se")
                    p.acquire("scalar", "en")
                    s2 = p.inc("scalar", lambda e, o=en[rs_, :], i_=pc_ps[rs_, :]: e.activation(
                        o, i_, AF.Exp), "se")
                    p.release("pc_ps", "se")
                    s3 = p.inc("scalar", lambda e, o=ecrow[h64:h64 + 1, cs], i_=pl_ps[h64:h64 + 1, :]:
                               e.activation(o, i_, AF.Exp, scale=-1.0), "se")
                    p.wait("vector", "te", t2)
                    p.wait("vector", "ve", wv5)
                    ct1 = p.inc("vector", lambda e, o=ctot2[h64:h64 + 1, cs], i_=pl_ps[h64:h64 + 1, :]:
                                e.tensor_tensor(o, o, i_, op=ALU.add), "ve")
                    p.release("pl_ps", "ve")
                    p.wait("vector", "se", s2)
                    bt1 = p.inc("vector", lambda e, o=Bt[rs_, cs], a=wn[rs_, cs], b=en[rs_, :]:
                                e.tensor_tensor(o, a, b, op=ALU.mult), "ve")
                    p.release("en", "ve")
                    p.wait("tensor", "se", s3)
                    p.acquire("tensor", "PB")
                    t3 = p.inc("tensor", lambda e, o=PB[:], l=onesf2[h64:h64 + 1, 0:128],
                               r=ecrow[h64:h64 + 1, cs]: e.matmul(o[:], l, r, start=True, stop=True), "te")
                    p.wait("vector", "te", t3)
                    p.wait("vector", "ve", bt1)
                    bh1 = p.inc("vector", lambda e, o=Bh[rs_, cs], a=Bt[rs_, cs], b=PB[rs_, :]:
                                e.tensor_tensor(o, b, a, op=ALU.mult), "ve")
                    p.wait("tensor", "ve", bh1)
                    p.acquire("tensor", "psk")
                    p.acquire("tensor", "psv")
                    for hh in range(4):
                        h = 4 * q2 + hh
                        hs = slice(128 * h, 128 * (h + 1))
                        os = slice(128 * hh, 128 * (hh + 1))
                        fn = lambda e, o=psk[:, os], l=kp[rs_, hs], r=Bh[rs_, hs]: \
                            e.matmul(o, l, r, start=True, stop=True)
                        p.inc("tensor", fn, "te")
                        fn = lambda e, o=psv[:, os], l=Bh[rs_, hs], r=vp[rs_, hs]: \
                            e.matmul(o, l, r, start=True, stop=True)
                        p.inc("tensor", fn, "te")
                    tsv = p.cnt["te"]
                    p.wait("vector", "te", tsv)
                    sk1 = p.inc("vector", lambda e, o=Skb[:, cs], b=PB[:]: e.tensor_tensor(
                        o, b, o, op=ALU.mult), "ve")
                    p.release("PB", "ve")
                    p.wait("vector", "ve", sk1)
                    sk2 = p.inc("vector", lambda e, o=Skb[:, cs], b=psk[:]: e.tensor_tensor(
                        o, b, o, op=ALU.add), "ve")
                    p.wait("vector", "ve", sk2)
                    p.release("psk", "ve")
                    for hh in range(4):
                        h = 4 * q2 + hh
                        hs = slice(128 * h, 128 * (h + 1))
                        os = slice(128 * hh, 128 * (hh + 1))
                        vv_ = p.inc("vector", lambda e, o=Svb[:, hs], s=eccol[:, h:h + 1],
                                    b=psv[:, os]: e.scalar_tensor_tensor(
                                        o, o, s, b, op0=ALU.mult, op1=ALU.add), "ve")
                        p.wait("vector", "ve", vv_)
                    p.release("psv", "ve")
                    p.release_add("eccolbuf", "ve")
                vv_all = p.cnt["ve"]
                se_all = p.cnt["se"]
                p.wait("scalar", "ve", vv_all)
                p.wait("scalar", "se", se_all)
                ws = "dwa" if ci % 2 == 0 else "dwb"
                p.dma("scalar", ws, Btd[64 * ci:64 * ci + 64, :], Bt[rs_, :])
                p.dma("scalar", ws, Bhd[64 * ci:64 * ci + 64, :], Bh[rs_, :])
                p.dma("scalar", ws, Kpd[64 * ci:64 * ci + 64, :], kp[rs_, :])
                p.dma("scalar", ws, Vpd[64 * ci:64 * ci + 64, :], vp[rs_, :])
                p.dma("scalar", ws, Dtd[64 * ci:64 * ci + 64, :], Dt[rs_, :])
                p.dma("scalar", ws, ecd[ci:ci + 1, :], ecrow[h64:h64 + 1, :])
                p.release(f"sw1sp{hc}", ws)
                p.release("eccolbuf", ws)
                p.release_add("eccolbuf", "ve")
                if hc == 1:
                    p.release(f"s1buf{b}", "te")
                    p.release_add(f"s1buf{b}", "ve")
                    p.release_add(f"s1buf{b}", "se")
        p.wait("tensor", "ve", p.cnt["ve"])
        for q2 in range(4):
            cs = slice(512 * q2, 512 * (q2 + 1))
            p.acquire("tensor", "PB")
            tv = p.inc("tensor", lambda e, o=PB[0:1, :], l=onesf2[:, 0:1], r=ctot2[:, cs]:
                       e.matmul(o, l, r, start=True, stop=True), "te")
            p.wait("vector", "te", tv)
            cv_ = p.inc("vector", lambda e, o=ctot[0:1, cs], i_=PB[0:1, :]: e.tensor_copy(
                o, i_), "ve")
            p.release("PB", "ve")
            p.wait("vector", "ve", cv_)

    p.phase("l0a_gather")
    p.barrier(sems=("te", "ve", "se", "dwa", "dwb", "dts"))
    p.wait("gpsimd", "ve", p.cnt["ve"])
    p.dma("gpsimd", "dcc", agin[:, 0:D], Skb[:])
    p.dma("gpsimd", "dcc", agin[:, D:2 * D], Svb[:])
    agv = p.dma("gpsimd", "dcc", ctin, ctot[:])
    p.wait("gpsimd", "dcc", p.cnt["dcc"])
    p.inc("gpsimd", lambda e, i_=agin, o_=agout, rg=replica_groups: e.collective_compute(
        "AllGather", ALU.bypass, replica_groups=rg, ins=[i_.opt()], outs=[o_.opt()]), "dcl", 1)
    p.inc("gpsimd", lambda e, i_=ctin, o_=ctout, rg=replica_groups: e.collective_compute(
        "AllGather", ALU.bypass, replica_groups=rg, ins=[i_.opt()], outs=[o_.opt()]), "dcl", 1)
    p.wait("gpsimd", "dcl", p.cnt["dcl"])
    with contextlib.ExitStack() as ph:
        S = lambda b, s, d: ph.enter_context(p.nc.sbuf_tensor(p.name(b), s, d))
        Lsb = S("Lsb", [128, 2 * D], BF16)
        Lms = S("Lms", [128, 512], BF16)
        ctj = S("ctj", [1, D], F32)
        erj = S("erj", [1, D], F32)
        ecjc = S("ecjc", [128, 16], F32)
        Sf = S("Sf", [128, 2 * D], F32)
        PE2 = ph.enter_context(p.nc.psum_tensor(p.name("PE2"), [128, 512], F32))
        pco2 = ph.enter_context(p.nc.psum_tensor(p.name("pco2"), [128, 16], F32))
        zf = p.inc("vector", lambda e, o=Sf: e.memset(o[:], 0.0), "ve")
        p.wait("vector", "ve", zf)
        for j in range(nworld - 1):
            p.acquire("gpsimd", "Lsb")
            lj = p.dma("gpsimd", "dcc", Lsb[:], agout[128 * j:128 * (j + 1), :])
            cj = p.dma("gpsimd", "dcc", ctj[:], ctout[j:j + 1, :])
            p.wait("vector", "dcc", p.cnt["dcc"])
            p.wait("scalar", "dcc", p.cnt["dcc"])
            m1 = p.inc("vector", lambda e, o=ctj, s=cmask2d[0:1, j:j + 1]: e.tensor_scalar(
                o[:], o[:], s, None, op0=ALU.mult), "ve")
            p.wait("scalar", "ve", m1)
            m2 = p.inc("scalar", lambda e, o=erj, i_=ctj: e.activation(
                o[:], i_[:], AF.Exp, scale=-1.0), "se")
            p.wait("tensor", "se", m2)
            for h in range(16):
                tv = p.inc("tensor", lambda e, o=pco2[:, h:h + 1], i_=erj[0:1, 128 * h:128 * (h + 1)],
                           id_=p.cv[0:1, C_ONE:C_ONE + 1]: e.transpose(o, i_, id_), "te")
            p.wait("vector", "te", p.cnt["te"])
            m3 = p.inc("vector", lambda e, o=ecjc, i_=pco2: e.tensor_copy(o[:], i_[:]), "ve")
            p.wait("vector", "ve", m3)
            for q2 in range(4):
                cs = slice(512 * q2, 512 * (q2 + 1))
                p.acquire("tensor", "PE2")
                tv = p.inc("tensor", lambda e, o=PE2[:], l=onesf2[0:1, 0:128], r=erj[0:1, cs]:
                           e.matmul(o[:], l, r, start=True, stop=True), "te")
                p.wait("vector", "te", tv)
                f1_ = p.inc("vector", lambda e, o=Sf[:, cs], b=PE2[:]: e.tensor_tensor(
                    o, o, b, op=ALU.mult), "ve")
                p.release("PE2", "ve")
                p.wait("vector", "ve", f1_)
                f2a = p.inc("vector", lambda e, o=Lms, a=Lsb[:, cs], s=cmask2d[:, j:j + 1]:
                            e.tensor_scalar(o[:], a, s, None, op0=ALU.mult), "ve")
                p.wait("vector", "ve", f2a)
                f2_ = p.inc("vector", lambda e, o=Sf[:, cs], b=Lms: e.tensor_tensor(
                    o, o, b[:], op=ALU.add), "ve")
                p.wait("vector", "ve", f2_)
            for h in range(16):
                hs = slice(D + 128 * h, D + 128 * (h + 1))
                f3 = p.inc("vector", lambda e, o=Sf[:, hs], s=ecjc[:, h:h + 1]: e.tensor_scalar(
                    o, o, s, None, op0=ALU.mult), "ve")
                p.wait("vector", "ve", f3)
                f4a = p.inc("vector", lambda e, o=Lms[:, 0:128], a=Lsb[:, hs], s=cmask2d[:, j:j + 1]:
                           e.tensor_scalar(o, a, s, None, op0=ALU.mult), "ve")
                p.wait("vector", "ve", f4a)
                f4 = p.inc("vector", lambda e, o=Sf[:, hs], b=Lms[:, 0:128]: e.tensor_tensor(
                    o, o, b, op=ALU.add), "ve")
                p.wait("vector", "ve", f4)
            p.release("Lsb", "ve")
        p.wait("vector", "dcc", agv)
        for q2 in range(4):
            cs = slice(512 * q2, 512 * (q2 + 1))
            cs2 = slice(D + 512 * q2, D + 512 * (q2 + 1))
            p.inc("vector", lambda e, o=Skb[:, cs], i_=Sf[:, cs]: e.tensor_copy(o, i_), "ve")
            p.inc("vector", lambda e, o=Svb[:, cs], i_=Sf[:, cs2]: e.tensor_copy(o, i_), "ve")

    p.phase("l0a_sweep2")
    p.barrier(sems=("te", "ve", "se"))
    with contextlib.ExitStack() as ph:
        S = lambda b, s, d: ph.enter_context(p.nc.sbuf_tensor(p.name(b), s, d))
        qfm2 = [S("qfm", [128, KC * 128], BF16) for _ in range(2)]
        kfm2 = [S("kfm", [128, KC * 128], BF16) for _ in range(2)]
        Bt22 = [S("Bt2", [128, D], BF16) for _ in range(2)]
        Bh22 = [S("Bh2", [128, D], BF16) for _ in range(2)]
        Kp22 = [S("Kp2", [128, D], BF16) for _ in range(2)]
        Vp22 = [S("Vp2", [128, D], BF16) for _ in range(2)]
        Dt22 = [S("Dt2", [128, D], BF16) for _ in range(2)]
        gs22 = [S("gs2", [128, D], BF16) for _ in range(2)]
        ecr22 = [S("ecr2", [128, D], F32) for _ in range(2)]
        ecc22 = [S("ecc2", [128, 2 * 16], F32) for _ in range(2)]
        amall = S("amall", [128, 1024], BF16)
        cmall = S("cmall", [128, 1024], BF16)
        amball = S("amball", [128, D], BF16)
        atfall = S("atfall", [128, 1024], BF16)
        btfall = S("btfall", [128, 1024], BF16)
        oraw = S("oraw", [128, D], F32)
        ssb = S("ssb", [128, D], F32)
        esb = S("esb", [128, D], F32)
        nmx = S("nmx", [128, 16], F32)
        rsm = S("rsm", [128, 16], F32)
        orr = S("orr", [128, 16], F32)
        ebst = S("ebst", [128, D], BF16)
        otm = S("otm", [128, D], BF16)
        pX = [ph.enter_context(p.nc.psum_tensor(p.name("pX"), [128, 128], F32)) for _ in range(2)]
        pS = ph.enter_context(p.nc.psum_tensor(p.name("pS"), [128, D], F32))
        ptr = [ph.enter_context(p.nc.psum_tensor(p.name("ptr"), [128, 64], BF16)) for _ in range(2)]

        def s2_loads(pr):
            b = pr % 2
            tp0 = 128 * pr
            sem = "dta" if b == 0 else "dtb"
            ssem = "dsa" if b == 0 else "dsb"
            gsem = "dga" if b == 0 else "dgb"
            p.acquire("sync", f"s2buf{b}")
            for kc in range(KC):
                p.dma("sync", sem, qfm2[b][:, 128 * kc:128 * (kc + 1)], qd[kc, :, tp0:tp0 + 128])
                p.dma("sync", sem, kfm2[b][:, 128 * kc:128 * (kc + 1)], kd[kc, :, tp0:tp0 + 128])
            p.acquire("scalar", f"s2buf{b}")
            p.dma("scalar", ssem, Bt22[b][:], Btd[tp0:tp0 + 128, :])
            p.dma("scalar", ssem, Bh22[b][:], Bhd[tp0:tp0 + 128, :])
            p.dma("scalar", ssem, Kp22[b][:], Kpd[tp0:tp0 + 128, :])
            p.acquire("gpsimd", f"s2buf{b}")
            p.dma("gpsimd", gsem, Vp22[b][:], Vpd[tp0:tp0 + 128, :])
            p.dma("gpsimd", gsem, Dt22[b][:], Dtd[tp0:tp0 + 128, :])
            p.dma("gpsimd", gsem, gs22[b][:], gsd[tp0:tp0 + 128, :])
            p.dma("gpsimd", gsem, ecr22[b][0:1, :], ecd[2 * pr:2 * pr + 1, :])
            p.dma("gpsimd", gsem, ecr22[b][64:65, :], ecd[2 * pr + 1:2 * pr + 2, :])
            p.dma("gpsimd", gsem, ecc22[b][:, 0:16], ecc[2 * pr, :, :])
            p.dma("gpsimd", gsem, ecc22[b][:, 16:32], ecc[2 * pr + 1, :, :])
            return p.cnt[sem], p.cnt[ssem], p.cnt[gsem]

        pend2 = s2_loads(0)
        for pr in range(NT):
            tp0 = 128 * pr
            b = pr % 2
            qfm, kfm = qfm2[b], kfm2[b]
            Bt2, Bh2, Kp2, Vp2 = Bt22[b], Bh22[b], Kp22[b], Vp22[b]
            Dt2, gs2, ecr2, ecc2 = Dt22[b], gs22[b], ecr22[b], ecc22[b]
            lv, sv_, gv_ = pend2
            if pr + 1 < NT:
                pend2 = s2_loads(pr + 1)
            sem = "dta" if b == 0 else "dtb"
            ssem = "dsa" if b == 0 else "dsb"
            gsem = "dga" if b == 0 else "dgb"
            for eng in ("tensor", "vector", "scalar"):
                p.wait(eng, sem, lv)
                p.wait(eng, ssem, sv_)
                p.wait(eng, gsem, gv_)
            p.acquire("vector", "otm_sp")
            for hc in range(2):
                ci = 2 * pr + hc
                rs_ = slice(64 * hc, 64 * hc + 64)
                h64 = 64 * hc
                tcol = slice(64 * hc, 64 * hc + 64)
                p.acquire("tensor", "pS_full")
                p.acquire("vector", "Am_all")
                for h in range(16):
                    bx = h % 2
                    p.acquire("tensor", f"pX{bx}")
                    t1 = p.inc("tensor", lambda e, o=pX[bx][rs_, 0:64], l=kfm[:, 128 * h:128 * h + 128][:, tcol],
                               r=qfm[:, 128 * h:128 * h + 128][:, tcol]: e.matmul(
                                   o, l, r, start=True, stop=True), "te")
                    p.wait("vector", "te", t1)
                    p.inc("vector", lambda e, o=amall[rs_, 64 * h:64 * (h + 1)], i_=pX[bx][rs_, 0:64],
                          s=betat[pr][rs_, h:h + 1], m=masksu[rs_, :]: e.scalar_tensor_tensor(
                              o, i_, s, m, op0=ALU.mult, op1=ALU.mult), "ve")
                    p.release(f"pX{bx}", "ve")
                amv = p.cnt["ve"]
                p.wait("tensor", "ve", amv)
                for h in range(16):
                    hs = slice(128 * h, 128 * (h + 1))
                    p.inc("tensor", lambda e, o=pS[rs_, hs], l=qfm[:, 128 * h:128 * h + 128][:, tcol],
                          r=Skb[:, hs]: e.matmul(o, l, r, start=True, stop=False), "te")
                    p.inc("tensor", lambda e, o=pS[rs_, hs], l=amall[rs_, 64 * h:64 * (h + 1)],
                          r=Bt2[rs_, hs]: e.matmul(o, l, r, start=False, stop=True), "te")
                tS = p.cnt["te"]
                p.rel["Am_all"] = [("te", tS)]
                p.wait("vector", "te", tS)
                sv1 = p.inc("vector", lambda e, o=ssb[rs_, :], i_=pS[rs_, :], d=Dt2[rs_, :]:
                            e.scalar_tensor_tensor(o, i_, SCL, d, op0=ALU.mult, op1=ALU.mult), "ve")
                p.release("pS_full", "ve")
                p.wait("vector", "ve", sv1)
                sv2 = p.inc("vector", lambda e, o=nmx[rs_, :], i_=ssb[rs_, :]: e.tensor_reduce(
                    o, i_.rearrange("p (h j) -> p h j", h=16), axis=AX.X, op=ALU.max, negate=True), "ve")
                p.wait("vector", "ve", sv2)
                for h in range(16):
                    hs = slice(128 * h, 128 * (h + 1))
                    p.inc("vector", lambda e, o=esb[rs_, hs], i_=ssb[rs_, hs], s=nmx[rs_, h:h + 1]:
                          e.tensor_scalar(o, i_, s, None, op0=ALU.add), "ve")
                sv3 = p.cnt["ve"]
                p.wait("scalar", "ve", sv3)
                sv4 = p.inc("scalar", lambda e, o=esb[rs_, :]: e.activation(o, o, AF.Exp), "se")
                p.wait("vector", "se", sv4)
                sv5 = p.inc("vector", lambda e, o=rsm[rs_, :], i_=esb[rs_, :]: e.tensor_reduce(
                    o, i_.rearrange("p (h j) -> p h j", h=16), axis=AX.X, op=ALU.add), "ve")
                p.wait("vector", "ve", sv5)
                sv6 = p.inc("vector", lambda e, o=rsm[rs_, :]: e.reciprocal(o, o), "ve")
                p.wait("vector", "ve", sv6)
                p.acquire("vector", "amball")
                for h in range(16):
                    hs = slice(128 * h, 128 * (h + 1))
                    p.inc("vector", lambda e, o=amball[rs_, hs], i_=esb[rs_, hs],
                          s=rsm[rs_, h:h + 1], d=Dt2[rs_, hs]: e.scalar_tensor_tensor(
                              o, i_, s, d, op0=ALU.mult, op1=ALU.mult), "ve")
                c1v = p.cnt["ve"]
                p.wait("tensor", "ve", c1v)
                for h in range(16):
                    hs = slice(128 * h, 128 * (h + 1))
                    bx = h % 2
                    p.acquire("tensor", f"ptr{bx}")
                    tva = p.inc("tensor", lambda e, o=ptr[bx][:, 0:64], i_=amball[rs_, hs],
                                id_=p.ident[rs_, rs_]: e.transpose(o, i_, id_), "te")
                    p.wait("vector", "te", tva)
                    p.inc("vector", lambda e, o=atfall[:, 64 * h:64 * (h + 1)], i_=ptr[bx][:, 0:64]:
                          e.tensor_copy(o, i_), "ve")
                    p.release(f"ptr{bx}", "ve")
                p.rel["amball"] = [("te", p.cnt["te"])]
                for h in range(16):
                    hs = slice(128 * h, 128 * (h + 1))
                    bx = h % 2
                    p.acquire("tensor", f"ptr{bx}")
                    tvb = p.inc("tensor", lambda e, o=ptr[bx][:, 0:64], i_=Bt2[rs_, hs],
                                id_=p.ident[rs_, rs_]: e.transpose(o, i_, id_), "te")
                    p.wait("vector", "te", tvb)
                    p.inc("vector", lambda e, o=btfall[:, 64 * h:64 * (h + 1)], i_=ptr[bx][:, 0:64]:
                          e.tensor_copy(o, i_), "ve")
                    p.release(f"ptr{bx}", "ve")
                tfv = p.cnt["ve"]
                p.wait("tensor", "ve", tfv)
                p.acquire("vector", "cmall")
                for h in range(16):
                    bx = h % 2
                    p.acquire("tensor", f"pX{bx}")
                    tv = p.inc("tensor", lambda e, o=pX[bx][rs_, 0:64], l=btfall[:, 64 * h:64 * (h + 1)],
                               r=atfall[:, 64 * h:64 * (h + 1)]: e.matmul(o, l, r, start=True, stop=True), "te")
                    p.wait("vector", "te", tv)
                    p.inc("vector", lambda e, o=cmall[rs_, 64 * h:64 * (h + 1)], i_=pX[bx][rs_, 0:64],
                          m=masksu[rs_, :]: e.tensor_tensor(o, i_, m, op=ALU.mult), "ve")
                    p.release(f"pX{bx}", "ve")
                cmv = p.cnt["ve"]
                p.wait("tensor", "ve", cmv)
                p.acquire("vector", "orawall")
                for h in range(16):
                    hs = slice(128 * h, 128 * (h + 1))
                    bx = h % 2
                    p.acquire("tensor", f"pX{bx}")
                    p.inc("tensor", lambda e, o=pX[bx][rs_, :], l=atfall[:, 64 * h:64 * (h + 1)],
                          r=Svb[:, hs]: e.matmul(o, l, r, start=True, stop=False), "te")
                    to = p.inc("tensor", lambda e, o=pX[bx][rs_, :], l=cmall[rs_, 64 * h:64 * (h + 1)],
                               r=Vp2[rs_, hs]: e.matmul(o, l, r, start=False, stop=True), "te")
                    p.wait("vector", "te", to)
                    p.inc("vector", lambda e, o=oraw[rs_, hs], i_=pX[bx][rs_, :]: e.tensor_copy(
                        o, i_), "ve")
                    p.release(f"pX{bx}", "ve")
                orv = p.cnt["ve"]
                p.rel["cmall"] = [("te", p.cnt["te"])]
                p.rel["atfall"] = [("te", p.cnt["te"])]
                p.rel["btfall"] = [("te", p.cnt["te"])]
                p.wait("scalar", "ve", orv)
                sq = p.inc("scalar", lambda e, o=ssb[rs_, :], i_=oraw[rs_, :]: e.activation(
                    o, i_, AF.Square), "se")
                p.wait("vector", "se", sq)
                rv_ = p.inc("vector", lambda e, o=orr[rs_, :], i_=ssb[rs_, :]: e.tensor_reduce(
                    o, i_.rearrange("p (h j) -> p h j", h=16), axis=AX.X, op=ALU.add), "ve")
                p.wait("scalar", "ve", rv_)
                o3 = p.inc("scalar", lambda e, o=orr[rs_, :], c=p.cv[rs_, C_LNEPS:C_LNEPS + 1]:
                    e.activation(o, o, AF.Sqrt, scale=1.0 / 128.0, bias=c), "se")
                p.wait("vector", "se", o3)
                o4 = p.inc("vector", lambda e, o=orr[rs_, :]: e.reciprocal(o, o), "ve")
                p.wait("vector", "ve", o4)
                for h in range(16):
                    hs = slice(128 * h, 128 * (h + 1))
                    p.inc("vector", lambda e, o=otm[rs_, hs], i_=oraw[rs_, hs], s=orr[rs_, h:h + 1],
                          g=gs2[rs_, hs]: e.scalar_tensor_tensor(
                              o, i_, s, g, op0=ALU.mult, op1=ALU.mult), "ve")
                p.rel["orawall"] = [("ve", p.cnt["ve"])]
                p.acquire("tensor", "pS_full")
                p.acquire("vector", "ebst")
                for q2 in range(4):
                    cs = slice(512 * q2, 512 * (q2 + 1))
                    tv = p.inc("tensor", lambda e, o=pS[:, cs],
                               l=onesf2[h64:h64 + 1, 0:128], r=ecr2[h64:h64 + 1, cs]:
                               e.matmul(o, l, r, start=True, stop=True), "te")
                    p.wait("vector", "te", tv)
                    p.inc("vector", lambda e, o=ebst[:, cs], i_=pS[:, cs]: e.tensor_copy(o, i_), "ve")
                ebv = p.cnt["ve"]
                p.release("pS_full", "ve")
                p.wait("vector", "ve", ebv)
                for h in range(16):
                    hs = slice(128 * h, 128 * (h + 1))
                    p.inc("vector", lambda e, o=Skb[:, hs], b=ebst[:, hs]: e.tensor_tensor(
                        o, b, o, op=ALU.mult), "ve")
                dkv = p.cnt["ve"]
                p.rel["ebst"] = [("ve", dkv)]
                p.wait("tensor", "ve", dkv)
                for h in range(16):
                    hs = slice(128 * h, 128 * (h + 1))
                    bx = h % 2
                    p.acquire("tensor", f"pX{bx}")
                    tk = p.inc("tensor", lambda e, o=pX[bx][:, :], l=Kp2[rs_, hs], r=Bh2[rs_, hs]:
                               e.matmul(o, l, r, start=True, stop=True), "te")
                    p.wait("vector", "te", tk)
                    p.inc("vector", lambda e, o=Skb[:, hs], b=pX[bx][:, :]: e.tensor_tensor(
                        o, b, o, op=ALU.add), "ve")
                    p.release(f"pX{bx}", "ve")
                skv = p.cnt["ve"]
                p.wait("tensor", "ve", skv)
                for h in range(16):
                    hs = slice(128 * h, 128 * (h + 1))
                    bx = h % 2
                    p.acquire("tensor", f"pX{bx}")
                    tv2 = p.inc("tensor", lambda e, o=pX[bx][:, :], l=Bh2[rs_, hs], r=Vp2[rs_, hs]:
                                e.matmul(o, l, r, start=True, stop=True), "te")
                    p.wait("vector", "te", tv2)
                    p.inc("vector", lambda e, o=Svb[:, hs], s=ecc2[:, 16 * hc + h:16 * hc + h + 1],
                          b=pX[bx][:, :]: e.scalar_tensor_tensor(
                              o, o, s, b, op0=ALU.mult, op1=ALU.add), "ve")
                    p.release(f"pX{bx}", "ve")
                p.wait("vector", "ve", p.cnt["ve"])
            ov = p.cnt["ve"]
            p.wait("scalar", "ve", ov)
            ws = "dwa" if pr % 2 == 0 else "dwb"
            p.dma("scalar", ws, otd[tp0:tp0 + 128, :], otm[:])
            p.release("otm_sp", ws)
            p.release(f"s2buf{b}", "te")
            p.release_add(f"s2buf{b}", "ve")
            p.release_add(f"s2buf{b}", "se")

    p.phase("l0a_wo")
    p.barrier(sems=("te", "ve", "se", "dwa", "dwb", "dts", "dgt"))
    dtm_all = p.sb("l0dtm", [128, NT * D], BF16)
    dtm = [dtm_all[:, D * t:D * (t + 1)] for t in range(NT)]
    with contextlib.ExitStack() as ph:
        S = lambda b, s, d: ph.enter_context(p.nc.sbuf_tensor(p.name(b), s, d))
        ofm = [S("ofm", [128, 512], BF16) for _ in range(KC)]
        wcol = [S("wcolo", [128, KC, 128], BF16) for _ in range(2)]
        dd = [S("l0dd", [128, 512], BF16) for _ in range(2)]
        pw = [ph.enter_context(p.nc.psum_tensor(p.name("pwo"), [128, 512], F32)) for _ in range(2)]
        pt = [ph.enter_context(p.nc.psum_tensor(p.name("pto"), [128, 128], BF16)) for _ in range(2)]
        for q in range(4):
            t0 = 512 * q
            p.wait("sync", "dwa", p.cnt["dwa"])
            p.wait("sync", "dwb", p.cnt["dwb"])
            p.wait("sync", "te", p.cnt["te"])
            for kc in range(KC):
                p.dma("sync", "dtot", ofm[kc][:], otd[t0:t0 + 512, 128 * kc:128 * (kc + 1)],
                      transpose=True)
            hv = p.cnt["dtot"]
            p.wait("tensor", "dtot", hv)
            for oc in range(KC):
                bw = oc % 2
                ws = "dc" if bw == 0 else "dd"
                p.acquire("gpsimd", f"wco{bw}")
                wv_ = p.dma("gpsimd", ws, wcol[bw][:],
                            Wo[:, 128 * oc:128 * (oc + 1)].rearrange("(kc p) j -> p kc j", p=128))
                p.wait("tensor", ws, wv_)
                p.acquire("tensor", f"pwo{bw}")
                tv = p.mm_group(pw[bw][:], [(wcol[bw][:, kc, :], ofm[kc][:]) for kc in range(KC)])
                p.release(f"wco{bw}", "te")
                p.wait("scalar", "te", tv)
                p.acquire("scalar", f"l0dd{bw}")
                sv = p.inc("scalar", lambda e, o=dd[bw], i_=pw[bw]: e.activation(
                    o[:], i_[:], AF.Copy), "se")
                p.release(f"pwo{bw}", "se")
                p.wait("tensor", "se", sv)
                for j in range(4):
                    bj = j % 2
                    tt_i = q * 4 + j
                    p.acquire("tensor", f"pto{bj}")
                    tpv = p.inc("tensor", lambda e, o=pt[bj], i_=dd[bw][:, 128 * j:128 * (j + 1)],
                                id_=p.ident: e.transpose(o[:], i_, id_[:]), "te")
                    p.wait("vector", "te", tpv)
                    p.inc("vector", lambda e, o=dtm[tt_i][:, 128 * oc:128 * (oc + 1)], i_=pt[bj]:
                          e.tensor_copy(o, i_[:]), "ve")
                    p.release(f"pto{bj}", "ve")
                p.release(f"l0dd{bw}", "te")

    rph = contextlib.ExitStack()
    xr = [rph.enter_context(p.nc.sbuf_tensor(p.name("xr0"), [128, D], F32)) for _ in range(2)]
    xo = [rph.enter_context(p.nc.sbuf_tensor(p.name("xo0"), [128, D], F32)) for _ in range(2)]
    p.wait("sync", "ve", p.cnt["ve"])
    p.wait("vector", "ve", p.cnt["ve"])
    for tt in range(NT):
        b = tt % 2
        xs = "da" if b == 0 else "db"
        ws = "dwa" if b == 0 else "dwb"
        p.acquire("sync", f"r0x{b}")
        lv = p.dma("sync", xs, xr[b][:], xin[128 * tt:128 * (tt + 1), :])
        p.wait("vector", xs, lv)
        p.acquire("vector", f"r0o{b}")
        p.inc("vector", lambda e, o=xo[b], a=xr[b], dl=dtm[tt]: e.tensor_tensor(
            o[:], a[:], dl[:], op=ALU.add), "ve")
        p.release(f"r0x{b}", "ve")
        p.wait("scalar", "ve", p.cnt["ve"])
        p.dma("scalar", ws, xout[128 * tt:128 * (tt + 1), :], xo[b][:])
        p.release(f"r0o{b}", ws)
    rph.close()




def emit_ffn(p, *a, **kw):
    with p.block_scope():
        _emit_ffn_impl(p, *a, **kw)


def emit_l1_attn(p, *a, **kw):
    with p.block_scope():
        _emit_l1_attn_impl(p, *a, **kw)


def emit_l0_attn(p, *a, **kw):
    with p.block_scope():
        _emit_l0_attn_impl(p, *a, **kw)


def finish(p):
    for s in list(p.cnt):
        if p.cnt[s] > 0:
            p.wait("sync", s, p.cnt[s])



def host_consts():
    return np.eye(128, dtype=ml_dtypes.bfloat16)


def host_ones():
    return np.ones((128, 128), dtype=ml_dtypes.bfloat16)


def host_cvec():
    cv = np.zeros((128, NCONST), np.float32)
    cv[:, C_LNEPS] = 1e-5
    cv[:, C_L2EPS] = 1e-6
    cv[:, C_ONE] = 1.0
    cv[:, C_SCALE] = 128.0 ** -0.5
    return cv


def host_pswap():
    P = np.zeros((128, 128), np.float32)
    for i in range(64):
        P[2 * i + 1, 2 * i] = -1.0
        P[2 * i, 2 * i + 1] = 1.0
    return P.astype(ml_dtypes.bfloat16)


def host_rope_tables(core_id):
    rot = 1024
    inv = 1.0 / (10000.0 ** (np.arange(0, rot, 2, dtype=np.float64) / rot))
    pos = np.arange(core_id * R, (core_id + 1) * R, dtype=np.float64)
    fr = pos[:, None] * inv[None, :]
    fr = np.repeat(fr, 2, axis=-1)
    cos = np.cos(fr).T.astype(ml_dtypes.bfloat16)
    sin = np.sin(fr).T.astype(ml_dtypes.bfloat16)
    return np.ascontiguousarray(cos), np.ascontiguousarray(sin)


def host_onesf():
    return np.ones((1, 128), np.float32)


def bcast_row(v):
    return np.ascontiguousarray(np.broadcast_to(np.asarray(v, np.float32), (128, v.shape[0])))




def host_trix():
    t = np.zeros((128, 64), np.float32)
    for s in range(64):
        t[s, s:] = 1.0
        t[64 + s, s:] = 1.0
    return t


def host_masksu():
    return host_trix().astype(ml_dtypes.bfloat16)


def host_onesf2():
    return np.ones((128, 128), np.float32)


def host_conv2d(c):
    return np.ascontiguousarray(c.reshape(KC, 128, 4).transpose(1, 0, 2).reshape(128, KC * 4))


def host_cmask(core_id):
    m = np.zeros((128, 8), np.float32)
    m[:, :core_id] = 1.0
    return m


def scale_2d(s):
    c = s.shape[0] // 128
    return np.ascontiguousarray(s.reshape(c, 128).T)



_GROUPS = [list(range(NCORES))]


def _ein(nc, n, sh, dt=F32):
    return nc.dram_tensor(n, sh, dt, kind="ExternalInput").ap()


def _build_l0attn_prog():
    p = P()
    nc = p.nc
    xin = _ein(nc, "xin", [R, D])
    xhalo = _ein(nc, "xhalo", [3, D])
    Wq, Wk, Wv, Ww = (_ein(nc, n, [D, D]) for n in ("Wq", "Wk", "Wv", "Ww"))
    Wf1, Wf2 = _ein(nc, "Wf1", [D, 128]), _ein(nc, "Wf2", [128, D])
    Wb = _ein(nc, "Wb", [D, 16])
    Wg1, Wg2 = _ein(nc, "Wg1", [D, 128]), _ein(nc, "Wg2", [128, D])
    Wo = _ein(nc, "Wo", [D, D])
    cqd, ckd, cvd = (_ein(nc, n, [128, 64]) for n in ("cqd", "ckd", "cvd"))
    bg2d, nwd = _ein(nc, "bg2d", [128, D]), _ein(nc, "nwd", [128, D])
    trixd = _ein(nc, "trixd", [128, 64])
    maskd = _ein(nc, "maskd", [128, 64], BF16)
    cmd = _ein(nc, "cmd", [128, 8])
    onf2 = _ein(nc, "onf2", [128, 128])
    idc = _ein(nc, "idc", [128, 128], BF16)
    onc = _ein(nc, "onc", [128, 128], BF16)
    cvec = _ein(nc, "cvec", [128, NCONST])
    xout = nc.dram_tensor("xout", [R, D], F32, kind="ExternalOutput").ap()
    emit_setup(p, cvec, idc, onc)
    cq2 = p.const2sb(cqd, [128, 64], F32, "cq2")
    ck2 = p.const2sb(ckd, [128, 64], F32, "ck2")
    cv2 = p.const2sb(cvd, [128, 64], F32, "cv2")
    bg2b = p.const2sb(bg2d, [128, D], F32, "bg2b")
    nwb = p.const2sb(nwd, [128, D], F32, "nwb")
    trix = p.const2sb(trixd, [128, 64], F32, "trix")
    masksu = p.const2sb(maskd, [128, 64], BF16, "masksu")
    cmask = p.const2sb(cmd, [128, 8], F32, "cmask")
    onesf2 = p.const2sb(onf2, [128, 128], F32, "onesf2")
    emit_l0_attn(p, xin, xhalo, xout, Wq, Wk, Wv, Ww, Wf1, Wf2, Wb, Wg1, Wg2, Wo,
                 cq2, ck2, cv2, bg2b, nwb, trix, masksu, cmask, onesf2, _GROUPS)
    finish(p)
    return p.build()


def _build_ffn_prog(int_weights):
    p = P()
    nc = p.nc
    wdt = I32 if int_weights else F32
    xin = _ein(nc, "xin", [R, D])
    gwT = _ein(nc, "gwT", [D, FFN_I], wdt)
    uwT = _ein(nc, "uwT", [D, FFN_I], wdt)
    dwT = _ein(nc, "dwT", [FFN_I, D], wdt)
    gs = _ein(nc, "gs", [128, NIC])
    us = _ein(nc, "us", [128, NIC])
    ds = _ein(nc, "ds", [128, KC])
    idc = _ein(nc, "idc", [128, 128], BF16)
    onc = _ein(nc, "onc", [128, 128], BF16)
    cvec = _ein(nc, "cvec", [128, NCONST])
    xout = nc.dram_tensor("xout", [R, D], F32, kind="ExternalOutput").ap()
    emit_setup(p, cvec, idc, onc)
    gsc = p.const2sb(gs, [128, NIC], F32, "gsc")
    usc = p.const2sb(us, [128, NIC], F32, "usc")
    dsc = p.const2sb(ds, [128, KC], F32, "dsc")
    emit_ffn(p, xin, xout, gwT, uwT, dwT, gsc, usc, dsc, zero_point=int_weights)
    finish(p)
    return p.build()


def _build_l1attn_prog():
    p = P()
    nc = p.nc
    xin = _ein(nc, "xin", [R, D])
    Win = _ein(nc, "Win", [D, 3 * D])
    Wout = _ein(nc, "Wout", [D, D])
    binc_d = _ein(nc, "binc", [128, 48])
    boutc_d = _ein(nc, "boutc", [128, KC])
    lnwb_d = _ein(nc, "lnwb", [128, D])
    lnbb_d = _ein(nc, "lnbb", [128, D])
    cost_d = _ein(nc, "cost", [1024, R], BF16)
    sint_d = _ein(nc, "sint", [1024, R], BF16)
    idc = _ein(nc, "idc", [128, 128], BF16)
    onc = _ein(nc, "onc", [128, 128], BF16)
    psw = _ein(nc, "psw", [128, 128], BF16)
    onf = _ein(nc, "onf", [1, 128])
    cvec = _ein(nc, "cvec", [128, NCONST])
    xout = nc.dram_tensor("xout", [R, D], F32, kind="ExternalOutput").ap()
    emit_setup(p, cvec, idc, onc, psw, onf)
    binc = p.const2sb(binc_d, [128, 48], F32, "binc")
    boutc = p.const2sb(boutc_d, [128, KC], F32, "boutc")
    lnwb = p.const2sb(lnwb_d, [128, D], F32, "lnwb")
    lnbb = p.const2sb(lnbb_d, [128, D], F32, "lnbb")
    emit_l1_attn(p, xin, xout, Win, Wout, binc, boutc, lnwb, lnbb,
                 cost_d, sint_d, p.onesf, _GROUPS)
    finish(p)
    return p.build()


def _build_fused_prog():
    p = P()
    nc = p.nc
    xin = _ein(nc, "xin", [R, D])
    xhalo = _ein(nc, "xhalo", [3, D])
    Wq, Wk, Wv, Ww = (_ein(nc, n, [D, D]) for n in ("Wq", "Wk", "Wv", "Ww"))
    Wf1, Wf2 = _ein(nc, "Wf1", [D, 128]), _ein(nc, "Wf2", [128, D])
    Wb = _ein(nc, "Wb", [D, 16])
    Wg1, Wg2 = _ein(nc, "Wg1", [D, 128]), _ein(nc, "Wg2", [128, D])
    Wo = _ein(nc, "Wo", [D, D])
    cqd, ckd, cvd = (_ein(nc, n, [128, 64]) for n in ("cqd", "ckd", "cvd"))
    bg2d, nwd = _ein(nc, "bg2d", [128, D]), _ein(nc, "nwd", [128, D])
    trixd = _ein(nc, "trixd", [128, 64])
    maskd = _ein(nc, "maskd", [128, 64], BF16)
    cmd = _ein(nc, "cmd", [128, 8])
    onf2 = _ein(nc, "onf2", [128, 128])
    f0g = _ein(nc, "f0g", [D, FFN_I])
    f0u = _ein(nc, "f0u", [D, FFN_I])
    f0d = _ein(nc, "f0d", [FFN_I, D])
    f0gs, f0us = _ein(nc, "f0gs", [128, NIC]), _ein(nc, "f0us", [128, NIC])
    f0ds = _ein(nc, "f0ds", [128, KC])
    Win = _ein(nc, "Win", [D, 3 * D])
    Wout = _ein(nc, "Wout", [D, D])
    binc_d = _ein(nc, "binc", [128, 48])
    boutc_d = _ein(nc, "boutc", [128, KC])
    lnwb_d, lnbb_d = _ein(nc, "lnwb", [128, D]), _ein(nc, "lnbb", [128, D])
    cost_d = _ein(nc, "cost", [1024, R], BF16)
    sint_d = _ein(nc, "sint", [1024, R], BF16)
    psw = _ein(nc, "psw", [128, 128], BF16)
    onf = _ein(nc, "onf", [1, 128])
    f1g = _ein(nc, "f1g", [D, FFN_I], I32)
    f1u = _ein(nc, "f1u", [D, FFN_I], I32)
    f1d = _ein(nc, "f1d", [FFN_I, D], I32)
    f1gs, f1us = _ein(nc, "f1gs", [128, NIC]), _ein(nc, "f1us", [128, NIC])
    f1ds = _ein(nc, "f1ds", [128, KC])
    idc = _ein(nc, "idc", [128, 128], BF16)
    onc = _ein(nc, "onc", [128, 128], BF16)
    cvec = _ein(nc, "cvec", [128, NCONST])
    xout = nc.dram_tensor("xout", [R, D], F32, kind="ExternalOutput").ap()
    x1 = p.dram("x1", [R, D], F32)
    x2 = p.dram("x2", [R, D], F32)
    x3 = p.dram("x3", [R, D], F32)

    emit_setup(p, cvec, idc, onc, psw, onf)

    with p.block_scope():
        cq2 = p.const2sb_blk(cqd, [128, 64], F32, "cq2", "dc1")
        ck2 = p.const2sb_blk(ckd, [128, 64], F32, "ck2", "dc1")
        cv2 = p.const2sb_blk(cvd, [128, 64], F32, "cv2", "dc1")
        bg2b = p.const2sb_blk(bg2d, [128, D], F32, "bg2b", "dc1")
        nwb = p.const2sb_blk(nwd, [128, D], F32, "nwb", "dc1")
        trix = p.const2sb_blk(trixd, [128, 64], F32, "trix", "dc1")
        masksu = p.const2sb_blk(maskd, [128, 64], BF16, "masksu", "dc1")
        cmask = p.const2sb_blk(cmd, [128, 8], F32, "cmask", "dc1")
        onesf2 = p.const2sb_blk(onf2, [128, 128], F32, "onesf2", "dc1")
        p.wait_blk_consts("dc1")
        _emit_l0_attn_impl(p, xin, xhalo, x1, Wq, Wk, Wv, Ww, Wf1, Wf2, Wb, Wg1, Wg2,
                           Wo, cq2, ck2, cv2, bg2b, nwb, trix, masksu, cmask, onesf2,
                           _GROUPS)
    with p.block_scope():
        f0gsc = p.const2sb_blk(f0gs, [128, NIC], F32, "f0gsc", "dc2")
        f0usc = p.const2sb_blk(f0us, [128, NIC], F32, "f0usc", "dc2")
        f0dsc = p.const2sb_blk(f0ds, [128, KC], F32, "f0dsc", "dc2")
        p.wait_blk_consts("dc2")
        _emit_ffn_impl(p, x1, x2, f0g, f0u, f0d, f0gsc, f0usc, f0dsc)
    with p.block_scope():
        binc = p.const2sb_blk(binc_d, [128, 48], F32, "binc", "dc3")
        boutc = p.const2sb_blk(boutc_d, [128, KC], F32, "boutc", "dc3")
        lnwb = p.const2sb_blk(lnwb_d, [128, D], F32, "lnwb", "dc3")
        lnbb = p.const2sb_blk(lnbb_d, [128, D], F32, "lnbb", "dc3")
        p.wait_blk_consts("dc3")
        _emit_l1_attn_impl(p, x2, x3, Win, Wout, binc, boutc, lnwb, lnbb,
                           cost_d, sint_d, p.onesf, _GROUPS)
    with p.block_scope():
        f1gsc = p.const2sb_blk(f1gs, [128, NIC], F32, "f1gsc", "dc4")
        f1usc = p.const2sb_blk(f1us, [128, NIC], F32, "f1usc", "dc4")
        f1dsc = p.const2sb_blk(f1ds, [128, KC], F32, "f1dsc", "dc4")
        p.wait_blk_consts("dc4")
        _emit_ffn_impl(p, x3, xout, f1g, f1u, f1d, f1gsc, f1usc, f1dsc, zero_point=True)
    finish(p)
    return p.build()


def _fused_maps(inputs, x):
    common = dict(idc=host_consts(), onc=host_ones(), cvec=host_cvec())
    g0 = np.ascontiguousarray(np.asarray(inputs["L0_ffn_gate_w"]).T)
    u0 = np.ascontiguousarray(np.asarray(inputs["L0_ffn_up_w"]).T)
    d0 = np.ascontiguousarray(np.asarray(inputs["L0_ffn_down_w"]).T)
    g1 = np.ascontiguousarray(np.asarray(inputs["L1_ffn_wq_gate"]).T)
    u1 = np.ascontiguousarray(np.asarray(inputs["L1_ffn_wq_up"]).T)
    d1 = np.ascontiguousarray(np.asarray(inputs["L1_ffn_wq_down"]).T)
    maps = []
    for c in range(NCORES):
        xh = x[c * R - 3:c * R] if c > 0 else np.zeros((3, D), np.float32)
        cos, sin = host_rope_tables(c)
        maps.append(dict(
            xin=x[c * R:(c + 1) * R], xhalo=np.ascontiguousarray(xh),
            Wq=inputs["L0_attn_Wq"], Wk=inputs["L0_attn_Wk"], Wv=inputs["L0_attn_Wv"],
            Ww=inputs["L0_attn_Ww"], Wf1=inputs["L0_attn_Wf1"], Wf2=inputs["L0_attn_Wf2"],
            Wb=inputs["L0_attn_Wb"], Wg1=inputs["L0_attn_Wg1"], Wg2=inputs["L0_attn_Wg2"],
            Wo=inputs["L0_attn_Wo"],
            cqd=host_conv2d(np.asarray(inputs["L0_attn_cq"])),
            ckd=host_conv2d(np.asarray(inputs["L0_attn_ck"])),
            cvd=host_conv2d(np.asarray(inputs["L0_attn_cv"])),
            bg2d=bcast_row(np.asarray(inputs["L0_attn_bg2"])),
            nwd=bcast_row(np.tile(np.asarray(inputs["L0_attn_norm_w"]), 16)),
            trixd=host_trix(), maskd=host_masksu(), cmd=host_cmask(c),
            onf2=host_onesf2(),
            f0g=g0, f0u=u0, f0d=d0,
            f0gs=scale_2d(np.asarray(inputs["L0_ffn_gate_s"])),
            f0us=scale_2d(np.asarray(inputs["L0_ffn_up_s"])),
            f0ds=scale_2d(np.asarray(inputs["L0_ffn_down_s"])),
            Win=inputs["L1_attn_W_in"], Wout=inputs["L1_attn_W_out"],
            binc=scale_2d(np.asarray(inputs["L1_attn_b_in"])),
            boutc=scale_2d(np.asarray(inputs["L1_attn_b_out"])),
            lnwb=bcast_row(np.asarray(inputs["L1_attn_ln_w"])),
            lnbb=bcast_row(np.asarray(inputs["L1_attn_ln_b"])),
            cost=cos, sint=sin, psw=host_pswap(), onf=host_onesf(),
            f1g=g1, f1u=u1, f1d=d1,
            f1gs=scale_2d(np.asarray(inputs["L1_ffn_s_gate"])),
            f1us=scale_2d(np.asarray(inputs["L1_ffn_s_up"])),
            f1ds=scale_2d(np.asarray(inputs["L1_ffn_s_down"])),
            **common))
    return maps


FUSED = True


_PROGS = {}


def _prog(key, builder):
    if key not in _PROGS:
        _PROGS[key] = builder()
    return _PROGS[key]


def kernel(**inputs):
    x = np.ascontiguousarray(np.asarray(inputs["L0_attn_hidden_states"],
                                        np.float32).reshape(NCORES * R, D))
    if FUSED:
        ncf = _prog("fused", _build_fused_prog)
        res = run_bass_kernel_spmd(ncf, _fused_maps(inputs, x), core_ids=list(range(NCORES)))
        x4 = np.concatenate([res.results[c]["xout"] for c in range(NCORES)], axis=0)
        return x4.reshape(1, NCORES * R, D)
    shards = lambda arr: [arr[c * R:(c + 1) * R] for c in range(NCORES)]
    common = dict(idc=host_consts(), onc=host_ones(), cvec=host_cvec())

    nc1 = _prog("l0a", _build_l0attn_prog)
    maps = []
    for c in range(NCORES):
        xh = x[c * R - 3:c * R] if c > 0 else np.zeros((3, D), np.float32)
        maps.append(dict(
            xin=x[c * R:(c + 1) * R], xhalo=np.ascontiguousarray(xh),
            Wq=inputs["L0_attn_Wq"], Wk=inputs["L0_attn_Wk"], Wv=inputs["L0_attn_Wv"],
            Ww=inputs["L0_attn_Ww"], Wf1=inputs["L0_attn_Wf1"], Wf2=inputs["L0_attn_Wf2"],
            Wb=inputs["L0_attn_Wb"], Wg1=inputs["L0_attn_Wg1"], Wg2=inputs["L0_attn_Wg2"],
            Wo=inputs["L0_attn_Wo"],
            cqd=host_conv2d(np.asarray(inputs["L0_attn_cq"])),
            ckd=host_conv2d(np.asarray(inputs["L0_attn_ck"])),
            cvd=host_conv2d(np.asarray(inputs["L0_attn_cv"])),
            bg2d=bcast_row(np.asarray(inputs["L0_attn_bg2"])),
            nwd=bcast_row(np.tile(np.asarray(inputs["L0_attn_norm_w"]), 16)),
            trixd=host_trix(), maskd=host_masksu(), cmd=host_cmask(c),
            onf2=host_onesf2(), **common))
    res = run_bass_kernel_spmd(nc1, maps, core_ids=list(range(NCORES)))
    x1 = np.concatenate([res.results[c]["xout"] for c in range(NCORES)], axis=0)

    nc2 = _prog("l0f", lambda: _build_ffn_prog(False))
    gwT = np.ascontiguousarray(np.asarray(inputs["L0_ffn_gate_w"]).T)
    uwT = np.ascontiguousarray(np.asarray(inputs["L0_ffn_up_w"]).T)
    dwT = np.ascontiguousarray(np.asarray(inputs["L0_ffn_down_w"]).T)
    maps = [dict(xin=s, gwT=gwT, uwT=uwT, dwT=dwT,
                 gs=scale_2d(np.asarray(inputs["L0_ffn_gate_s"])),
                 us=scale_2d(np.asarray(inputs["L0_ffn_up_s"])),
                 ds=scale_2d(np.asarray(inputs["L0_ffn_down_s"])), **common)
            for s in shards(x1)]
    res = run_bass_kernel_spmd(nc2, maps, core_ids=list(range(NCORES)))
    x2 = np.concatenate([res.results[c]["xout"] for c in range(NCORES)], axis=0)

    nc3 = _prog("l1a", _build_l1attn_prog)
    maps = []
    for c, s in enumerate(shards(x2)):
        cos, sin = host_rope_tables(c)
        maps.append(dict(
            xin=s, Win=inputs["L1_attn_W_in"], Wout=inputs["L1_attn_W_out"],
            binc=scale_2d(np.asarray(inputs["L1_attn_b_in"])),
            boutc=scale_2d(np.asarray(inputs["L1_attn_b_out"])),
            lnwb=bcast_row(np.asarray(inputs["L1_attn_ln_w"])),
            lnbb=bcast_row(np.asarray(inputs["L1_attn_ln_b"])),
            cost=cos, sint=sin, psw=host_pswap(), onf=host_onesf(), **common))
    res = run_bass_kernel_spmd(nc3, maps, core_ids=list(range(NCORES)))
    x3 = np.concatenate([res.results[c]["xout"] for c in range(NCORES)], axis=0)

    nc4 = _prog("l1f", lambda: _build_ffn_prog(True))
    gwT = np.ascontiguousarray(np.asarray(inputs["L1_ffn_wq_gate"]).T)
    uwT = np.ascontiguousarray(np.asarray(inputs["L1_ffn_wq_up"]).T)
    dwT = np.ascontiguousarray(np.asarray(inputs["L1_ffn_wq_down"]).T)
    maps = [dict(xin=s, gwT=gwT, uwT=uwT, dwT=dwT,
                 gs=scale_2d(np.asarray(inputs["L1_ffn_s_gate"])),
                 us=scale_2d(np.asarray(inputs["L1_ffn_s_up"])),
                 ds=scale_2d(np.asarray(inputs["L1_ffn_s_down"])), **common)
            for s in shards(x3)]
    res = run_bass_kernel_spmd(nc4, maps, core_ids=list(range(NCORES)))
    x4 = np.concatenate([res.results[c]["xout"] for c in range(NCORES)], axis=0)
    return x4.reshape(1, NCORES * R, D)


# version 06
# speedup vs baseline: 1.0524x; 1.0524x over previous
import contextlib
import numpy as np
import ml_dtypes
import concourse.bass as bass
import concourse.mybir as mybir
from concourse.bass_utils import run_bass_kernel_spmd

F32 = mybir.dt.float32
BF16 = mybir.dt.bfloat16
I32 = mybir.dt.int32
AF = mybir.ActivationFunctionType
ALU = mybir.AluOpType
AX = mybir.AxisListType

NCORES = 8
R = 2048
D = 2048
NT = R // 128
KC = D // 128
FFN_I = 5632
NIC = FFN_I // 128
LN_EPS = 1e-5

ENGINES = ("sync", "gpsimd", "tensor", "vector", "scalar")
DMA_SEM = {"sync": "lq", "gpsimd": "gq", "scalar": "wq", "vector": "vq"}


class P:

    def __init__(self):
        self.nc = bass.Bass("TRN2", target_bir_lowering=False, debug=False)
        self.stk = contextlib.ExitStack()
        self.q = {e: [] for e in ENGINES}
        self.sems = {}
        self.cnt = {}
        self.rel = {}
        self.uid = 0
        self.consts = {}

    def name(self, base):
        self.uid += 1
        return f"{base}_{self.uid}"

    def sem(self, name):
        if name not in self.sems:
            self.sems[name] = self.stk.enter_context(self.nc.semaphore(name))
            self.cnt[name] = 0
        return self.sems[name]

    def sb(self, base, shape, dtype):
        stk = self._scopes[-1] if getattr(self, "_scopes", None) else self.stk
        return stk.enter_context(self.nc.sbuf_tensor(self.name(base), shape, dtype))

    def ps(self, base, shape, dtype=F32):
        stk = self._scopes[-1] if getattr(self, "_scopes", None) else self.stk
        return stk.enter_context(self.nc.psum_tensor(self.name(base), shape, dtype))

    @contextlib.contextmanager
    def block_scope(self):
        if not hasattr(self, "_scopes"):
            self._scopes = []
        blk = contextlib.ExitStack()
        self._scopes.append(blk)
        try:
            yield
        except BaseException:
            self._scopes.pop()
            raise
        else:
            self._scopes.pop()
            blk.close()
            self.barrier(sems=tuple(s for s in self.cnt))

    def const2sb_blk(self, ap, shape, dtype, base, sem):
        t = self.sb(base, shape, dtype)
        self.dma("sync", sem, t[:], ap)
        return t

    def wait_blk_consts(self, sem):
        for eng in ("tensor", "vector", "scalar"):
            self.wait(eng, sem, self.cnt[sem])

    def dram(self, base, shape, dtype, kind="Internal"):
        if kind == "Internal" and base in getattr(self, "dbg_out", ()):
            ap = self.nc.dram_tensor("dbg_" + base, shape, dtype, kind="ExternalOutput").ap()
            return ap
        return self.nc.dram_tensor(self.name(base), shape, dtype, kind=kind).ap()

    def on(self, eng, fn):
        self.q[eng].append(fn)

    def phase(self, name):
        for e in ENGINES:
            self.q[e].append(("phase", name))

    def inc(self, eng, fn, sem, n=1):
        h = self.sem(sem)
        self.cnt[sem] += n
        self.q[eng].append(lambda e, fn=fn, h=h, n=n: fn(e).then_inc(h, n))
        return self.cnt[sem]

    def wait(self, eng, sem, val=None):
        h = self.sem(sem)
        if val is None:
            val = self.cnt[sem]
        if val > 0:
            self.q[eng].append(lambda e, h=h, v=val: e.wait_ge(h, v))

    def dma(self, eng, sem, out, in_, **kw):
        return self.inc(eng, lambda e, o=out, i=in_, kw=kw: e.dma_start(out=o, in_=i, **kw), sem, 16)

    def release(self, key, sem):
        self.rel[key] = [(sem, self.cnt[sem])]

    def release_add(self, key, sem):
        self.rel.setdefault(key, []).append((sem, self.cnt[sem]))

    def acquire(self, eng, key):
        for sem, val in self.rel.get(key, ()):
            self.wait(eng, sem, val)

    def barrier(self, engines=ENGINES, sems=("te", "ve", "se")):
        for eng in engines:
            for s in sems:
                if s in self.cnt and self.cnt[s] > 0:
                    self.wait(eng, s, self.cnt[s])

    def mm_group(self, psum, pairs, sem="te"):
        n = len(pairs)
        for j, (l, r) in enumerate(pairs):
            s, t = j == 0, j == n - 1
            if t:
                v = self.inc("tensor", lambda e, o=psum, l=l, r=r, s=s: e.matmul(o, l, r, start=s, stop=True), sem)
            else:
                self.on("tensor", lambda e, o=psum, l=l, r=r, s=s: e.matmul(o, l, r, start=s, stop=False))
        return v

    def const2sb(self, ap, shape, dtype, base="const"):
        t = self.sb(base, shape, dtype)
        self.dma("sync", "dcn", t[:], ap)
        return t

    def wait_consts(self, eng):
        self.wait(eng, "dcn", self.cnt["dcn"])

    def build(self):
        with self.stk:
            with self.nc.Block() as block:
                for eng in ENGINES:
                    dec = getattr(block, eng)
                    fns = self.q[eng]

                    def body(e, fns=fns):
                        cur = contextlib.ExitStack()
                        try:
                            for f in fns:
                                if isinstance(f, tuple) and f[0] == "phase":
                                    cur.close()
                                    cur = contextlib.ExitStack()
                                    cur.enter_context(self.nc.named_scope(f[1]))
                                else:
                                    f(e)
                        finally:
                            cur.close()
                    dec(body)
        return self.nc



NCONST = 8
C_LNEPS, C_L2EPS, C_ONE, C_SCALE = 0, 1, 2, 3


def emit_setup(p, cvec, identc, onesc, pswapc=None, onesfc=None):
    p.cv = p.const2sb(cvec, [128, NCONST], F32, "cvec")
    p.ident = p.const2sb(identc, [128, 128], BF16, "ident")
    p.ones = p.const2sb(onesc, [128, 128], BF16, "ones")
    if pswapc is not None:
        p.pswap = p.const2sb(pswapc, [128, 128], BF16, "pswap")
    if onesfc is not None:
        p.onesf = p.const2sb(onesfc, [1, 128], F32, "onesf")


def emit_ln(p, xin, h_tm_d, n_tiles=NT):
    ph = contextlib.ExitStack()
    S = lambda b, s, d: ph.enter_context(p.nc.sbuf_tensor(p.name(b), s, d))
    xt = [S("ln_x", [128, D], F32) for _ in range(2)]
    ht = [S("ln_h", [128, D], BF16) for _ in range(2)]
    dump1 = S("ln_dump1", [128, D], BF16)
    dump2 = S("ln_dump2", [128, D], BF16)
    ssum = S("ln_ssum", [128, 1], F32)
    ssq = S("ln_ssq", [128, 1], F32)
    mean = S("ln_mean", [128, 1], F32)
    var = S("ln_var", [128, 1], F32)
    msq = S("ln_msq", [128, 1], F32)
    rstd = S("ln_rstd", [128, 1], F32)

    for tt in range(n_tiles):
        b = tt % 2
        xs = "da" if b == 0 else "db"
        p.acquire("sync", f"ln_x{b}")
        lv = p.dma("sync", xs, xt[b][:], xin[128 * tt:128 * (tt + 1), :])
        p.wait("vector", xs, lv)
        p.wait("scalar", xs, lv)
        sv = p.inc("vector", lambda e, o=dump1, i=xt[b], s=ssum: e.tensor_scalar(
            o[:], i[:], 1.0, 0.0, op0=ALU.mult, op1=ALU.add, accum_out=s[:]), "ve")
        sq = p.inc("scalar", lambda e, o=dump2, i=xt[b], s=ssq: e.activation(
            o[:], i[:], AF.Square, accum_out=s[:]), "se")
        p.wait("vector", "se", sq)
        p.wait("vector", "ve", sv)
        mv = p.inc("vector", lambda e, m=mean, s=ssum: e.tensor_scalar(
            m[:], s[:], 1.0 / D, None, op0=ALU.mult), "ve")
        p.inc("vector", lambda e, v=var, s=ssq: e.tensor_scalar(
            v[:], s[:], 1.0 / D, None, op0=ALU.mult), "ve")
        p.wait("vector", "ve", mv)
        mqv = p.inc("vector", lambda e, m=msq, a=mean: e.tensor_tensor(
            m[:], a[:], a[:], op=ALU.mult), "ve")
        p.wait("vector", "ve", mqv)
        vv = p.inc("vector", lambda e, v=var, m=msq: e.tensor_tensor(
            v[:], v[:], m[:], op=ALU.subtract), "ve")
        p.wait("scalar", "ve", vv)
        sqv = p.inc("scalar", lambda e, v=var, r=rstd, c=p.cv: e.activation(
            r[:], v[:], AF.Sqrt, bias=c[:, C_LNEPS:C_LNEPS + 1]), "se")
        p.wait("vector", "se", sqv)
        rv = p.inc("vector", lambda e, r=rstd: e.reciprocal(r[:], r[:]), "ve")
        p.acquire("vector", f"ln_h{b}")
        p.wait("vector", "ve", rv)
        av = p.inc("vector", lambda e, h=ht[b], x=xt[b], m=mean, r=rstd: e.tensor_scalar(
            h[:], x[:], m[:], r[:], op0=ALU.subtract, op1=ALU.mult), "ve")
        p.release(f"ln_x{b}", "ve")
        p.wait("scalar", "ve", av)
        ws = "dwa" if b == 0 else "dwb"
        wv = p.dma("scalar", ws, h_tm_d[128 * tt:128 * (tt + 1), :], ht[b][:])
        p.release(f"ln_h{b}", ws)
    ph.close()


def emit_load_h_fm(p, h_tm_d, hf, t0, tcols):
    for kc in range(KC):
        p.dma("sync", "dtot", hf[kc][:, 0:tcols],
              h_tm_d[t0:t0 + tcols, 128 * kc:128 * (kc + 1)], transpose=True)
    return p.cnt["dtot"]



def _emit_ffn_impl(p, xin, xout, gwT, uwT, dwT, gsc, usc, dsc, zero_point=False):
    h_tm_d = p.dram("hL0f", [R, D], BF16)
    hdn_d = p.dram("hdnL0f", [NIC, 128, R], BF16)
    for eng in ("tensor", "vector", "scalar"):
        p.wait_consts(eng)

    fid = getattr(p, "_ffn_id", 0)
    p._ffn_id = fid + 1
    p.phase(f"ffn{fid}_ln")
    emit_ln(p, xin, h_tm_d)

    p.phase(f"ffn{fid}_gateup")
    dfm_d = p.dram("dfm", [KC * 128, R], BF16)

    with contextlib.ExitStack() as ph:
        hf_all = ph.enter_context(p.nc.sbuf_tensor(p.name("hf"), [128, KC * R], BF16))
        hf = [hf_all[:, R * kc:R * (kc + 1)] for kc in range(KC)]
        NWB = 4
        gw = [ph.enter_context(p.nc.sbuf_tensor(p.name("gw"), [128, KC, 128], BF16)) for _ in range(NWB)]
        uw = [ph.enter_context(p.nc.sbuf_tensor(p.name("uw"), [128, KC, 128], BF16)) for _ in range(NWB)]
        sg = [ph.enter_context(p.nc.sbuf_tensor(p.name("sg"), [128, 512], BF16)) for _ in range(3)]
        hdn_t = [ph.enter_context(p.nc.sbuf_tensor(p.name("hdn"), [128, R], BF16)) for _ in range(2)]
        pg = [ph.enter_context(p.nc.psum_tensor(p.name("pg"), [128, 512], F32)) for _ in range(3)]
        pu = [ph.enter_context(p.nc.psum_tensor(p.name("pu"), [128, 512], F32)) for _ in range(3)]

        p.wait("sync", "dwa", p.cnt["dwa"])
        p.wait("sync", "dwb", p.cnt["dwb"])
        hv = emit_load_h_fm(p, h_tm_d, hf, 0, R)
        p.wait("tensor", "dtot", p.cnt["dtot"])

        sbc = None
        if zero_point:
            sbc_all = ph.enter_context(p.nc.sbuf_tensor(p.name("sbc"), [128, R], BF16))
            sbc = [sbc_all[:, 512 * q:512 * (q + 1)] for q in range(4)]
            psb = ph.enter_context(p.nc.psum_tensor(p.name("psb"), [128, 512], F32))
            for q in range(4):
                tv = p.mm_group(psb[:], [(p.ones[:], hf[kc][:, 512 * q:512 * (q + 1)])
                                         for kc in range(KC)])
                p.wait("scalar", "te", tv)
                sv = p.inc("scalar", lambda e, o=sbc[q], i_=psb: e.activation(
                    o, i_[:], AF.Copy, scale=-1.0), "se")
                p.wait("tensor", "se", sv)

        WSEM = ["dc", "dd", "dc2", "dd2"]

        def gu_load(i):
            bi = i % NWB
            ws = WSEM[bi]
            p.acquire("gpsimd", f"gw{bi}")
            p.dma("gpsimd", ws, gw[bi][:],
                  gwT[:, 128 * i:128 * (i + 1)].rearrange("(kc p) j -> p kc j", p=128))
            return p.dma("gpsimd", ws, uw[bi][:],
                         uwT[:, 128 * i:128 * (i + 1)].rearrange("(kc p) j -> p kc j", p=128))

        pw_ = [gu_load(i) for i in range(min(3, NIC))]
        for i in range(NIC):
            bi = i % NWB
            ws = WSEM[bi]
            gv = pw_[i]
            if i + 3 < NIC:
                pw_.append(gu_load(i + 3))
            p.wait("tensor", ws, gv)
            p.acquire("vector", f"hdn{i % 2}")
            for q in range(4):
                bq = (i * 4 + q) % 3
                zp = [(p.ones[:], sbc[q])] if zero_point else []
                p.acquire("tensor", f"pg{bq}")
                tg = p.mm_group(pg[bq][:], [(gw[bi][:, kc, :], hf[kc][:, 512 * q:512 * (q + 1)])
                                            for kc in range(KC)] + zp)
                p.acquire("tensor", f"pu{bq}")
                tu = p.mm_group(pu[bq][:], [(uw[bi][:, kc, :], hf[kc][:, 512 * q:512 * (q + 1)])
                                            for kc in range(KC)] + zp)
                p.wait("scalar", "te", tg)
                p.acquire("scalar", f"sg{bq}")
                ev = p.inc("scalar", lambda e, o=sg[bq], i_=pg[bq], s=gsc[:, i:i + 1]:
                           e.activation(o[:], i_[:], AF.Silu, scale=s), "se")
                p.release(f"pg{bq}", "se")
                p.wait("vector", "te", tu)
                p.wait("vector", "se", ev)
                p.inc("vector", lambda e, o=hdn_t[i % 2][:, 512 * q:512 * (q + 1)], i_=pu[bq],
                      s=usc[:, i:i + 1], g=sg[bq]: e.scalar_tensor_tensor(
                          o, i_[:], s, g[:], op0=ALU.mult, op1=ALU.mult), "ve")
                p.release(f"pu{bq}", "ve")
                p.release(f"sg{bq}", "ve")
            p.release(f"gw{bi}", "te")
            p.wait("scalar", "ve", p.cnt["ve"])
            hs = "dwa" if i % 2 == 0 else "dwb"
            p.dma("scalar", hs, hdn_d[i, :, :], hdn_t[i % 2][:])
            p.release(f"hdn{i % 2}", hs)

    p.phase(f"ffn{fid}_down")
    with contextlib.ExitStack() as ph:
        hh_all = ph.enter_context(p.nc.sbuf_tensor(p.name("hh"), [128, NIC * 1024], BF16))
        hh = [hh_all[:, 1024 * i:1024 * (i + 1)] for i in range(NIC)]
        dw = [ph.enter_context(p.nc.sbuf_tensor(p.name("dw"), [128, NIC, 128], BF16)) for _ in range(4)]
        dd = [ph.enter_context(p.nc.sbuf_tensor(p.name("dd"), [128, 512], BF16)) for _ in range(2)]
        pd = [ph.enter_context(p.nc.psum_tensor(p.name("pd"), [128, 512], F32)) for _ in range(2)]

        sbh_all = ph.enter_context(p.nc.sbuf_tensor(p.name("sbh"), [128, 1024], BF16)) if zero_point else None
        psb2 = ph.enter_context(p.nc.psum_tensor(p.name("psb2"), [128, 512], F32)) if zero_point else None
        for half in range(2):
            p.wait("gpsimd", "dwa", p.cnt["dwa"])
            p.wait("gpsimd", "dwb", p.cnt["dwb"])
            p.acquire("gpsimd", "hh")
            for i in range(NIC):
                p.dma("gpsimd", "dgt", hh[i][:], hdn_d[i, :, 1024 * half:1024 * (half + 1)])
            hv = p.cnt["dgt"]
            p.wait("tensor", "dgt", hv)
            if zero_point:
                sbh = [sbh_all[:, 512 * q:512 * (q + 1)] for q in range(2)]
                p.wait("tensor", "se", p.cnt["se"])
                for q in range(2):
                    tv = p.mm_group(psb2[:], [(p.ones[:], hh[i][:, 512 * q:512 * (q + 1)])
                                              for i in range(NIC)])
                    p.wait("scalar", "te", tv)
                    sv = p.inc("scalar", lambda e, o=sbh[q], i_=psb2: e.activation(
                        o, i_[:], AF.Copy, scale=-1.0), "se")
                    p.wait("tensor", "se", sv)
            def dw_load(d):
                bd = d % 4
                ws = WSEM[bd]
                p.acquire("gpsimd", f"dwb{bd}")
                return p.dma("gpsimd", ws, dw[bd][:],
                             dwT[:, 128 * d:128 * (d + 1)].rearrange("(ic p) j -> p ic j", p=128))

            pdw = [dw_load(d) for d in range(3)]
            for d in range(KC):
                bd = d % 4
                ws = WSEM[bd]
                wv = pdw[d]
                if d + 3 < KC:
                    pdw.append(dw_load(d + 3))
                p.wait("tensor", ws, wv)
                for q in range(2):
                    bq = q % 2
                    zp = [(p.ones[:], sbh[q])] if zero_point else []
                    p.acquire("tensor", f"pd{bq}")
                    tv = p.mm_group(pd[bq][:], [(dw[bd][:, i, :], hh[i][:, 512 * q:512 * (q + 1)])
                                                for i in range(NIC)] + zp)
                    p.wait("scalar", "te", tv)
                    p.acquire("scalar", f"dd{bq}")
                    sv = p.inc("scalar", lambda e, o=dd[bq], i_=pd[bq], s=dsc[:, d:d + 1]:
                               e.activation(o[:], i_[:], AF.Copy, scale=s), "se")
                    p.release(f"pd{bq}", "se")
                    p.wait("scalar", "se", sv)
                    ws = "dwa" if bq == 0 else "dwb"
                    p.dma("scalar", ws, dfm_d[128 * d:128 * (d + 1), 1024 * half + 512 * q:1024 * half + 512 * (q + 1)],
                          dd[bq][:])
                    p.release(f"dd{bq}", ws)
                p.release(f"dwb{bd}", "te")
            p.release("hh", "te")

    p.phase(f"ffn{fid}_res")
    rph = contextlib.ExitStack()
    xr = [rph.enter_context(p.nc.sbuf_tensor(p.name("xr"), [128, D], F32)) for _ in range(2)]
    xo = [rph.enter_context(p.nc.sbuf_tensor(p.name("xo"), [128, D], F32)) for _ in range(2)]
    dtt = [rph.enter_context(p.nc.sbuf_tensor(p.name("dtt"), [128, D], BF16)) for _ in range(2)]
    p.wait("sync", "dwa", p.cnt["dwa"])
    p.wait("sync", "dwb", p.cnt["dwb"])
    for tt in range(NT):
        b = tt % 2
        xs = "da" if b == 0 else "db"
        ds_ = "dta" if b == 0 else "dtb"
        ws = "dwa" if b == 0 else "dwb"
        p.acquire("sync", f"res_x{b}")
        lv = p.dma("sync", xs, xr[b][:], xin[128 * tt:128 * (tt + 1), :])
        p.dma("sync", ds_, dtt[b][:], dfm_d[:, 128 * tt:128 * (tt + 1)], transpose=True)
        dv = p.cnt[ds_]
        p.wait("vector", xs, lv)
        p.wait("vector", ds_, dv)
        p.acquire("vector", f"res_o{b}")
        p.inc("vector", lambda e, o=xo[b], a=xr[b], dl=dtt[b]: e.tensor_tensor(
            o[:], a[:], dl[:], op=ALU.add), "ve")
        p.release(f"res_x{b}", "ve")
        p.wait("scalar", "ve", p.cnt["ve"])
        p.dma("scalar", ws, xout[128 * tt:128 * (tt + 1), :], xo[b][:])
        p.release(f"res_o{b}", ws)
    rph.close()



def emit_ln2_l1(p, xin, X_d, h2_d, lnwb, lnbb):
    ph = contextlib.ExitStack()
    S = lambda b, s, d: ph.enter_context(p.nc.sbuf_tensor(p.name(b), s, d))
    xt = [S("l1x", [128, D], F32) for _ in range(2)]
    Xt = [S("l1X", [128, D], F32) for _ in range(2)]
    ht = [S("l1h", [128, D], BF16) for _ in range(2)]
    tmp = S("l1tmp", [128, D], F32)
    dump1 = S("l1d1", [128, D], BF16)
    dump2 = S("l1d2", [128, D], BF16)
    ssum = S("l1ssum", [128, 1], F32)
    ssq = S("l1ssq", [128, 1], F32)
    mean = S("l1mean", [128, 1], F32)
    var = S("l1var", [128, 1], F32)
    msq = S("l1msq", [128, 1], F32)
    rstd = S("l1rstd", [128, 1], F32)

    def stats(src_tile):
        sv = p.inc("vector", lambda e, o=dump1, i=src_tile, s=ssum: e.tensor_scalar(
            o[:], i[:], 1.0, 0.0, op0=ALU.mult, op1=ALU.add, accum_out=s[:]), "ve")
        sq = p.inc("scalar", lambda e, o=dump2, i=src_tile, s=ssq: e.activation(
            o[:], i[:], AF.Square, accum_out=s[:]), "se")
        p.wait("vector", "se", sq)
        p.wait("vector", "ve", sv)
        mv = p.inc("vector", lambda e, m=mean, s=ssum: e.tensor_scalar(
            m[:], s[:], 1.0 / D, None, op0=ALU.mult), "ve")
        p.inc("vector", lambda e, v=var, s=ssq: e.tensor_scalar(
            v[:], s[:], 1.0 / D, None, op0=ALU.mult), "ve")
        p.wait("vector", "ve", mv)
        mqv = p.inc("vector", lambda e, m=msq, a=mean: e.tensor_tensor(
            m[:], a[:], a[:], op=ALU.mult), "ve")
        p.wait("vector", "ve", mqv)
        vv = p.inc("vector", lambda e, v=var, m=msq: e.tensor_tensor(
            v[:], v[:], m[:], op=ALU.subtract), "ve")
        p.wait("scalar", "ve", vv)
        sqv = p.inc("scalar", lambda e, v=var, r=rstd, c=p.cv: e.activation(
            r[:], v[:], AF.Sqrt, bias=c[:, C_LNEPS:C_LNEPS + 1]), "se")
        p.wait("vector", "se", sqv)
        rv = p.inc("vector", lambda e, r=rstd: e.reciprocal(r[:], r[:]), "ve")
        p.wait("vector", "ve", rv)

    for tt in range(NT):
        b = tt % 2
        xs = "da" if b == 0 else "db"
        ws = "dwa" if b == 0 else "dwb"
        p.acquire("sync", f"l1x{b}")
        lv = p.dma("sync", xs, xt[b][:], xin[128 * tt:128 * (tt + 1), :])
        p.wait("vector", xs, lv)
        p.wait("scalar", xs, lv)
        stats(xt[b])
        p.acquire("vector", f"l1X{b}")
        xv = p.inc("vector", lambda e, o=Xt[b], x=xt[b], m=mean, r=rstd: e.tensor_scalar(
            o[:], x[:], m[:], r[:], op0=ALU.subtract, op1=ALU.mult), "ve")
        p.release(f"l1x{b}", "ve")
        p.wait("scalar", "ve", xv)
        p.dma("scalar", ws, X_d[128 * tt:128 * (tt + 1), :], Xt[b][:])
        p.wait("vector", "ve", xv)
        stats(Xt[b])
        nv = p.inc("vector", lambda e, o=tmp, x=Xt[b], m=mean, r=rstd: e.tensor_scalar(
            o[:], x[:], m[:], r[:], op0=ALU.subtract, op1=ALU.mult), "ve")
        p.release(f"l1X{b}", "ve")
        p.release_add(f"l1X{b}", ws)
        p.wait("vector", "ve", nv)
        wv_ = p.inc("vector", lambda e, o=tmp, w=lnwb: e.tensor_tensor(
            o[:], o[:], w[:], op=ALU.mult), "ve")
        p.wait("vector", "ve", wv_)
        p.acquire("vector", f"l1h{b}")
        hv = p.inc("vector", lambda e, o=ht[b], t=tmp, bb=lnbb: e.tensor_tensor(
            o[:], t[:], bb[:], op=ALU.add), "ve")
        p.wait("scalar", "ve", hv)
        ws2 = "dwc" if b == 0 else "dwd"
        p.dma("scalar", ws2, h2_d[128 * tt:128 * (tt + 1), :], ht[b][:])
        p.release(f"l1h{b}", ws2)
    ph.close()


def _emit_l1_attn_impl(p, xin, xout, WinAp, WoutAp, binc, boutc, lnwb, lnbb,
                 costab, sintab, onesf, replica_groups):
    X_d = p.dram("Xl1", [R, D], F32)
    h2_d = p.dram("h2l1", [R, D], BF16)
    Q_d = p.dram("Ql1", [KC, 128, R], BF16)
    ccin = p.dram("ccin", [128, KC], F32)
    ccout = p.dram("ccout", [128, KC], F32)
    for eng in ("tensor", "vector", "scalar"):
        p.wait_consts(eng)

    p.phase("l1a_ln")
    emit_ln2_l1(p, xin, X_d, h2_d, lnwb, lnbb)

    dfm_d = p.dram("l1dfm", [KC * 128, R], BF16)
    accA = p.sb("accA", [128, KC], F32)
    zv = p.inc("vector", lambda e, o=accA: e.memset(o[:], 0.0), "ve")

    p.phase("l1a_p1")
    with contextlib.ExitStack() as ph:
        S = lambda b, s, d: ph.enter_context(p.nc.sbuf_tensor(p.name(b), s, d))
        hf2 = [S("hf2", [128, 512], BF16) for _ in range(KC)]
        qsb = [S("qsb", [128, 512], BF16) for _ in range(KC)]
        ksb = [S("ksb", [128, 512], BF16) for _ in range(KC)]
        vsb = [S("vsb", [128, 512], BF16) for _ in range(KC)]
        ctb = [S("ctb", [128, 512], BF16) for _ in range(8)]
        stb = [S("stb", [128, 512], BF16) for _ in range(8)]
        wcol = [S("wcol", [128, KC, 128], BF16) for _ in range(4)]
        rsc1 = S("rsc1", [128, 512], F32)
        rsc2 = S("rsc2", [128, 512], F32)
        ksq = [S("ksq", [128, 512], BF16) for _ in range(2)]
        knrm = S("knrm", [128, 512], BF16)
        n2row = S("n2row", [1, 512], F32)
        rdump = S("rdump", [128, 512], F32)
        rcol = S("rcol", [128, 1], F32)
        pqkv = [ph.enter_context(p.nc.psum_tensor(p.name("pqkv"), [128, 512], F32)) for _ in range(2)]
        prh = ph.enter_context(p.nc.psum_tensor(p.name("prh"), [128, 512], F32))
        pn2 = ph.enter_context(p.nc.psum_tensor(p.name("pn2"), [1, 512], F32))
        prb = ph.enter_context(p.nc.psum_tensor(p.name("prb"), [128, 512], F32))

        for q in range(4):
            t0 = 512 * q
            p.wait("sync", "dwa", p.cnt["dwa"])
            p.wait("sync", "dwb", p.cnt["dwb"])
            p.wait("sync", "dwc", p.cnt["dwc"])
            p.wait("sync", "dwd", p.cnt["dwd"])
            for kc in range(KC):
                p.acquire("sync", f"hf2_{kc}")
                p.dma("sync", "dtot", hf2[kc][:], h2_d[t0:t0 + 512, 128 * kc:128 * (kc + 1)],
                      transpose=True)
            for c in range(8):
                p.acquire("sync", f"rt{c}")
                p.dma("sync", "dtot", ctb[c][:], costab[128 * c:128 * (c + 1), t0:t0 + 512])
                p.dma("sync", "dtot", stb[c][:], sintab[128 * c:128 * (c + 1), t0:t0 + 512])
            hv = p.cnt["dtot"]
            p.wait("tensor", "dtot", hv)
            p.wait("vector", "dtot", hv)
            WS4 = ["dc", "dd", "dc2", "dd2"]

            def qkv_load(oc):
                bw = oc % 4
                p.acquire("gpsimd", f"wcol{bw}")
                return p.dma("gpsimd", WS4[bw], wcol[bw][:],
                             WinAp[:, 128 * oc:128 * (oc + 1)].rearrange("(kc p) j -> p kc j", p=128))

            pq_ = [qkv_load(oc) for oc in range(3)]
            for oc in range(3 * KC):
                bw = oc % 4
                wv = pq_[oc]
                if oc + 3 < 3 * KC:
                    pq_.append(qkv_load(oc + 3))
                p.wait("tensor", WS4[bw], wv)
                p.acquire("tensor", f"pqkv{oc % 2}")
                bwp = oc % 2
                tv = p.mm_group(pqkv[bwp][:], [(wcol[bw][:, kc, :], hf2[kc][:]) for kc in range(KC)])
                p.release(f"wcol{bw}", "te")
                dst = (qsb if oc < KC else ksb if oc < 2 * KC else vsb)[oc % KC]
                p.wait("scalar", "te", tv)
                p.acquire("scalar", f"l1t_{oc % KC}_{oc // KC}")
                sv = p.inc("scalar", lambda e, o=dst, i_=pqkv[bwp], bc=binc[:, oc:oc + 1]:
                           e.activation(o[:], i_[:], AF.Identity, bias=bc), "se")
                p.release(f"pqkv{bwp}", "se")
            sev = p.cnt["se"]
            v3 = 0
            for tiles in (qsb, ksb):
                for c in range(8):
                    p.wait("tensor", "se", sev)
                    p.acquire("tensor", "prh")
                    rv = p.inc("tensor", lambda e, o=prh, t=tiles[c], pw=p.pswap:
                               e.matmul(o[:], pw[:], t[:], start=True, stop=True), "te")
                    p.wait("vector", "te", rv)
                    p.wait("vector", "ve", v3)
                    v1 = p.inc("vector", lambda e, o=rsc1, i_=prh, s=stb[c]: e.tensor_tensor(
                        o[:], i_[:], s[:], op=ALU.mult), "ve")
                    p.release("prh", "ve")
                    v2 = p.inc("vector", lambda e, o=rsc2, t=tiles[c], ct=ctb[c]: e.tensor_tensor(
                        o[:], t[:], ct[:], op=ALU.mult), "ve")
                    p.wait("vector", "ve", v2)
                    v3 = p.inc("vector", lambda e, o=tiles[c], a=rsc1, bb=rsc2: e.tensor_tensor(
                        o[:], a[:], bb[:], op=ALU.add), "ve")
            p.wait("scalar", "ve", p.cnt["ve"])
            for kc in range(KC):
                p.dma("scalar", "dwa", Q_d[kc, :, t0:t0 + 512], qsb[kc][:])
            qspill = p.cnt["dwa"]
            vev = p.cnt["ve"]
            p.acquire("tensor", "pn2")
            for c in range(KC):
                bw = c % 2
                p.wait("scalar", "ve", vev)
                p.acquire("scalar", f"ksq{bw}")
                qv = p.inc("scalar", lambda e, o=ksq[bw], i_=ksb[c]: e.activation(
                    o[:], i_[:], AF.Square), "se")
                p.wait("tensor", "se", qv)
                fn = lambda e, o=pn2, l=p.ones[:, 0:1], r=ksq[bw], s=(c == 0), t=(c == KC - 1): \
                    e.matmul(o[:], l, r[:], start=s, stop=t)
                tv = p.inc("tensor", fn, "te")
                p.release(f"ksq{bw}", "te")
            p.wait("vector", "te", p.cnt["te"])
            nv = p.inc("vector", lambda e, o=n2row, i_=pn2: e.tensor_copy(o[:], i_[:]), "ve")
            p.wait("scalar", "ve", nv)
            sv = p.inc("scalar", lambda e, o=n2row: e.activation(o[:], o[:], AF.Sqrt), "se")
            p.wait("vector", "se", sv)
            rv = p.inc("vector", lambda e, o=n2row: e.reciprocal(o[:], o[:]), "ve")
            p.wait("tensor", "ve", rv)
            bv = p.inc("tensor", lambda e, o=prb, l=onesf[:], r=n2row: e.matmul(
                o[:], l, r[:], start=True, stop=True), "te")
            p.wait("vector", "te", bv)
            for c in range(KC):
                k1 = p.inc("vector", lambda e, o=knrm, k=ksb[c], r=prb: e.tensor_tensor(
                    o[:], k[:], r[:], op=ALU.mult), "ve")
                p.wait("vector", "ve", k1)
                k2 = p.inc("vector", lambda e, o=rdump, k=knrm, v=vsb[c]: e.tensor_tensor(
                    o[:], k[:], v[:], op=ALU.mult), "ve")
                p.wait("vector", "ve", k2)
                k2 = p.inc("vector", lambda e, o=rcol, i_=rdump: e.tensor_reduce(
                    o[:], i_[:], axis=AX.X, op=ALU.add), "ve")
                p.wait("vector", "ve", k2)
                k3 = p.inc("vector", lambda e, o=accA[:, c:c + 1], rc=rcol: e.tensor_tensor(
                    o, o, rc[:], op=ALU.add), "ve")
            vv_all = p.cnt["ve"]
            p.release("pn2", "ve")
            for kc in range(KC):
                p.rel[f"hf2_{kc}"] = [("te", p.cnt["te"])]
                p.rel[f"l1t_{kc}_0"] = [("dwa", qspill)]
                p.rel[f"l1t_{kc}_1"] = [("ve", vv_all)]
                p.rel[f"l1t_{kc}_2"] = [("ve", vv_all)]
            for c in range(8):
                p.rel[f"rt{c}"] = [("ve", vv_all)]

    p.wait("gpsimd", "ve", p.cnt["ve"])
    p.dma("gpsimd", "dcc", ccin, accA[:])
    p.wait("gpsimd", "dcc", p.cnt["dcc"])
    p.inc("gpsimd", lambda e, i_=ccin, o_=ccout, rg=replica_groups: e.collective_compute(
        "AllReduce", ALU.add, replica_groups=rg, ins=[i_.opt()], outs=[o_.opt()]), "dcl", 1)
    p.wait("gpsimd", "dcl", p.cnt["dcl"])
    p.dma("gpsimd", "dcc", accA[:], ccout)
    ccv = p.cnt["dcc"]
    p.wait("vector", "dcc", ccv)

    p.phase("l1a_p2")
    p.barrier(sems=("te", "ve", "se", "dwa", "dwb", "dwc", "dwd"))
    with contextlib.ExitStack() as ph:
        S = lambda b, s, d: ph.enter_context(p.nc.sbuf_tensor(p.name(b), s, d))
        ysb = [S("ysb", [128, 512], BF16) for _ in range(KC)]
        wcol = [S("wcol2", [128, KC, 128], BF16) for _ in range(4)]
        dd = [S("l1dd", [128, 512], BF16) for _ in range(2)]
        pw = [ph.enter_context(p.nc.psum_tensor(p.name("pw"), [128, 512], F32)) for _ in range(2)]
        for q in range(4):
            t0 = 512 * q
            p.wait("sync", "dwa", p.cnt["dwa"])
            p.wait("sync", "dwb", p.cnt["dwb"])
            for kc in range(KC):
                p.acquire("sync", f"ysb{kc}")
                p.dma("sync", "dtot", ysb[kc][:], Q_d[kc, :, t0:t0 + 512])
            hv = p.cnt["dtot"]
            p.wait("vector", "dtot", hv)
            for kc in range(KC):
                p.inc("vector", lambda e, o=ysb[kc], a=accA[:, kc:kc + 1]: e.tensor_scalar(
                    o[:], o[:], a, None, op0=ALU.mult), "ve")
            yv = p.cnt["ve"]
            p.wait("tensor", "ve", yv)
            WS4 = ["dc", "dd", "dc2", "dd2"]

            def wout_load(oc):
                bw = oc % 4
                p.acquire("gpsimd", f"wc2{bw}")
                return p.dma("gpsimd", WS4[bw], wcol[bw][:],
                             WoutAp[:, 128 * oc:128 * (oc + 1)].rearrange("(kc p) j -> p kc j", p=128))

            pq2_ = [wout_load(oc) for oc in range(3)]
            for oc in range(KC):
                bw = oc % 4
                wv = pq2_[oc]
                if oc + 3 < KC:
                    pq2_.append(wout_load(oc + 3))
                p.wait("tensor", WS4[bw], wv)
                b2 = oc % 2
                p.acquire("tensor", f"pw{b2}")
                tv = p.mm_group(pw[b2][:], [(wcol[bw][:, kc, :], ysb[kc][:]) for kc in range(KC)])
                p.release(f"wc2{bw}", "te")
                p.wait("scalar", "te", tv)
                p.acquire("scalar", f"l1dd{b2}")
                sv = p.inc("scalar", lambda e, o=dd[b2], i_=pw[b2], bc=boutc[:, oc:oc + 1]:
                           e.activation(o[:], i_[:], AF.Identity, bias=bc), "se")
                p.release(f"pw{b2}", "se")
                p.wait("scalar", "se", sv)
                wsd = "dwa" if b2 == 0 else "dwb"
                p.dma("scalar", wsd, dfm_d[128 * oc:128 * (oc + 1), t0:t0 + 512], dd[b2][:])
                p.release(f"l1dd{b2}", wsd)
            for kc in range(KC):
                p.release(f"ysb{kc}", "te")

    rph = contextlib.ExitStack()
    xr = [rph.enter_context(p.nc.sbuf_tensor(p.name("xr"), [128, D], F32)) for _ in range(2)]
    Xr = [rph.enter_context(p.nc.sbuf_tensor(p.name("Xr"), [128, D], F32)) for _ in range(2)]
    xo = [rph.enter_context(p.nc.sbuf_tensor(p.name("xo"), [128, D], F32)) for _ in range(2)]
    dtt = [rph.enter_context(p.nc.sbuf_tensor(p.name("dtt"), [128, D], BF16)) for _ in range(2)]
    p.wait("sync", "dwa", p.cnt["dwa"])
    p.wait("sync", "dwb", p.cnt["dwb"])
    p.wait("gpsimd", "ve", p.cnt["ve"])
    p.wait("gpsimd", "te", p.cnt["te"])
    p.wait("gpsimd", "se", p.cnt["se"])
    p.wait("gpsimd", "dwa", p.cnt["dwa"])
    p.wait("gpsimd", "dwb", p.cnt["dwb"])
    for tt in range(NT):
        b = tt % 2
        xs = "da" if b == 0 else "db"
        ds_ = "dta" if b == 0 else "dtb"
        Xs = "dc" if b == 0 else "dd"
        ws = "dwa" if b == 0 else "dwb"
        p.acquire("sync", f"r2x{b}")
        lv = p.dma("sync", xs, xr[b][:], xin[128 * tt:128 * (tt + 1), :])
        p.dma("sync", ds_, dtt[b][:], dfm_d[:, 128 * tt:128 * (tt + 1)], transpose=True)
        dv = p.cnt[ds_]
        p.acquire("gpsimd", f"r2X{b}")
        Xv = p.dma("gpsimd", Xs, Xr[b][:], X_d[128 * tt:128 * (tt + 1), :])
        p.wait("vector", xs, lv)
        p.wait("vector", Xs, Xv)
        p.wait("vector", ds_, dv)
        p.acquire("vector", f"r2o{b}")
        a1 = p.inc("vector", lambda e, o=xo[b], a=xr[b], x2=Xr[b]: e.tensor_tensor(
            o[:], a[:], x2[:], op=ALU.add), "ve")
        p.release(f"r2X{b}", "ve")
        p.wait("vector", "ve", a1)
        p.inc("vector", lambda e, o=xo[b], dl=dtt[b]: e.tensor_tensor(
            o[:], o[:], dl[:], op=ALU.add), "ve")
        p.release(f"r2x{b}", "ve")
        p.wait("scalar", "ve", p.cnt["ve"])
        p.dma("scalar", ws, xout[128 * tt:128 * (tt + 1), :], xo[b][:])
        p.release(f"r2o{b}", ws)
    rph.close()



C64 = 64
NCH = R // C64
NG = 4
SCL = 128.0 ** -0.5


def emit_ln_l0(p, xin, xhalo, h_d):
    emit_ln(p, xin, h_d[3:3 + R, :])
    p.barrier(engines=("sync", "vector", "scalar"), sems=("ve", "se", "dwa", "dwb"))
    ph = contextlib.ExitStack()
    S = lambda b, s, d: ph.enter_context(p.nc.sbuf_tensor(p.name(b), s, d))
    hx = S("hlx", [3, D], F32)
    hh = S("hlh", [3, D], BF16)
    hd1 = S("hld1", [3, D], BF16)
    hd2 = S("hld2", [3, D], BF16)
    hsum = S("hlsum", [3, 1], F32)
    hsq = S("hlsq", [3, 1], F32)
    hmean = S("hlmean", [3, 1], F32)
    hvar = S("hlvar", [3, 1], F32)
    hmsq = S("hlmsq", [3, 1], F32)
    hrstd = S("hlrstd", [3, 1], F32)
    lv = p.dma("sync", "da", hx[:], xhalo)
    p.wait("vector", "da", lv)
    p.wait("scalar", "da", lv)
    sv = p.inc("vector", lambda e, o=hd1, i=hx, s=hsum: e.tensor_scalar(
        o[:], i[:], 1.0, 0.0, op0=ALU.mult, op1=ALU.add, accum_out=s[:]), "ve")
    sq = p.inc("scalar", lambda e, o=hd2, i=hx, s=hsq: e.activation(
        o[:], i[:], AF.Square, accum_out=s[:]), "se")
    p.wait("vector", "se", sq)
    p.wait("vector", "ve", sv)
    mv = p.inc("vector", lambda e, m=hmean, s=hsum: e.tensor_scalar(
        m[:], s[:], 1.0 / D, None, op0=ALU.mult), "ve")
    p.inc("vector", lambda e, v=hvar, s=hsq: e.tensor_scalar(
        v[:], s[:], 1.0 / D, None, op0=ALU.mult), "ve")
    p.wait("vector", "ve", mv)
    mqv = p.inc("vector", lambda e, m=hmsq, a=hmean: e.tensor_tensor(
        m[:], a[:], a[:], op=ALU.mult), "ve")
    p.wait("vector", "ve", mqv)
    vv = p.inc("vector", lambda e, v=hvar, m=hmsq: e.tensor_tensor(
        v[:], v[:], m[:], op=ALU.subtract), "ve")
    p.wait("scalar", "ve", vv)
    sqv = p.inc("scalar", lambda e, v=hvar, r=hrstd, c=p.cv: e.activation(
        r[:], v[:], AF.Sqrt, bias=c[0:3, C_LNEPS:C_LNEPS + 1]), "se")
    p.wait("vector", "se", sqv)
    rv = p.inc("vector", lambda e, r=hrstd: e.reciprocal(r[:], r[:]), "ve")
    p.wait("vector", "ve", rv)
    av = p.inc("vector", lambda e, h=hh, x=hx, m=hmean, r=hrstd: e.tensor_scalar(
        h[:], x[:], m[:], r[:], op0=ALU.subtract, op1=ALU.mult), "ve")
    p.wait("scalar", "ve", av)
    p.dma("scalar", "dwa", h_d[0:3, :], hh[:])
    ph.close()


def _emit_l0_attn_impl(p, xin, xhalo, xout, Wq, Wk, Wv, Ww, Wf1, Wf2, Wb, Wg1, Wg2, Wo,
                 cq2d, ck2d, cv2d, bg2b, normwb, trix, masksu, cmask2d, onesf2,
                 replica_groups):
    h_d = p.dram("hL0a", [3 + R, D], BF16)
    qd = p.dram("qdL0", [KC, 128, R], BF16)
    kd = p.dram("kdL0", [KC, 128, R], BF16)
    vd = p.dram("vdL0", [KC, 128, R], BF16)
    wd = p.dram("wdL0", [KC, 128, R], BF16)
    gnd = p.dram("gnd", [R, D], F32)
    gsd = p.dram("gsd", [R, D], BF16)
    Btd = p.dram("Btd", [R, D], BF16)
    Bhd = p.dram("Bhd", [R, D], BF16)
    Kpd = p.dram("Kpd", [R, D], BF16)
    Vpd = p.dram("Vpd", [R, D], BF16)
    Dtd = p.dram("Dtd", [R, D], BF16)
    ecd = p.dram("ecd", [NCH, D], F32)
    ecc = p.dram("ecc", [NCH, 128, 16], F32)
    otd = p.dram("otd", [R, D], BF16)
    nworld = len(replica_groups[0])
    agin = p.dram("agin", [128, 2 * D], BF16)
    agout = p.dram("agout", [nworld * 128, 2 * D], BF16)
    ctin = p.dram("ctin", [1, D], F32)
    ctout = p.dram("ctout", [nworld, D], F32)
    for eng in ("tensor", "vector", "scalar"):
        p.wait_consts(eng)

    p.phase("l0a_ln")
    emit_ln_l0(p, xin, xhalo, h_d)

    beta_sb = p.sb("beta", [128, NT * 16], F32)
    betat = [beta_sb[:, 16 * t:16 * (t + 1)] for t in range(NT)]
    Skb = p.sb("Skb", [128, D], BF16)
    Svb = p.sb("Svb", [128, D], BF16)
    ctot = p.sb("ctot", [1, D], F32)
    hfm_all2 = [p.sb("hfm", [128, KC * 512], BF16) for _ in range(2)]
    hfm2 = [[t[:, 512 * kc:512 * (kc + 1)] for kc in range(KC)] for t in hfm_all2]
    hhalo_all = p.sb("hhalo", [128, KC * 3], BF16)
    hhalo = [hhalo_all[:, 3 * kc:3 * (kc + 1)] for kc in range(KC)]

    with contextlib.ExitStack() as ph:
        S = lambda b, s, d: ph.enter_context(p.nc.sbuf_tensor(p.name(b), s, d))
        htm3 = S("htm3", [3, D], BF16)
        ptp = ph.enter_context(p.nc.psum_tensor(p.name("ptp"), [128, 3], BF16))
        p.wait("sync", "dwa", p.cnt["dwa"])
        p.wait("sync", "dwb", p.cnt["dwb"])
        lv = p.dma("sync", "dtot", htm3[:], h_d[0:3, :])
        p.wait("tensor", "dtot", lv)
        for kc in range(KC):
            p.acquire("tensor", "ptp")
            tv = p.inc("tensor", lambda e, o=ptp, i_=htm3[:, 128 * kc:128 * (kc + 1)],
                       id_=p.ident: e.transpose(o[:], i_, id_[0:3, 0:3]), "te")
            p.wait("vector", "te", tv)
            vv = p.inc("vector", lambda e, o=hhalo[kc], i_=ptp: e.tensor_copy(o, i_[:]), "ve")
            p.release("ptp", "ve")
        p.wait("tensor", "ve", p.cnt["ve"])

    p.phase("l0a_proj")
    vzero = p.inc("vector", lambda e, o=ctot: e.memset(o[:], 0.0), "ve")
    with contextlib.ExitStack() as ph:
        S = lambda b, s, d: ph.enter_context(p.nc.sbuf_tensor(p.name(b), s, d))
        wfull = S("wfull", [128, KC, KC, 128], BF16)
        tails = S("tails", [128, KC * 3], F32)
        acc = [S("acc", [128, 512], F32) for _ in range(4)]
        qf = [S("qf", [128, 512], BF16) for _ in range(4)]
        pp = [ph.enter_context(p.nc.psum_tensor(p.name("pp"), [128, 512], F32)) for _ in range(4)]
        pmini = ph.enter_context(p.nc.psum_tensor(p.name("pmini"), [128, 3], F32))

        for (Wt, convt, dstd) in ((Wq, cq2d, qd), (Wk, ck2d, kd), (Wv, cv2d, vd), (Ww, cv2d, wd)):
            p.wait("gpsimd", "te", p.cnt["te"])
            for oc in range(KC):
                p.dma("gpsimd", "dgt", wfull[:, oc, :, :],
                      Wt[:, 128 * oc:128 * (oc + 1)].rearrange("(kc p) j -> p kc j", p=128))
            wv_ = p.cnt["dgt"]
            p.wait("tensor", "dgt", wv_)
            for oc in range(KC):
                p.acquire("tensor", "pmini")
                tv = p.mm_group(pmini[:], [(wfull[:, oc, kc, :], hhalo[kc][:]) for kc in range(KC)])
                p.wait("vector", "te", tv)
                vv = p.inc("vector", lambda e, o=tails[:, 3 * oc:3 * (oc + 1)], i_=pmini:
                           e.tensor_copy(o, i_[:]), "ve")
                p.release("pmini", "ve")
            def p1_loads(tile):
                bh = tile % 2
                semh = "dta" if bh == 0 else "dtb"
                t0 = 512 * tile
                p.acquire("sync", f"hfmb{bh}")
                for kc in range(KC):
                    p.dma("sync", semh, hfm2[bh][kc][:], h_d[3 + t0:3 + t0 + 512,
                                                             128 * kc:128 * (kc + 1)], transpose=True)
                return p.cnt[semh]

            pendh = p1_loads(0)
            for tile in range(NG):
                t0 = 512 * tile
                bh = tile % 2
                hfm = hfm2[bh]
                semh = "dta" if bh == 0 else "dtb"
                hv = pendh
                if tile + 1 < NG:
                    pendh = p1_loads(tile + 1)
                p.wait("tensor", semh, hv)
                for oc in range(KC):
                    bq = (tile * KC + oc) % 4
                    p.acquire("tensor", f"pp{bq}")
                    tv = p.mm_group(pp[bq][:], [(wfull[:, oc, kc, :], hfm[kc][:]) for kc in range(KC)])
                    p.wait("scalar", "te", tv)
                    p.acquire("scalar", f"accb{bq}")
                    c3 = p.inc("scalar", lambda e, o=acc[bq], i_=pp[bq], s=convt[:, 4 * oc + 3:4 * oc + 4]:
                               e.activation(o[:], i_[:], AF.Identity, scale=s), "se")
                    p.wait("vector", "se", c3)
                    for i in (2, 1, 0):
                        a0 = 3 - i
                        cv_ = p.inc("vector", lambda e, o=acc[bq][:, a0:512], i_=pp[bq][:, 0:512 - a0],
                              s=convt[:, 4 * oc + i:4 * oc + i + 1], ac=acc[bq][:, a0:512]:
                              e.scalar_tensor_tensor(o, i_, s, ac, op0=ALU.mult, op1=ALU.add), "ve")
                        p.wait("vector", "ve", cv_)
                        cv_ = p.inc("vector", lambda e, o=acc[bq][:, 0:3 - i], t_=tails[:, 3 * oc + i:3 * oc + 3],
                              s=convt[:, 4 * oc + i:4 * oc + i + 1]:
                              e.scalar_tensor_tensor(o, t_, s, o, op0=ALU.mult, op1=ALU.add), "ve")
                        p.wait("vector", "ve", cv_)
                    tnv = p.inc("vector", lambda e, o=tails[:, 3 * oc:3 * (oc + 1)], i_=pp[bq][:, 509:512]:
                                e.tensor_copy(o, i_), "ve")
                    p.release(f"pp{bq}", "ve")
                    p.wait("scalar", "ve", tnv)
                    p.acquire("scalar", f"qf{bq}")
                    fv = p.inc("scalar", lambda e, o=qf[bq], i_=acc[bq]: e.activation(
                        o[:], i_[:], AF.Silu), "se")
                    p.rel[f"accb{bq}"] = [("se", fv)]
                    ws = ["dwa", "dwb", "dwc", "dwd"][bq]
                    p.wait("scalar", "se", fv)
                    p.dma("scalar", ws, dstd[oc, :, t0:t0 + 512], qf[bq][:])
                    p.release(f"qf{bq}", ws)
                    if oc == KC - 1:
                        p.release(f"hfmb{bh}", "te")

    p.phase("l0a_gates")
    p.barrier(sems=("te", "ve", "se", "dwa", "dwb", "dwc", "dwd"))
    with contextlib.ExitStack() as ph:
        S = lambda b, s, d: ph.enter_context(p.nc.sbuf_tensor(p.name(b), s, d))
        wf1s = S("wf1s", [128, KC, 128], BF16)
        wg1s = S("wg1s", [128, KC, 128], BF16)
        wbs = S("wbs", [128, KC, 16], BF16)
        wf2s = S("wf2s", [128, D], BF16)
        wg2s = S("wg2s", [128, D], BF16)
        f1 = S("f1", [128, 512], BF16)
        g1 = S("g1", [128, 512], BF16)
        sc1 = S("sc1", [128, 512], F32)
        sc2 = S("sc2", [128, 512], F32)
        gnt = S("gnt", [128, D], F32)
        gst = S("gst", [128, D], BF16)
        pj = [ph.enter_context(p.nc.psum_tensor(p.name("pj"), [128, 512], F32)) for _ in range(2)]
        pb = ph.enter_context(p.nc.psum_tensor(p.name("pb"), [128, 16], F32))

        p.wait("gpsimd", "te", p.cnt["te"])
        p.dma("gpsimd", "dgt", wf1s[:], Wf1.rearrange("(kc p) j -> p kc j", p=128))
        p.dma("gpsimd", "dgt", wg1s[:], Wg1.rearrange("(kc p) j -> p kc j", p=128))
        p.dma("gpsimd", "dgt", wbs[:], Wb.rearrange("(kc p) j -> p kc j", p=128))
        p.dma("gpsimd", "dgt", wf2s[:], Wf2)
        wv_ = p.dma("gpsimd", "dgt", wg2s[:], Wg2)
        p.wait("tensor", "dgt", wv_)
        def p2_loads(tile):
            bh = tile % 2
            semh = "dta" if bh == 0 else "dtb"
            t0 = 512 * tile
            p.acquire("sync", f"hfmb{bh}")
            for kc in range(KC):
                p.dma("sync", semh, hfm2[bh][kc][:], h_d[3 + t0:3 + t0 + 512,
                                                         128 * kc:128 * (kc + 1)], transpose=True)
            return p.cnt[semh]

        pendh = p2_loads(0)
        for tile in range(NG):
            t0 = 512 * tile
            bh = tile % 2
            hfm = hfm2[bh]
            semh = "dta" if bh == 0 else "dtb"
            hv = pendh
            if tile + 1 < NG:
                pendh = p2_loads(tile + 1)
            p.wait("tensor", semh, hv)
            p.acquire("tensor", "pj0")
            tv = p.mm_group(pj[0][:], [(wf1s[:, kc, :], hfm[kc][:]) for kc in range(KC)])
            p.wait("scalar", "te", tv)
            p.acquire("scalar", "f1")
            fv = p.inc("scalar", lambda e, o=f1, i_=pj[0]: e.activation(o[:], i_[:], AF.Copy), "se")
            p.release("pj0", "se")
            p.acquire("tensor", "pj1")
            tv = p.mm_group(pj[1][:], [(wg1s[:, kc, :], hfm[kc][:]) for kc in range(KC)])
            p.wait("scalar", "te", tv)
            p.acquire("scalar", "g1")
            gv = p.inc("scalar", lambda e, o=g1, i_=pj[1]: e.activation(o[:], i_[:], AF.Copy), "se")
            p.release("pj1", "se")
            for j in range(4):
                tt_i = tile * 4 + j
                p.acquire("scalar", "gn_sp")
                p.acquire("vector", "gn_sp")
                p.acquire("tensor", "pb")
                tv = p.mm_group(pb[:], [(hfm[kc][:, 128 * j:128 * (j + 1)], wbs[:, kc, :])
                                        for kc in range(KC)])
                p.wait("scalar", "te", tv)
                bv = p.inc("scalar", lambda e, o=betat[tt_i], i_=pb: e.activation(
                    o, i_[:], AF.Sigmoid), "se")
                p.release("pb", "se")
                p.wait("tensor", "se", max(fv, gv))
                for q2 in range(4):
                    cs = slice(512 * q2, 512 * (q2 + 1))
                    p.acquire("tensor", "pj0")
                    tv = p.inc("tensor", lambda e, o=pj[0], l=f1[:, 128 * j:128 * (j + 1)],
                               r=wf2s[:, cs]: e.matmul(o[:], l, r, start=True, stop=True), "te")
                    p.wait("scalar", "te", tv)
                    p.acquire("scalar", "sc1")
                    e1 = p.inc("scalar", lambda e, o=sc1, i_=pj[0]: e.activation(
                        o[:], i_[:], AF.Exp, scale=-1.0), "se")
                    p.release("pj0", "se")
                    p.wait("scalar", "se", e1)
                    e2 = p.inc("scalar", lambda e, o=gnt[:, cs], i_=sc1: e.activation(
                        o, i_[:], AF.Ln, bias=1.0), "se")
                    p.release("sc1", "se")
                    p.acquire("tensor", "pj1")
                    tv = p.inc("tensor", lambda e, o=pj[1], l=g1[:, 128 * j:128 * (j + 1)],
                               r=wg2s[:, cs]: e.matmul(o[:], l, r, start=True, stop=True), "te")
                    p.wait("vector", "te", tv)
                    p.acquire("vector", "sc2")
                    v1 = p.inc("vector", lambda e, o=sc2, i_=pj[1], b=bg2b[:, cs]: e.tensor_tensor(
                        o[:], i_[:], b, op=ALU.add), "ve")
                    p.release("pj1", "ve")
                    p.wait("scalar", "ve", v1)
                    s1 = p.inc("scalar", lambda e, o=sc2: e.activation(o[:], o[:], AF.Sigmoid), "se")
                    p.wait("vector", "se", s1)
                    v2 = p.inc("vector", lambda e, o=gst[:, cs], i_=sc2, n=normwb[:, cs]:
                               e.tensor_tensor(o, i_[:], n, op=ALU.mult), "ve")
                    p.release("sc2", "ve")
                sev = p.cnt["se"]
                vev = p.cnt["ve"]
                p.wait("scalar", "se", sev)
                p.wait("scalar", "ve", vev)
                ws = "dwa" if tt_i % 2 == 0 else "dwb"
                p.dma("scalar", ws, gnd[128 * tt_i:128 * (tt_i + 1), :], gnt[:])
                p.dma("scalar", ws, gsd[128 * tt_i:128 * (tt_i + 1), :], gst[:])
                p.release("gn_sp", ws)
                p.rel["f1"] = [("te", p.cnt["te"])]
                p.rel["g1"] = [("te", p.cnt["te"])]
                p.rel["sc1"] = [("se", p.cnt["se"])]
                p.rel["sc2"] = [("ve", p.cnt["ve"])]
                if j == 3:
                    p.release(f"hfmb{bh}", "te")

    p.phase("l0a_sweep1")
    p.barrier(sems=("te", "ve", "se", "dwa", "dwb", "dwc", "dwd"))
    zk = p.inc("vector", lambda e, o=Skb: e.memset(o[:], 0.0), "ve")
    zv2 = p.inc("vector", lambda e, o=Svb: e.memset(o[:], 0.0), "ve")
    with contextlib.ExitStack() as ph:
        S = lambda b, s, d: ph.enter_context(p.nc.sbuf_tensor(p.name(b), s, d))
        ktm2 = [S("ktm", [128, D], BF16) for _ in range(2)]
        vtm2 = [S("vtm", [128, D], BF16) for _ in range(2)]
        wtm2 = [S("wtm", [128, D], BF16) for _ in range(2)]
        gn22 = [S("gn2", [128, D], F32) for _ in range(2)]
        wsq = S("wsq", [128, D], F32)
        sq16 = S("sq16", [128, 16], F32)
        wn = S("wn", [128, D], BF16)
        kp = S("kp", [128, D], BF16)
        vp = S("vp", [128, D], BF16)
        Bt = S("Bt", [128, D], BF16)
        Bh = S("Bh", [128, D], BF16)
        Dt = S("Dt", [128, D], BF16)
        en = S("en", [128, 512], BF16)
        ecrow = S("ecrow", [128, D], F32)
        eccol = S("eccol", [128, 16], F32)
        ctot2 = S("ctot2", [128, D], F32)
        pc_ps = ph.enter_context(p.nc.psum_tensor(p.name("pc"), [128, 512], F32))
        pl_ps = ph.enter_context(p.nc.psum_tensor(p.name("pl"), [128, 512], F32))
        PB = ph.enter_context(p.nc.psum_tensor(p.name("PB"), [128, 512], F32))
        pcol = ph.enter_context(p.nc.psum_tensor(p.name("pcol"), [128, 16], F32))
        psk = ph.enter_context(p.nc.psum_tensor(p.name("psk"), [128, 512], F32))
        psv = ph.enter_context(p.nc.psum_tensor(p.name("psv"), [128, 512], F32))
        zc = p.inc("vector", lambda e, o=ctot2: e.memset(o[:], 0.0), "ve")

        def s1_loads(pr):
            b = pr % 2
            sem = "dta" if b == 0 else "dtb"
            gsem = "dga" if b == 0 else "dgb"
            p.acquire("sync", f"s1buf{b}")
            for kc in range(KC):
                p.dma("sync", sem, ktm2[b][:, 128 * kc:128 * (kc + 1)],
                      kd[kc, :, 128 * pr:128 * pr + 128], transpose=True)
                p.dma("sync", sem, vtm2[b][:, 128 * kc:128 * (kc + 1)],
                      vd[kc, :, 128 * pr:128 * pr + 128], transpose=True)
                p.dma("sync", sem, wtm2[b][:, 128 * kc:128 * (kc + 1)],
                      wd[kc, :, 128 * pr:128 * pr + 128], transpose=True)
            p.acquire("gpsimd", f"s1buf{b}")
            p.dma("gpsimd", gsem, gn22[b][:], gnd[128 * pr:128 * pr + 128, :])
            return p.cnt[sem], p.cnt[gsem]

        pend = s1_loads(0)
        for pr in range(NT):
            tp0 = 128 * pr
            b = pr % 2
            ktm, vtm, wtm, gn2 = ktm2[b], vtm2[b], wtm2[b], gn22[b]
            lv, gv_ = pend
            if pr + 1 < NT:
                pend = s1_loads(pr + 1)
            sem = "dta" if b == 0 else "dtb"
            gsem = "dga" if b == 0 else "dgb"
            for eng in ("tensor", "vector", "scalar"):
                p.wait(eng, sem, lv)
                p.wait(eng, gsem, gv_)
            for hc in range(2):
                ci = 2 * pr + hc
                rs_ = slice(64 * hc, 64 * hc + 64)
                h64 = 64 * hc
                tt_i = pr
                p.acquire("vector", f"sw1sp{hc}")
                p.acquire("scalar", f"sw1sp{hc}")
                wv1 = p.inc("scalar", lambda e, o=wsq[rs_, :], i_=wtm[rs_, :]: e.activation(
                    o, i_, AF.Square), "se")
                p.wait("vector", "se", wv1)
                wv2 = p.inc("vector", lambda e, o=sq16[rs_, :], i_=wsq[rs_, :]: e.tensor_reduce(
                    o, i_.rearrange("p (h j) -> p h j", h=16), axis=AX.X, op=ALU.add), "ve")
                p.wait("scalar", "ve", wv2)
                wv3 = p.inc("scalar", lambda e, o=sq16[rs_, :], c=p.cv[rs_, C_L2EPS:C_L2EPS + 1]:
                    e.activation(o, o, AF.Sqrt, bias=c), "se")
                p.wait("vector", "se", wv3)
                wv4 = p.inc("vector", lambda e, o=sq16[rs_, :]: e.reciprocal(o, o), "ve")
                p.wait("vector", "ve", wv4)
                for h in range(16):
                    hs = slice(128 * h, 128 * (h + 1))
                    p.inc("vector", lambda e, o=wn[rs_, hs], i_=wtm[rs_, hs], s=sq16[rs_, h:h + 1]:
                          e.tensor_scalar(o, i_, s, None, op0=ALU.mult), "ve")
                    p.inc("vector", lambda e, o=kp[rs_, hs], i_=ktm[rs_, hs],
                          s=betat[tt_i][rs_, h:h + 1]: e.tensor_scalar(o, i_, s, None, op0=ALU.mult), "ve")
                    p.inc("vector", lambda e, o=vp[rs_, hs], i_=vtm[rs_, hs],
                          s=betat[tt_i][rs_, h:h + 1]: e.tensor_scalar(o, i_, s, None, op0=ALU.mult), "ve")
                wv5 = p.cnt["ve"]
                p.wait("tensor", "se", p.cnt["se"])
                p.acquire("tensor", "pcol")
                for h in range(16):
                    fn = lambda e, o=pcol[:, h:h + 1], l=gn2[rs_, 128 * h:128 * (h + 1)], \
                        r=p.cv[rs_, C_ONE:C_ONE + 1]: e.matmul(o, l, r, start=True, stop=True)
                    cv_ = p.inc("tensor", fn, "te")
                p.wait("scalar", "te", p.cnt["te"])
                p.acquire("scalar", "eccolbuf")
                ecv = p.inc("scalar", lambda e, o=eccol, i_=pcol: e.activation(
                    o[:], i_[:], AF.Exp, scale=-1.0), "se")
                p.release("pcol", "se")
                p.wait("scalar", "se", ecv)
                p.dma("scalar", "dwa" if ci % 2 == 0 else "dwb", ecc[ci, :, :], eccol[:])
                for q2 in range(4):
                    cs = slice(512 * q2, 512 * (q2 + 1))
                    p.acquire("tensor", "pc_ps")
                    t1 = p.inc("tensor", lambda e, o=pc_ps[rs_, :], l=trix[rs_, 0:64], r=gn2[rs_, cs]:
                               e.matmul(o, l, r, start=True, stop=True), "te")
                    p.acquire("tensor", "pl_ps")
                    t2 = p.inc("tensor", lambda e, o=pl_ps[h64:h64 + 1, :], l=p.cv[rs_, C_ONE:C_ONE + 1],
                               r=gn2[rs_, cs]: e.matmul(o, l, r, start=True, stop=True), "te")
                    p.wait("scalar", "te", t2)
                    s1 = p.inc("scalar", lambda e, o=Dt[rs_, cs], i_=pc_ps[rs_, :]: e.activation(
                        o, i_, AF.Exp, scale=-1.0), "se")
                    p.acquire("scalar", "en")
                    s2 = p.inc("scalar", lambda e, o=en[rs_, :], i_=pc_ps[rs_, :]: e.activation(
                        o, i_, AF.Exp), "se")
                    p.release("pc_ps", "se")
                    s3 = p.inc("scalar", lambda e, o=ecrow[h64:h64 + 1, cs], i_=pl_ps[h64:h64 + 1, :]:
                               e.activation(o, i_, AF.Exp, scale=-1.0), "se")
                    p.wait("vector", "te", t2)
                    p.wait("vector", "ve", wv5)
                    ct1 = p.inc("vector", lambda e, o=ctot2[h64:h64 + 1, cs], i_=pl_ps[h64:h64 + 1, :]:
                                e.tensor_tensor(o, o, i_, op=ALU.add), "ve")
                    p.release("pl_ps", "ve")
                    p.wait("vector", "se", s2)
                    bt1 = p.inc("vector", lambda e, o=Bt[rs_, cs], a=wn[rs_, cs], b=en[rs_, :]:
                                e.tensor_tensor(o, a, b, op=ALU.mult), "ve")
                    p.release("en", "ve")
                    p.wait("tensor", "se", s3)
                    p.acquire("tensor", "PB")
                    t3 = p.inc("tensor", lambda e, o=PB[:], l=onesf2[h64:h64 + 1, 0:128],
                               r=ecrow[h64:h64 + 1, cs]: e.matmul(o[:], l, r, start=True, stop=True), "te")
                    p.wait("vector", "te", t3)
                    p.wait("vector", "ve", bt1)
                    bh1 = p.inc("vector", lambda e, o=Bh[rs_, cs], a=Bt[rs_, cs], b=PB[rs_, :]:
                                e.tensor_tensor(o, b, a, op=ALU.mult), "ve")
                    p.wait("tensor", "ve", bh1)
                    p.acquire("tensor", "psk")
                    p.acquire("tensor", "psv")
                    for hh in range(4):
                        h = 4 * q2 + hh
                        hs = slice(128 * h, 128 * (h + 1))
                        os = slice(128 * hh, 128 * (hh + 1))
                        fn = lambda e, o=psk[:, os], l=kp[rs_, hs], r=Bh[rs_, hs]: \
                            e.matmul(o, l, r, start=True, stop=True)
                        p.inc("tensor", fn, "te")
                        fn = lambda e, o=psv[:, os], l=Bh[rs_, hs], r=vp[rs_, hs]: \
                            e.matmul(o, l, r, start=True, stop=True)
                        p.inc("tensor", fn, "te")
                    tsv = p.cnt["te"]
                    p.wait("vector", "te", tsv)
                    sk1 = p.inc("vector", lambda e, o=Skb[:, cs], b=PB[:]: e.tensor_tensor(
                        o, b, o, op=ALU.mult), "ve")
                    p.release("PB", "ve")
                    p.wait("vector", "ve", sk1)
                    sk2 = p.inc("vector", lambda e, o=Skb[:, cs], b=psk[:]: e.tensor_tensor(
                        o, b, o, op=ALU.add), "ve")
                    p.wait("vector", "ve", sk2)
                    p.release("psk", "ve")
                    for hh in range(4):
                        h = 4 * q2 + hh
                        hs = slice(128 * h, 128 * (h + 1))
                        os = slice(128 * hh, 128 * (hh + 1))
                        vv_ = p.inc("vector", lambda e, o=Svb[:, hs], s=eccol[:, h:h + 1],
                                    b=psv[:, os]: e.scalar_tensor_tensor(
                                        o, o, s, b, op0=ALU.mult, op1=ALU.add), "ve")
                        p.wait("vector", "ve", vv_)
                    p.release("psv", "ve")
                    p.release_add("eccolbuf", "ve")
                vv_all = p.cnt["ve"]
                se_all = p.cnt["se"]
                p.wait("scalar", "ve", vv_all)
                p.wait("scalar", "se", se_all)
                ws = "dwa" if ci % 2 == 0 else "dwb"
                p.dma("scalar", ws, Btd[64 * ci:64 * ci + 64, :], Bt[rs_, :])
                p.dma("scalar", ws, Bhd[64 * ci:64 * ci + 64, :], Bh[rs_, :])
                p.dma("scalar", ws, Kpd[64 * ci:64 * ci + 64, :], kp[rs_, :])
                p.dma("scalar", ws, Vpd[64 * ci:64 * ci + 64, :], vp[rs_, :])
                p.dma("scalar", ws, Dtd[64 * ci:64 * ci + 64, :], Dt[rs_, :])
                p.dma("scalar", ws, ecd[ci:ci + 1, :], ecrow[h64:h64 + 1, :])
                p.release(f"sw1sp{hc}", ws)
                p.release("eccolbuf", ws)
                p.release_add("eccolbuf", "ve")
                if hc == 1:
                    p.release(f"s1buf{b}", "te")
                    p.release_add(f"s1buf{b}", "ve")
                    p.release_add(f"s1buf{b}", "se")
        p.wait("tensor", "ve", p.cnt["ve"])
        for q2 in range(4):
            cs = slice(512 * q2, 512 * (q2 + 1))
            p.acquire("tensor", "PB")
            tv = p.inc("tensor", lambda e, o=PB[0:1, :], l=onesf2[:, 0:1], r=ctot2[:, cs]:
                       e.matmul(o, l, r, start=True, stop=True), "te")
            p.wait("vector", "te", tv)
            cv_ = p.inc("vector", lambda e, o=ctot[0:1, cs], i_=PB[0:1, :]: e.tensor_copy(
                o, i_), "ve")
            p.release("PB", "ve")
            p.wait("vector", "ve", cv_)

    p.phase("l0a_gather")
    p.barrier(sems=("te", "ve", "se", "dwa", "dwb", "dts"))
    p.wait("gpsimd", "ve", p.cnt["ve"])
    p.dma("gpsimd", "dcc", agin[:, 0:D], Skb[:])
    p.dma("gpsimd", "dcc", agin[:, D:2 * D], Svb[:])
    agv = p.dma("gpsimd", "dcc", ctin, ctot[:])
    p.wait("gpsimd", "dcc", p.cnt["dcc"])
    p.inc("gpsimd", lambda e, i_=agin, o_=agout, rg=replica_groups: e.collective_compute(
        "AllGather", ALU.bypass, replica_groups=rg, ins=[i_.opt()], outs=[o_.opt()]), "dcl", 1)
    p.inc("gpsimd", lambda e, i_=ctin, o_=ctout, rg=replica_groups: e.collective_compute(
        "AllGather", ALU.bypass, replica_groups=rg, ins=[i_.opt()], outs=[o_.opt()]), "dcl", 1)
    p.wait("gpsimd", "dcl", p.cnt["dcl"])
    with contextlib.ExitStack() as ph:
        S = lambda b, s, d: ph.enter_context(p.nc.sbuf_tensor(p.name(b), s, d))
        Lsb = S("Lsb", [128, 2 * D], BF16)
        Lms = S("Lms", [128, 512], BF16)
        ctj = S("ctj", [1, D], F32)
        erj = S("erj", [1, D], F32)
        ecjc = S("ecjc", [128, 16], F32)
        Sf = S("Sf", [128, 2 * D], F32)
        PE2 = ph.enter_context(p.nc.psum_tensor(p.name("PE2"), [128, 512], F32))
        pco2 = ph.enter_context(p.nc.psum_tensor(p.name("pco2"), [128, 16], F32))
        zf = p.inc("vector", lambda e, o=Sf: e.memset(o[:], 0.0), "ve")
        p.wait("vector", "ve", zf)
        for j in range(nworld - 1):
            p.acquire("gpsimd", "Lsb")
            lj = p.dma("gpsimd", "dcc", Lsb[:], agout[128 * j:128 * (j + 1), :])
            cj = p.dma("gpsimd", "dcc", ctj[:], ctout[j:j + 1, :])
            p.wait("vector", "dcc", p.cnt["dcc"])
            p.wait("scalar", "dcc", p.cnt["dcc"])
            m1 = p.inc("vector", lambda e, o=ctj, s=cmask2d[0:1, j:j + 1]: e.tensor_scalar(
                o[:], o[:], s, None, op0=ALU.mult), "ve")
            p.wait("scalar", "ve", m1)
            m2 = p.inc("scalar", lambda e, o=erj, i_=ctj: e.activation(
                o[:], i_[:], AF.Exp, scale=-1.0), "se")
            p.wait("tensor", "se", m2)
            for h in range(16):
                tv = p.inc("tensor", lambda e, o=pco2[:, h:h + 1], i_=erj[0:1, 128 * h:128 * (h + 1)],
                           id_=p.cv[0:1, C_ONE:C_ONE + 1]: e.transpose(o, i_, id_), "te")
            p.wait("vector", "te", p.cnt["te"])
            m3 = p.inc("vector", lambda e, o=ecjc, i_=pco2: e.tensor_copy(o[:], i_[:]), "ve")
            p.wait("vector", "ve", m3)
            for q2 in range(4):
                cs = slice(512 * q2, 512 * (q2 + 1))
                p.acquire("tensor", "PE2")
                tv = p.inc("tensor", lambda e, o=PE2[:], l=onesf2[0:1, 0:128], r=erj[0:1, cs]:
                           e.matmul(o[:], l, r, start=True, stop=True), "te")
                p.wait("vector", "te", tv)
                f1_ = p.inc("vector", lambda e, o=Sf[:, cs], b=PE2[:]: e.tensor_tensor(
                    o, o, b, op=ALU.mult), "ve")
                p.release("PE2", "ve")
                p.wait("vector", "ve", f1_)
                f2a = p.inc("vector", lambda e, o=Lms, a=Lsb[:, cs], s=cmask2d[:, j:j + 1]:
                            e.tensor_scalar(o[:], a, s, None, op0=ALU.mult), "ve")
                p.wait("vector", "ve", f2a)
                f2_ = p.inc("vector", lambda e, o=Sf[:, cs], b=Lms: e.tensor_tensor(
                    o, o, b[:], op=ALU.add), "ve")
                p.wait("vector", "ve", f2_)
            for h in range(16):
                hs = slice(D + 128 * h, D + 128 * (h + 1))
                f3 = p.inc("vector", lambda e, o=Sf[:, hs], s=ecjc[:, h:h + 1]: e.tensor_scalar(
                    o, o, s, None, op0=ALU.mult), "ve")
                p.wait("vector", "ve", f3)
                f4a = p.inc("vector", lambda e, o=Lms[:, 0:128], a=Lsb[:, hs], s=cmask2d[:, j:j + 1]:
                           e.tensor_scalar(o, a, s, None, op0=ALU.mult), "ve")
                p.wait("vector", "ve", f4a)
                f4 = p.inc("vector", lambda e, o=Sf[:, hs], b=Lms[:, 0:128]: e.tensor_tensor(
                    o, o, b, op=ALU.add), "ve")
                p.wait("vector", "ve", f4)
            p.release("Lsb", "ve")
        p.wait("vector", "dcc", agv)
        for q2 in range(4):
            cs = slice(512 * q2, 512 * (q2 + 1))
            cs2 = slice(D + 512 * q2, D + 512 * (q2 + 1))
            p.inc("vector", lambda e, o=Skb[:, cs], i_=Sf[:, cs]: e.tensor_copy(o, i_), "ve")
            p.inc("vector", lambda e, o=Svb[:, cs], i_=Sf[:, cs2]: e.tensor_copy(o, i_), "ve")

    p.phase("l0a_sweep2")
    p.barrier(sems=("te", "ve", "se"))
    with contextlib.ExitStack() as ph:
        S = lambda b, s, d: ph.enter_context(p.nc.sbuf_tensor(p.name(b), s, d))
        qfm2 = [S("qfm", [128, KC * 128], BF16) for _ in range(2)]
        kfm2 = [S("kfm", [128, KC * 128], BF16) for _ in range(2)]
        Bt22 = [S("Bt2", [128, D], BF16) for _ in range(2)]
        Bh22 = [S("Bh2", [128, D], BF16) for _ in range(2)]
        Kp22 = [S("Kp2", [128, D], BF16) for _ in range(2)]
        Vp22 = [S("Vp2", [128, D], BF16) for _ in range(2)]
        Dt22 = [S("Dt2", [128, D], BF16) for _ in range(2)]
        gs22 = [S("gs2", [128, D], BF16) for _ in range(2)]
        ecr22 = [S("ecr2", [128, D], F32) for _ in range(2)]
        ecc22 = [S("ecc2", [128, 2 * 16], F32) for _ in range(2)]
        amall = S("amall", [128, 1024], BF16)
        cmall = S("cmall", [128, 1024], BF16)
        amball = S("amball", [128, D], BF16)
        atfall = S("atfall", [128, 1024], BF16)
        btfall = S("btfall", [128, 1024], BF16)
        oraw = S("oraw", [128, D], F32)
        ssb = S("ssb", [128, D], F32)
        esb = S("esb", [128, D], F32)
        nmx = S("nmx", [128, 16], F32)
        rsm = S("rsm", [128, 16], F32)
        orr = S("orr", [128, 16], F32)
        ebst = S("ebst", [128, D], BF16)
        otm = S("otm", [128, D], BF16)
        pX = [ph.enter_context(p.nc.psum_tensor(p.name("pX"), [128, 128], F32)) for _ in range(2)]
        pS = ph.enter_context(p.nc.psum_tensor(p.name("pS"), [128, D], F32))
        ptr = [ph.enter_context(p.nc.psum_tensor(p.name("ptr"), [128, 64], BF16)) for _ in range(2)]

        def s2_loads(pr):
            b = pr % 2
            tp0 = 128 * pr
            sem = "dta" if b == 0 else "dtb"
            ssem = "dsa" if b == 0 else "dsb"
            gsem = "dga" if b == 0 else "dgb"
            p.acquire("sync", f"s2buf{b}")
            for kc in range(KC):
                p.dma("sync", sem, qfm2[b][:, 128 * kc:128 * (kc + 1)], qd[kc, :, tp0:tp0 + 128])
                p.dma("sync", sem, kfm2[b][:, 128 * kc:128 * (kc + 1)], kd[kc, :, tp0:tp0 + 128])
            p.acquire("scalar", f"s2buf{b}")
            p.dma("scalar", ssem, Bt22[b][:], Btd[tp0:tp0 + 128, :])
            p.dma("scalar", ssem, Bh22[b][:], Bhd[tp0:tp0 + 128, :])
            p.dma("scalar", ssem, Kp22[b][:], Kpd[tp0:tp0 + 128, :])
            p.acquire("gpsimd", f"s2buf{b}")
            p.dma("gpsimd", gsem, Vp22[b][:], Vpd[tp0:tp0 + 128, :])
            p.dma("gpsimd", gsem, Dt22[b][:], Dtd[tp0:tp0 + 128, :])
            p.dma("gpsimd", gsem, gs22[b][:], gsd[tp0:tp0 + 128, :])
            p.dma("gpsimd", gsem, ecr22[b][0:1, :], ecd[2 * pr:2 * pr + 1, :])
            p.dma("gpsimd", gsem, ecr22[b][64:65, :], ecd[2 * pr + 1:2 * pr + 2, :])
            p.dma("gpsimd", gsem, ecc22[b][:, 0:16], ecc[2 * pr, :, :])
            p.dma("gpsimd", gsem, ecc22[b][:, 16:32], ecc[2 * pr + 1, :, :])
            return p.cnt[sem], p.cnt[ssem], p.cnt[gsem]

        pend2 = s2_loads(0)
        for pr in range(NT):
            tp0 = 128 * pr
            b = pr % 2
            qfm, kfm = qfm2[b], kfm2[b]
            Bt2, Bh2, Kp2, Vp2 = Bt22[b], Bh22[b], Kp22[b], Vp22[b]
            Dt2, gs2, ecr2, ecc2 = Dt22[b], gs22[b], ecr22[b], ecc22[b]
            lv, sv_, gv_ = pend2
            if pr + 1 < NT:
                pend2 = s2_loads(pr + 1)
            sem = "dta" if b == 0 else "dtb"
            ssem = "dsa" if b == 0 else "dsb"
            gsem = "dga" if b == 0 else "dgb"
            for eng in ("tensor", "vector", "scalar"):
                p.wait(eng, sem, lv)
                p.wait(eng, ssem, sv_)
                p.wait(eng, gsem, gv_)
            p.acquire("vector", "otm_sp")
            for hc in range(2):
                ci = 2 * pr + hc
                rs_ = slice(64 * hc, 64 * hc + 64)
                h64 = 64 * hc
                tcol = slice(64 * hc, 64 * hc + 64)
                p.acquire("tensor", "pS_full")
                p.acquire("vector", "Am_all")
                for h in range(16):
                    bx = h % 2
                    p.acquire("tensor", f"pX{bx}")
                    t1 = p.inc("tensor", lambda e, o=pX[bx][rs_, 0:64], l=kfm[:, 128 * h:128 * h + 128][:, tcol],
                               r=qfm[:, 128 * h:128 * h + 128][:, tcol]: e.matmul(
                                   o, l, r, start=True, stop=True), "te")
                    p.wait("vector", "te", t1)
                    p.inc("vector", lambda e, o=amall[rs_, 64 * h:64 * (h + 1)], i_=pX[bx][rs_, 0:64],
                          s=betat[pr][rs_, h:h + 1], m=masksu[rs_, :]: e.scalar_tensor_tensor(
                              o, i_, s, m, op0=ALU.mult, op1=ALU.mult), "ve")
                    p.release(f"pX{bx}", "ve")
                amv = p.cnt["ve"]
                p.wait("tensor", "ve", amv)
                for h in range(16):
                    hs = slice(128 * h, 128 * (h + 1))
                    p.inc("tensor", lambda e, o=pS[rs_, hs], l=qfm[:, 128 * h:128 * h + 128][:, tcol],
                          r=Skb[:, hs]: e.matmul(o, l, r, start=True, stop=False), "te")
                    p.inc("tensor", lambda e, o=pS[rs_, hs], l=amall[rs_, 64 * h:64 * (h + 1)],
                          r=Bt2[rs_, hs]: e.matmul(o, l, r, start=False, stop=True), "te")
                tS = p.cnt["te"]
                p.rel["Am_all"] = [("te", tS)]
                p.wait("vector", "te", tS)
                sv1 = p.inc("vector", lambda e, o=ssb[rs_, :], i_=pS[rs_, :], d=Dt2[rs_, :]:
                            e.scalar_tensor_tensor(o, i_, SCL, d, op0=ALU.mult, op1=ALU.mult), "ve")
                p.release("pS_full", "ve")
                p.wait("vector", "ve", sv1)
                sv2 = p.inc("vector", lambda e, o=nmx[rs_, :], i_=ssb[rs_, :]: e.tensor_reduce(
                    o, i_.rearrange("p (h j) -> p h j", h=16), axis=AX.X, op=ALU.max, negate=True), "ve")
                p.wait("vector", "ve", sv2)
                for h in range(16):
                    hs = slice(128 * h, 128 * (h + 1))
                    p.inc("vector", lambda e, o=esb[rs_, hs], i_=ssb[rs_, hs], s=nmx[rs_, h:h + 1]:
                          e.tensor_scalar(o, i_, s, None, op0=ALU.add), "ve")
                sv3 = p.cnt["ve"]
                p.wait("scalar", "ve", sv3)
                sv4 = p.inc("scalar", lambda e, o=esb[rs_, :]: e.activation(o, o, AF.Exp), "se")
                p.wait("vector", "se", sv4)
                sv5 = p.inc("vector", lambda e, o=rsm[rs_, :], i_=esb[rs_, :]: e.tensor_reduce(
                    o, i_.rearrange("p (h j) -> p h j", h=16), axis=AX.X, op=ALU.add), "ve")
                p.wait("vector", "ve", sv5)
                sv6 = p.inc("vector", lambda e, o=rsm[rs_, :]: e.reciprocal(o, o), "ve")
                p.wait("vector", "ve", sv6)
                p.acquire("vector", "amball")
                for h in range(16):
                    hs = slice(128 * h, 128 * (h + 1))
                    p.inc("vector", lambda e, o=amball[rs_, hs], i_=esb[rs_, hs],
                          s=rsm[rs_, h:h + 1], d=Dt2[rs_, hs]: e.scalar_tensor_tensor(
                              o, i_, s, d, op0=ALU.mult, op1=ALU.mult), "ve")
                c1v = p.cnt["ve"]
                p.wait("tensor", "ve", c1v)
                for h in range(16):
                    hs = slice(128 * h, 128 * (h + 1))
                    bx = h % 2
                    p.acquire("tensor", f"ptr{bx}")
                    tva = p.inc("tensor", lambda e, o=ptr[bx][:, 0:64], i_=amball[rs_, hs],
                                id_=p.ident[rs_, rs_]: e.transpose(o, i_, id_), "te")
                    p.wait("vector", "te", tva)
                    p.inc("vector", lambda e, o=atfall[:, 64 * h:64 * (h + 1)], i_=ptr[bx][:, 0:64]:
                          e.tensor_copy(o, i_), "ve")
                    p.release(f"ptr{bx}", "ve")
                p.rel["amball"] = [("te", p.cnt["te"])]
                for h in range(16):
                    hs = slice(128 * h, 128 * (h + 1))
                    bx = h % 2
                    p.acquire("tensor", f"ptr{bx}")
                    tvb = p.inc("tensor", lambda e, o=ptr[bx][:, 0:64], i_=Bt2[rs_, hs],
                                id_=p.ident[rs_, rs_]: e.transpose(o, i_, id_), "te")
                    p.wait("vector", "te", tvb)
                    p.inc("vector", lambda e, o=btfall[:, 64 * h:64 * (h + 1)], i_=ptr[bx][:, 0:64]:
                          e.tensor_copy(o, i_), "ve")
                    p.release(f"ptr{bx}", "ve")
                tfv = p.cnt["ve"]
                p.wait("tensor", "ve", tfv)
                p.acquire("vector", "cmall")
                for h in range(16):
                    bx = h % 2
                    p.acquire("tensor", f"pX{bx}")
                    tv = p.inc("tensor", lambda e, o=pX[bx][rs_, 0:64], l=btfall[:, 64 * h:64 * (h + 1)],
                               r=atfall[:, 64 * h:64 * (h + 1)]: e.matmul(o, l, r, start=True, stop=True), "te")
                    p.wait("vector", "te", tv)
                    p.inc("vector", lambda e, o=cmall[rs_, 64 * h:64 * (h + 1)], i_=pX[bx][rs_, 0:64],
                          m=masksu[rs_, :]: e.tensor_tensor(o, i_, m, op=ALU.mult), "ve")
                    p.release(f"pX{bx}", "ve")
                cmv = p.cnt["ve"]
                p.wait("tensor", "ve", cmv)
                p.acquire("vector", "orawall")
                for h in range(16):
                    hs = slice(128 * h, 128 * (h + 1))
                    bx = h % 2
                    p.acquire("tensor", f"pX{bx}")
                    p.inc("tensor", lambda e, o=pX[bx][rs_, :], l=atfall[:, 64 * h:64 * (h + 1)],
                          r=Svb[:, hs]: e.matmul(o, l, r, start=True, stop=False), "te")
                    to = p.inc("tensor", lambda e, o=pX[bx][rs_, :], l=cmall[rs_, 64 * h:64 * (h + 1)],
                               r=Vp2[rs_, hs]: e.matmul(o, l, r, start=False, stop=True), "te")
                    p.wait("vector", "te", to)
                    p.inc("vector", lambda e, o=oraw[rs_, hs], i_=pX[bx][rs_, :]: e.tensor_copy(
                        o, i_), "ve")
                    p.release(f"pX{bx}", "ve")
                orv = p.cnt["ve"]
                p.rel["cmall"] = [("te", p.cnt["te"])]
                p.rel["atfall"] = [("te", p.cnt["te"])]
                p.rel["btfall"] = [("te", p.cnt["te"])]
                p.wait("scalar", "ve", orv)
                sq = p.inc("scalar", lambda e, o=ssb[rs_, :], i_=oraw[rs_, :]: e.activation(
                    o, i_, AF.Square), "se")
                p.wait("vector", "se", sq)
                rv_ = p.inc("vector", lambda e, o=orr[rs_, :], i_=ssb[rs_, :]: e.tensor_reduce(
                    o, i_.rearrange("p (h j) -> p h j", h=16), axis=AX.X, op=ALU.add), "ve")
                p.wait("scalar", "ve", rv_)
                o3 = p.inc("scalar", lambda e, o=orr[rs_, :], c=p.cv[rs_, C_LNEPS:C_LNEPS + 1]:
                    e.activation(o, o, AF.Sqrt, scale=1.0 / 128.0, bias=c), "se")
                p.wait("vector", "se", o3)
                o4 = p.inc("vector", lambda e, o=orr[rs_, :]: e.reciprocal(o, o), "ve")
                p.wait("vector", "ve", o4)
                for h in range(16):
                    hs = slice(128 * h, 128 * (h + 1))
                    p.inc("vector", lambda e, o=otm[rs_, hs], i_=oraw[rs_, hs], s=orr[rs_, h:h + 1],
                          g=gs2[rs_, hs]: e.scalar_tensor_tensor(
                              o, i_, s, g, op0=ALU.mult, op1=ALU.mult), "ve")
                p.rel["orawall"] = [("ve", p.cnt["ve"])]
                p.acquire("tensor", "pS_full")
                p.acquire("vector", "ebst")
                for q2 in range(4):
                    cs = slice(512 * q2, 512 * (q2 + 1))
                    tv = p.inc("tensor", lambda e, o=pS[:, cs],
                               l=onesf2[h64:h64 + 1, 0:128], r=ecr2[h64:h64 + 1, cs]:
                               e.matmul(o, l, r, start=True, stop=True), "te")
                    p.wait("vector", "te", tv)
                    p.inc("vector", lambda e, o=ebst[:, cs], i_=pS[:, cs]: e.tensor_copy(o, i_), "ve")
                ebv = p.cnt["ve"]
                p.release("pS_full", "ve")
                p.wait("vector", "ve", ebv)
                for h in range(16):
                    hs = slice(128 * h, 128 * (h + 1))
                    p.inc("vector", lambda e, o=Skb[:, hs], b=ebst[:, hs]: e.tensor_tensor(
                        o, b, o, op=ALU.mult), "ve")
                dkv = p.cnt["ve"]
                p.rel["ebst"] = [("ve", dkv)]
                p.wait("tensor", "ve", dkv)
                for h in range(16):
                    hs = slice(128 * h, 128 * (h + 1))
                    bx = h % 2
                    p.acquire("tensor", f"pX{bx}")
                    tk = p.inc("tensor", lambda e, o=pX[bx][:, :], l=Kp2[rs_, hs], r=Bh2[rs_, hs]:
                               e.matmul(o, l, r, start=True, stop=True), "te")
                    p.wait("vector", "te", tk)
                    p.inc("vector", lambda e, o=Skb[:, hs], b=pX[bx][:, :]: e.tensor_tensor(
                        o, b, o, op=ALU.add), "ve")
                    p.release(f"pX{bx}", "ve")
                skv = p.cnt["ve"]
                p.wait("tensor", "ve", skv)
                for h in range(16):
                    hs = slice(128 * h, 128 * (h + 1))
                    bx = h % 2
                    p.acquire("tensor", f"pX{bx}")
                    tv2 = p.inc("tensor", lambda e, o=pX[bx][:, :], l=Bh2[rs_, hs], r=Vp2[rs_, hs]:
                                e.matmul(o, l, r, start=True, stop=True), "te")
                    p.wait("vector", "te", tv2)
                    p.inc("vector", lambda e, o=Svb[:, hs], s=ecc2[:, 16 * hc + h:16 * hc + h + 1],
                          b=pX[bx][:, :]: e.scalar_tensor_tensor(
                              o, o, s, b, op0=ALU.mult, op1=ALU.add), "ve")
                    p.release(f"pX{bx}", "ve")
                p.wait("vector", "ve", p.cnt["ve"])
            ov = p.cnt["ve"]
            p.wait("scalar", "ve", ov)
            ws = "dwa" if pr % 2 == 0 else "dwb"
            p.dma("scalar", ws, otd[tp0:tp0 + 128, :], otm[:])
            p.release("otm_sp", ws)
            p.release(f"s2buf{b}", "te")
            p.release_add(f"s2buf{b}", "ve")
            p.release_add(f"s2buf{b}", "se")

    p.phase("l0a_wo")
    p.barrier(sems=("te", "ve", "se", "dwa", "dwb", "dts", "dgt"))
    dfm_d = p.dram("l0dfm", [KC * 128, R], BF16)
    with contextlib.ExitStack() as ph:
        S = lambda b, s, d: ph.enter_context(p.nc.sbuf_tensor(p.name(b), s, d))
        ofm = [S("ofm", [128, 512], BF16) for _ in range(KC)]
        wcol = [S("wcolo", [128, KC, 128], BF16) for _ in range(4)]
        dd = [S("l0dd", [128, 512], BF16) for _ in range(2)]
        pw = [ph.enter_context(p.nc.psum_tensor(p.name("pwo"), [128, 512], F32)) for _ in range(2)]
        for q in range(4):
            t0 = 512 * q
            p.wait("sync", "dwa", p.cnt["dwa"])
            p.wait("sync", "dwb", p.cnt["dwb"])
            p.wait("sync", "te", p.cnt["te"])
            for kc in range(KC):
                p.dma("sync", "dtot", ofm[kc][:], otd[t0:t0 + 512, 128 * kc:128 * (kc + 1)],
                      transpose=True)
            hv = p.cnt["dtot"]
            p.wait("tensor", "dtot", hv)
            WS4o = ["dc", "dd", "dc2", "dd2"]

            def wo_load(oc):
                bw = oc % 4
                p.acquire("gpsimd", f"wco{bw}")
                return p.dma("gpsimd", WS4o[bw], wcol[bw][:],
                             Wo[:, 128 * oc:128 * (oc + 1)].rearrange("(kc p) j -> p kc j", p=128))

            pqo_ = [wo_load(oc) for oc in range(3)]
            for oc in range(KC):
                bw = oc % 4
                wv_ = pqo_[oc]
                if oc + 3 < KC:
                    pqo_.append(wo_load(oc + 3))
                p.wait("tensor", WS4o[bw], wv_)
                b2 = oc % 2
                p.acquire("tensor", f"pwo{b2}")
                tv = p.mm_group(pw[b2][:], [(wcol[bw][:, kc, :], ofm[kc][:]) for kc in range(KC)])
                p.release(f"wco{bw}", "te")
                p.wait("scalar", "te", tv)
                p.acquire("scalar", f"l0dd{b2}")
                sv = p.inc("scalar", lambda e, o=dd[b2], i_=pw[b2]: e.activation(
                    o[:], i_[:], AF.Copy), "se")
                p.release(f"pwo{b2}", "se")
                p.wait("scalar", "se", sv)
                wsd = "dwa" if b2 == 0 else "dwb"
                p.dma("scalar", wsd, dfm_d[128 * oc:128 * (oc + 1), t0:t0 + 512], dd[b2][:])
                p.release(f"l0dd{b2}", wsd)

    rph = contextlib.ExitStack()
    xr = [rph.enter_context(p.nc.sbuf_tensor(p.name("xr0"), [128, D], F32)) for _ in range(2)]
    xo = [rph.enter_context(p.nc.sbuf_tensor(p.name("xo0"), [128, D], F32)) for _ in range(2)]
    dtt = [rph.enter_context(p.nc.sbuf_tensor(p.name("dtt0"), [128, D], BF16)) for _ in range(2)]
    p.wait("sync", "dwa", p.cnt["dwa"])
    p.wait("sync", "dwb", p.cnt["dwb"])
    for tt in range(NT):
        b = tt % 2
        xs = "da" if b == 0 else "db"
        ds_ = "dta" if b == 0 else "dtb"
        ws = "dwa" if b == 0 else "dwb"
        p.acquire("sync", f"r0x{b}")
        lv = p.dma("sync", xs, xr[b][:], xin[128 * tt:128 * (tt + 1), :])
        p.dma("sync", ds_, dtt[b][:], dfm_d[:, 128 * tt:128 * (tt + 1)], transpose=True)
        dv = p.cnt[ds_]
        p.wait("vector", xs, lv)
        p.wait("vector", ds_, dv)
        p.acquire("vector", f"r0o{b}")
        p.inc("vector", lambda e, o=xo[b], a=xr[b], dl=dtt[b]: e.tensor_tensor(
            o[:], a[:], dl[:], op=ALU.add), "ve")
        p.release(f"r0x{b}", "ve")
        p.wait("scalar", "ve", p.cnt["ve"])
        p.dma("scalar", ws, xout[128 * tt:128 * (tt + 1), :], xo[b][:])
        p.release(f"r0o{b}", ws)
    rph.close()




def emit_ffn(p, *a, **kw):
    with p.block_scope():
        _emit_ffn_impl(p, *a, **kw)


def emit_l1_attn(p, *a, **kw):
    with p.block_scope():
        _emit_l1_attn_impl(p, *a, **kw)


def emit_l0_attn(p, *a, **kw):
    with p.block_scope():
        _emit_l0_attn_impl(p, *a, **kw)


def finish(p):
    for s in list(p.cnt):
        if p.cnt[s] > 0:
            p.wait("sync", s, p.cnt[s])



def host_consts():
    return np.eye(128, dtype=ml_dtypes.bfloat16)


def host_ones():
    return np.ones((128, 128), dtype=ml_dtypes.bfloat16)


def host_cvec():
    cv = np.zeros((128, NCONST), np.float32)
    cv[:, C_LNEPS] = 1e-5
    cv[:, C_L2EPS] = 1e-6
    cv[:, C_ONE] = 1.0
    cv[:, C_SCALE] = 128.0 ** -0.5
    return cv


def host_pswap():
    P = np.zeros((128, 128), np.float32)
    for i in range(64):
        P[2 * i + 1, 2 * i] = -1.0
        P[2 * i, 2 * i + 1] = 1.0
    return P.astype(ml_dtypes.bfloat16)


def host_rope_tables(core_id):
    rot = 1024
    inv = 1.0 / (10000.0 ** (np.arange(0, rot, 2, dtype=np.float64) / rot))
    pos = np.arange(core_id * R, (core_id + 1) * R, dtype=np.float64)
    fr = pos[:, None] * inv[None, :]
    fr = np.repeat(fr, 2, axis=-1)
    cos = np.cos(fr).T.astype(ml_dtypes.bfloat16)
    sin = np.sin(fr).T.astype(ml_dtypes.bfloat16)
    return np.ascontiguousarray(cos), np.ascontiguousarray(sin)


def host_onesf():
    return np.ones((1, 128), np.float32)


def bcast_row(v):
    return np.ascontiguousarray(np.broadcast_to(np.asarray(v, np.float32), (128, v.shape[0])))




def host_trix():
    t = np.zeros((128, 64), np.float32)
    for s in range(64):
        t[s, s:] = 1.0
        t[64 + s, s:] = 1.0
    return t


def host_masksu():
    return host_trix().astype(ml_dtypes.bfloat16)


def host_onesf2():
    return np.ones((128, 128), np.float32)


def host_conv2d(c):
    return np.ascontiguousarray(c.reshape(KC, 128, 4).transpose(1, 0, 2).reshape(128, KC * 4))


def host_cmask(core_id):
    m = np.zeros((128, 8), np.float32)
    m[:, :core_id] = 1.0
    return m


def scale_2d(s):
    c = s.shape[0] // 128
    return np.ascontiguousarray(s.reshape(c, 128).T)



_GROUPS = [list(range(NCORES))]


def _ein(nc, n, sh, dt=F32):
    return nc.dram_tensor(n, sh, dt, kind="ExternalInput").ap()


def _build_l0attn_prog():
    p = P()
    nc = p.nc
    xin = _ein(nc, "xin", [R, D])
    xhalo = _ein(nc, "xhalo", [3, D])
    Wq, Wk, Wv, Ww = (_ein(nc, n, [D, D]) for n in ("Wq", "Wk", "Wv", "Ww"))
    Wf1, Wf2 = _ein(nc, "Wf1", [D, 128]), _ein(nc, "Wf2", [128, D])
    Wb = _ein(nc, "Wb", [D, 16])
    Wg1, Wg2 = _ein(nc, "Wg1", [D, 128]), _ein(nc, "Wg2", [128, D])
    Wo = _ein(nc, "Wo", [D, D])
    cqd, ckd, cvd = (_ein(nc, n, [128, 64]) for n in ("cqd", "ckd", "cvd"))
    bg2d, nwd = _ein(nc, "bg2d", [128, D]), _ein(nc, "nwd", [128, D])
    trixd = _ein(nc, "trixd", [128, 64])
    maskd = _ein(nc, "maskd", [128, 64], BF16)
    cmd = _ein(nc, "cmd", [128, 8])
    onf2 = _ein(nc, "onf2", [128, 128])
    idc = _ein(nc, "idc", [128, 128], BF16)
    onc = _ein(nc, "onc", [128, 128], BF16)
    cvec = _ein(nc, "cvec", [128, NCONST])
    xout = nc.dram_tensor("xout", [R, D], F32, kind="ExternalOutput").ap()
    emit_setup(p, cvec, idc, onc)
    cq2 = p.const2sb(cqd, [128, 64], F32, "cq2")
    ck2 = p.const2sb(ckd, [128, 64], F32, "ck2")
    cv2 = p.const2sb(cvd, [128, 64], F32, "cv2")
    bg2b = p.const2sb(bg2d, [128, D], F32, "bg2b")
    nwb = p.const2sb(nwd, [128, D], F32, "nwb")
    trix = p.const2sb(trixd, [128, 64], F32, "trix")
    masksu = p.const2sb(maskd, [128, 64], BF16, "masksu")
    cmask = p.const2sb(cmd, [128, 8], F32, "cmask")
    onesf2 = p.const2sb(onf2, [128, 128], F32, "onesf2")
    emit_l0_attn(p, xin, xhalo, xout, Wq, Wk, Wv, Ww, Wf1, Wf2, Wb, Wg1, Wg2, Wo,
                 cq2, ck2, cv2, bg2b, nwb, trix, masksu, cmask, onesf2, _GROUPS)
    finish(p)
    return p.build()


def _build_ffn_prog(int_weights):
    p = P()
    nc = p.nc
    wdt = I32 if int_weights else F32
    xin = _ein(nc, "xin", [R, D])
    gwT = _ein(nc, "gwT", [D, FFN_I], wdt)
    uwT = _ein(nc, "uwT", [D, FFN_I], wdt)
    dwT = _ein(nc, "dwT", [FFN_I, D], wdt)
    gs = _ein(nc, "gs", [128, NIC])
    us = _ein(nc, "us", [128, NIC])
    ds = _ein(nc, "ds", [128, KC])
    idc = _ein(nc, "idc", [128, 128], BF16)
    onc = _ein(nc, "onc", [128, 128], BF16)
    cvec = _ein(nc, "cvec", [128, NCONST])
    xout = nc.dram_tensor("xout", [R, D], F32, kind="ExternalOutput").ap()
    emit_setup(p, cvec, idc, onc)
    gsc = p.const2sb(gs, [128, NIC], F32, "gsc")
    usc = p.const2sb(us, [128, NIC], F32, "usc")
    dsc = p.const2sb(ds, [128, KC], F32, "dsc")
    emit_ffn(p, xin, xout, gwT, uwT, dwT, gsc, usc, dsc, zero_point=int_weights)
    finish(p)
    return p.build()


def _build_l1attn_prog():
    p = P()
    nc = p.nc
    xin = _ein(nc, "xin", [R, D])
    Win = _ein(nc, "Win", [D, 3 * D])
    Wout = _ein(nc, "Wout", [D, D])
    binc_d = _ein(nc, "binc", [128, 48])
    boutc_d = _ein(nc, "boutc", [128, KC])
    lnwb_d = _ein(nc, "lnwb", [128, D])
    lnbb_d = _ein(nc, "lnbb", [128, D])
    cost_d = _ein(nc, "cost", [1024, R], BF16)
    sint_d = _ein(nc, "sint", [1024, R], BF16)
    idc = _ein(nc, "idc", [128, 128], BF16)
    onc = _ein(nc, "onc", [128, 128], BF16)
    psw = _ein(nc, "psw", [128, 128], BF16)
    onf = _ein(nc, "onf", [1, 128])
    cvec = _ein(nc, "cvec", [128, NCONST])
    xout = nc.dram_tensor("xout", [R, D], F32, kind="ExternalOutput").ap()
    emit_setup(p, cvec, idc, onc, psw, onf)
    binc = p.const2sb(binc_d, [128, 48], F32, "binc")
    boutc = p.const2sb(boutc_d, [128, KC], F32, "boutc")
    lnwb = p.const2sb(lnwb_d, [128, D], F32, "lnwb")
    lnbb = p.const2sb(lnbb_d, [128, D], F32, "lnbb")
    emit_l1_attn(p, xin, xout, Win, Wout, binc, boutc, lnwb, lnbb,
                 cost_d, sint_d, p.onesf, _GROUPS)
    finish(p)
    return p.build()


def _build_fused_prog():
    p = P()
    nc = p.nc
    xin = _ein(nc, "xin", [R, D])
    xhalo = _ein(nc, "xhalo", [3, D])
    Wq, Wk, Wv, Ww = (_ein(nc, n, [D, D]) for n in ("Wq", "Wk", "Wv", "Ww"))
    Wf1, Wf2 = _ein(nc, "Wf1", [D, 128]), _ein(nc, "Wf2", [128, D])
    Wb = _ein(nc, "Wb", [D, 16])
    Wg1, Wg2 = _ein(nc, "Wg1", [D, 128]), _ein(nc, "Wg2", [128, D])
    Wo = _ein(nc, "Wo", [D, D])
    cqd, ckd, cvd = (_ein(nc, n, [128, 64]) for n in ("cqd", "ckd", "cvd"))
    bg2d, nwd = _ein(nc, "bg2d", [128, D]), _ein(nc, "nwd", [128, D])
    trixd = _ein(nc, "trixd", [128, 64])
    maskd = _ein(nc, "maskd", [128, 64], BF16)
    cmd = _ein(nc, "cmd", [128, 8])
    onf2 = _ein(nc, "onf2", [128, 128])
    f0g = _ein(nc, "f0g", [D, FFN_I])
    f0u = _ein(nc, "f0u", [D, FFN_I])
    f0d = _ein(nc, "f0d", [FFN_I, D])
    f0gs, f0us = _ein(nc, "f0gs", [128, NIC]), _ein(nc, "f0us", [128, NIC])
    f0ds = _ein(nc, "f0ds", [128, KC])
    Win = _ein(nc, "Win", [D, 3 * D])
    Wout = _ein(nc, "Wout", [D, D])
    binc_d = _ein(nc, "binc", [128, 48])
    boutc_d = _ein(nc, "boutc", [128, KC])
    lnwb_d, lnbb_d = _ein(nc, "lnwb", [128, D]), _ein(nc, "lnbb", [128, D])
    cost_d = _ein(nc, "cost", [1024, R], BF16)
    sint_d = _ein(nc, "sint", [1024, R], BF16)
    psw = _ein(nc, "psw", [128, 128], BF16)
    onf = _ein(nc, "onf", [1, 128])
    f1g = _ein(nc, "f1g", [D, FFN_I], I32)
    f1u = _ein(nc, "f1u", [D, FFN_I], I32)
    f1d = _ein(nc, "f1d", [FFN_I, D], I32)
    f1gs, f1us = _ein(nc, "f1gs", [128, NIC]), _ein(nc, "f1us", [128, NIC])
    f1ds = _ein(nc, "f1ds", [128, KC])
    idc = _ein(nc, "idc", [128, 128], BF16)
    onc = _ein(nc, "onc", [128, 128], BF16)
    cvec = _ein(nc, "cvec", [128, NCONST])
    xout = nc.dram_tensor("xout", [R, D], F32, kind="ExternalOutput").ap()
    x1 = p.dram("x1", [R, D], F32)
    x2 = p.dram("x2", [R, D], F32)
    x3 = p.dram("x3", [R, D], F32)

    emit_setup(p, cvec, idc, onc, psw, onf)

    with p.block_scope():
        cq2 = p.const2sb_blk(cqd, [128, 64], F32, "cq2", "dc1")
        ck2 = p.const2sb_blk(ckd, [128, 64], F32, "ck2", "dc1")
        cv2 = p.const2sb_blk(cvd, [128, 64], F32, "cv2", "dc1")
        bg2b = p.const2sb_blk(bg2d, [128, D], F32, "bg2b", "dc1")
        nwb = p.const2sb_blk(nwd, [128, D], F32, "nwb", "dc1")
        trix = p.const2sb_blk(trixd, [128, 64], F32, "trix", "dc1")
        masksu = p.const2sb_blk(maskd, [128, 64], BF16, "masksu", "dc1")
        cmask = p.const2sb_blk(cmd, [128, 8], F32, "cmask", "dc1")
        onesf2 = p.const2sb_blk(onf2, [128, 128], F32, "onesf2", "dc1")
        p.wait_blk_consts("dc1")
        _emit_l0_attn_impl(p, xin, xhalo, x1, Wq, Wk, Wv, Ww, Wf1, Wf2, Wb, Wg1, Wg2,
                           Wo, cq2, ck2, cv2, bg2b, nwb, trix, masksu, cmask, onesf2,
                           _GROUPS)
    with p.block_scope():
        f0gsc = p.const2sb_blk(f0gs, [128, NIC], F32, "f0gsc", "dc2")
        f0usc = p.const2sb_blk(f0us, [128, NIC], F32, "f0usc", "dc2")
        f0dsc = p.const2sb_blk(f0ds, [128, KC], F32, "f0dsc", "dc2")
        p.wait_blk_consts("dc2")
        _emit_ffn_impl(p, x1, x2, f0g, f0u, f0d, f0gsc, f0usc, f0dsc)
    with p.block_scope():
        binc = p.const2sb_blk(binc_d, [128, 48], F32, "binc", "dc3")
        boutc = p.const2sb_blk(boutc_d, [128, KC], F32, "boutc", "dc3")
        lnwb = p.const2sb_blk(lnwb_d, [128, D], F32, "lnwb", "dc3")
        lnbb = p.const2sb_blk(lnbb_d, [128, D], F32, "lnbb", "dc3")
        p.wait_blk_consts("dc3")
        _emit_l1_attn_impl(p, x2, x3, Win, Wout, binc, boutc, lnwb, lnbb,
                           cost_d, sint_d, p.onesf, _GROUPS)
    with p.block_scope():
        f1gsc = p.const2sb_blk(f1gs, [128, NIC], F32, "f1gsc", "dc4")
        f1usc = p.const2sb_blk(f1us, [128, NIC], F32, "f1usc", "dc4")
        f1dsc = p.const2sb_blk(f1ds, [128, KC], F32, "f1dsc", "dc4")
        p.wait_blk_consts("dc4")
        _emit_ffn_impl(p, x3, xout, f1g, f1u, f1d, f1gsc, f1usc, f1dsc, zero_point=True)
    finish(p)
    return p.build()


def _fused_maps(inputs, x):
    common = dict(idc=host_consts(), onc=host_ones(), cvec=host_cvec())
    g0 = np.ascontiguousarray(np.asarray(inputs["L0_ffn_gate_w"]).T)
    u0 = np.ascontiguousarray(np.asarray(inputs["L0_ffn_up_w"]).T)
    d0 = np.ascontiguousarray(np.asarray(inputs["L0_ffn_down_w"]).T)
    g1 = np.ascontiguousarray(np.asarray(inputs["L1_ffn_wq_gate"]).T)
    u1 = np.ascontiguousarray(np.asarray(inputs["L1_ffn_wq_up"]).T)
    d1 = np.ascontiguousarray(np.asarray(inputs["L1_ffn_wq_down"]).T)
    maps = []
    for c in range(NCORES):
        xh = x[c * R - 3:c * R] if c > 0 else np.zeros((3, D), np.float32)
        cos, sin = host_rope_tables(c)
        maps.append(dict(
            xin=x[c * R:(c + 1) * R], xhalo=np.ascontiguousarray(xh),
            Wq=inputs["L0_attn_Wq"], Wk=inputs["L0_attn_Wk"], Wv=inputs["L0_attn_Wv"],
            Ww=inputs["L0_attn_Ww"], Wf1=inputs["L0_attn_Wf1"], Wf2=inputs["L0_attn_Wf2"],
            Wb=inputs["L0_attn_Wb"], Wg1=inputs["L0_attn_Wg1"], Wg2=inputs["L0_attn_Wg2"],
            Wo=inputs["L0_attn_Wo"],
            cqd=host_conv2d(np.asarray(inputs["L0_attn_cq"])),
            ckd=host_conv2d(np.asarray(inputs["L0_attn_ck"])),
            cvd=host_conv2d(np.asarray(inputs["L0_attn_cv"])),
            bg2d=bcast_row(np.asarray(inputs["L0_attn_bg2"])),
            nwd=bcast_row(np.tile(np.asarray(inputs["L0_attn_norm_w"]), 16)),
            trixd=host_trix(), maskd=host_masksu(), cmd=host_cmask(c),
            onf2=host_onesf2(),
            f0g=g0, f0u=u0, f0d=d0,
            f0gs=scale_2d(np.asarray(inputs["L0_ffn_gate_s"])),
            f0us=scale_2d(np.asarray(inputs["L0_ffn_up_s"])),
            f0ds=scale_2d(np.asarray(inputs["L0_ffn_down_s"])),
            Win=inputs["L1_attn_W_in"], Wout=inputs["L1_attn_W_out"],
            binc=scale_2d(np.asarray(inputs["L1_attn_b_in"])),
            boutc=scale_2d(np.asarray(inputs["L1_attn_b_out"])),
            lnwb=bcast_row(np.asarray(inputs["L1_attn_ln_w"])),
            lnbb=bcast_row(np.asarray(inputs["L1_attn_ln_b"])),
            cost=cos, sint=sin, psw=host_pswap(), onf=host_onesf(),
            f1g=g1, f1u=u1, f1d=d1,
            f1gs=scale_2d(np.asarray(inputs["L1_ffn_s_gate"])),
            f1us=scale_2d(np.asarray(inputs["L1_ffn_s_up"])),
            f1ds=scale_2d(np.asarray(inputs["L1_ffn_s_down"])),
            **common))
    return maps


FUSED = True


_PROGS = {}


def _prog(key, builder):
    if key not in _PROGS:
        _PROGS[key] = builder()
    return _PROGS[key]


def kernel(**inputs):
    x = np.ascontiguousarray(np.asarray(inputs["L0_attn_hidden_states"],
                                        np.float32).reshape(NCORES * R, D))
    if FUSED:
        ncf = _prog("fused", _build_fused_prog)
        res = run_bass_kernel_spmd(ncf, _fused_maps(inputs, x), core_ids=list(range(NCORES)))
        x4 = np.concatenate([res.results[c]["xout"] for c in range(NCORES)], axis=0)
        return x4.reshape(1, NCORES * R, D)
    shards = lambda arr: [arr[c * R:(c + 1) * R] for c in range(NCORES)]
    common = dict(idc=host_consts(), onc=host_ones(), cvec=host_cvec())

    nc1 = _prog("l0a", _build_l0attn_prog)
    maps = []
    for c in range(NCORES):
        xh = x[c * R - 3:c * R] if c > 0 else np.zeros((3, D), np.float32)
        maps.append(dict(
            xin=x[c * R:(c + 1) * R], xhalo=np.ascontiguousarray(xh),
            Wq=inputs["L0_attn_Wq"], Wk=inputs["L0_attn_Wk"], Wv=inputs["L0_attn_Wv"],
            Ww=inputs["L0_attn_Ww"], Wf1=inputs["L0_attn_Wf1"], Wf2=inputs["L0_attn_Wf2"],
            Wb=inputs["L0_attn_Wb"], Wg1=inputs["L0_attn_Wg1"], Wg2=inputs["L0_attn_Wg2"],
            Wo=inputs["L0_attn_Wo"],
            cqd=host_conv2d(np.asarray(inputs["L0_attn_cq"])),
            ckd=host_conv2d(np.asarray(inputs["L0_attn_ck"])),
            cvd=host_conv2d(np.asarray(inputs["L0_attn_cv"])),
            bg2d=bcast_row(np.asarray(inputs["L0_attn_bg2"])),
            nwd=bcast_row(np.tile(np.asarray(inputs["L0_attn_norm_w"]), 16)),
            trixd=host_trix(), maskd=host_masksu(), cmd=host_cmask(c),
            onf2=host_onesf2(), **common))
    res = run_bass_kernel_spmd(nc1, maps, core_ids=list(range(NCORES)))
    x1 = np.concatenate([res.results[c]["xout"] for c in range(NCORES)], axis=0)

    nc2 = _prog("l0f", lambda: _build_ffn_prog(False))
    gwT = np.ascontiguousarray(np.asarray(inputs["L0_ffn_gate_w"]).T)
    uwT = np.ascontiguousarray(np.asarray(inputs["L0_ffn_up_w"]).T)
    dwT = np.ascontiguousarray(np.asarray(inputs["L0_ffn_down_w"]).T)
    maps = [dict(xin=s, gwT=gwT, uwT=uwT, dwT=dwT,
                 gs=scale_2d(np.asarray(inputs["L0_ffn_gate_s"])),
                 us=scale_2d(np.asarray(inputs["L0_ffn_up_s"])),
                 ds=scale_2d(np.asarray(inputs["L0_ffn_down_s"])), **common)
            for s in shards(x1)]
    res = run_bass_kernel_spmd(nc2, maps, core_ids=list(range(NCORES)))
    x2 = np.concatenate([res.results[c]["xout"] for c in range(NCORES)], axis=0)

    nc3 = _prog("l1a", _build_l1attn_prog)
    maps = []
    for c, s in enumerate(shards(x2)):
        cos, sin = host_rope_tables(c)
        maps.append(dict(
            xin=s, Win=inputs["L1_attn_W_in"], Wout=inputs["L1_attn_W_out"],
            binc=scale_2d(np.asarray(inputs["L1_attn_b_in"])),
            boutc=scale_2d(np.asarray(inputs["L1_attn_b_out"])),
            lnwb=bcast_row(np.asarray(inputs["L1_attn_ln_w"])),
            lnbb=bcast_row(np.asarray(inputs["L1_attn_ln_b"])),
            cost=cos, sint=sin, psw=host_pswap(), onf=host_onesf(), **common))
    res = run_bass_kernel_spmd(nc3, maps, core_ids=list(range(NCORES)))
    x3 = np.concatenate([res.results[c]["xout"] for c in range(NCORES)], axis=0)

    nc4 = _prog("l1f", lambda: _build_ffn_prog(True))
    gwT = np.ascontiguousarray(np.asarray(inputs["L1_ffn_wq_gate"]).T)
    uwT = np.ascontiguousarray(np.asarray(inputs["L1_ffn_wq_up"]).T)
    dwT = np.ascontiguousarray(np.asarray(inputs["L1_ffn_wq_down"]).T)
    maps = [dict(xin=s, gwT=gwT, uwT=uwT, dwT=dwT,
                 gs=scale_2d(np.asarray(inputs["L1_ffn_s_gate"])),
                 us=scale_2d(np.asarray(inputs["L1_ffn_s_up"])),
                 ds=scale_2d(np.asarray(inputs["L1_ffn_s_down"])), **common)
            for s in shards(x3)]
    res = run_bass_kernel_spmd(nc4, maps, core_ids=list(range(NCORES)))
    x4 = np.concatenate([res.results[c]["xout"] for c in range(NCORES)], axis=0)
    return x4.reshape(1, NCORES * R, D)


# version 07 (final)
# speedup vs baseline: 1.0724x; 1.0190x over previous
import contextlib
import numpy as np
import ml_dtypes
import concourse.bass as bass
import concourse.mybir as mybir
from concourse.bass_utils import run_bass_kernel_spmd

F32 = mybir.dt.float32
BF16 = mybir.dt.bfloat16
I32 = mybir.dt.int32
AF = mybir.ActivationFunctionType
ALU = mybir.AluOpType
AX = mybir.AxisListType

NCORES = 8
R = 2048
D = 2048
NT = R // 128
KC = D // 128
FFN_I = 5632
NIC = FFN_I // 128
LN_EPS = 1e-5

ENGINES = ("sync", "gpsimd", "tensor", "vector", "scalar")
DMA_SEM = {"sync": "lq", "gpsimd": "gq", "scalar": "wq", "vector": "vq"}


class P:

    def __init__(self):
        self.nc = bass.Bass("TRN2", target_bir_lowering=False, debug=False)
        self.stk = contextlib.ExitStack()
        self.q = {e: [] for e in ENGINES}
        self.sems = {}
        self.cnt = {}
        self.rel = {}
        self.uid = 0
        self.consts = {}

    def name(self, base):
        self.uid += 1
        return f"{base}_{self.uid}"

    def sem(self, name):
        if name not in self.sems:
            self.sems[name] = self.stk.enter_context(self.nc.semaphore(name))
            self.cnt[name] = 0
        return self.sems[name]

    def sb(self, base, shape, dtype):
        stk = self._scopes[-1] if getattr(self, "_scopes", None) else self.stk
        return stk.enter_context(self.nc.sbuf_tensor(self.name(base), shape, dtype))

    def ps(self, base, shape, dtype=F32):
        stk = self._scopes[-1] if getattr(self, "_scopes", None) else self.stk
        return stk.enter_context(self.nc.psum_tensor(self.name(base), shape, dtype))

    @contextlib.contextmanager
    def block_scope(self):
        if not hasattr(self, "_scopes"):
            self._scopes = []
        blk = contextlib.ExitStack()
        self._scopes.append(blk)
        try:
            yield
        except BaseException:
            self._scopes.pop()
            raise
        else:
            self._scopes.pop()
            blk.close()
            self.barrier(sems=tuple(s for s in self.cnt))

    def const2sb_blk(self, ap, shape, dtype, base, sem):
        t = self.sb(base, shape, dtype)
        self.dma("sync", sem, t[:], ap)
        return t

    def wait_blk_consts(self, sem):
        for eng in ("tensor", "vector", "scalar"):
            self.wait(eng, sem, self.cnt[sem])

    def dram(self, base, shape, dtype, kind="Internal"):
        if kind == "Internal" and base in getattr(self, "dbg_out", ()):
            ap = self.nc.dram_tensor("dbg_" + base, shape, dtype, kind="ExternalOutput").ap()
            return ap
        return self.nc.dram_tensor(self.name(base), shape, dtype, kind=kind).ap()

    def on(self, eng, fn):
        self.q[eng].append(fn)

    def phase(self, name):
        for e in ENGINES:
            self.q[e].append(("phase", name))

    def inc(self, eng, fn, sem, n=1):
        h = self.sem(sem)
        self.cnt[sem] += n
        self.q[eng].append(lambda e, fn=fn, h=h, n=n: fn(e).then_inc(h, n))
        return self.cnt[sem]

    def wait(self, eng, sem, val=None):
        h = self.sem(sem)
        if val is None:
            val = self.cnt[sem]
        if val > 0:
            self.q[eng].append(lambda e, h=h, v=val: e.wait_ge(h, v))

    def dma(self, eng, sem, out, in_, **kw):
        return self.inc(eng, lambda e, o=out, i=in_, kw=kw: e.dma_start(out=o, in_=i, **kw), sem, 16)

    def release(self, key, sem):
        self.rel[key] = [(sem, self.cnt[sem])]

    def release_add(self, key, sem):
        self.rel.setdefault(key, []).append((sem, self.cnt[sem]))

    def acquire(self, eng, key):
        for sem, val in self.rel.get(key, ()):
            self.wait(eng, sem, val)

    def barrier(self, engines=ENGINES, sems=("te", "ve", "se")):
        for eng in engines:
            for s in sems:
                if s in self.cnt and self.cnt[s] > 0:
                    self.wait(eng, s, self.cnt[s])

    def mm_group(self, psum, pairs, sem="te"):
        n = len(pairs)
        for j, (l, r) in enumerate(pairs):
            s, t = j == 0, j == n - 1
            if t:
                v = self.inc("tensor", lambda e, o=psum, l=l, r=r, s=s: e.matmul(o, l, r, start=s, stop=True), sem)
            else:
                self.on("tensor", lambda e, o=psum, l=l, r=r, s=s: e.matmul(o, l, r, start=s, stop=False))
        return v

    def const2sb(self, ap, shape, dtype, base="const"):
        t = self.sb(base, shape, dtype)
        self.dma("sync", "dcn", t[:], ap)
        return t

    def wait_consts(self, eng):
        self.wait(eng, "dcn", self.cnt["dcn"])

    def build(self):
        with self.stk:
            with self.nc.Block() as block:
                for eng in ENGINES:
                    dec = getattr(block, eng)
                    fns = self.q[eng]

                    def body(e, fns=fns):
                        cur = contextlib.ExitStack()
                        try:
                            for f in fns:
                                if isinstance(f, tuple) and f[0] == "phase":
                                    cur.close()
                                    cur = contextlib.ExitStack()
                                    cur.enter_context(self.nc.named_scope(f[1]))
                                else:
                                    f(e)
                        finally:
                            cur.close()
                    dec(body)
        return self.nc



NCONST = 8
C_LNEPS, C_L2EPS, C_ONE, C_SCALE = 0, 1, 2, 3


def emit_setup(p, cvec, identc, onesc, pswapc=None, onesfc=None):
    p.cv = p.const2sb(cvec, [128, NCONST], F32, "cvec")
    p.ident = p.const2sb(identc, [128, 128], BF16, "ident")
    p.ones = p.const2sb(onesc, [128, 128], BF16, "ones")
    if pswapc is not None:
        p.pswap = p.const2sb(pswapc, [128, 128], BF16, "pswap")
    if onesfc is not None:
        p.onesf = p.const2sb(onesfc, [1, 128], F32, "onesf")


def emit_ln(p, xin, h_tm_d, n_tiles=NT):
    ph = contextlib.ExitStack()
    S = lambda b, s, d: ph.enter_context(p.nc.sbuf_tensor(p.name(b), s, d))
    xt = [S("ln_x", [128, D], F32) for _ in range(2)]
    ht = [S("ln_h", [128, D], BF16) for _ in range(2)]
    dump1 = S("ln_dump1", [128, D], BF16)
    dump2 = S("ln_dump2", [128, D], BF16)
    ssum = S("ln_ssum", [128, 1], F32)
    ssq = S("ln_ssq", [128, 1], F32)
    mean = S("ln_mean", [128, 1], F32)
    var = S("ln_var", [128, 1], F32)
    msq = S("ln_msq", [128, 1], F32)
    rstd = S("ln_rstd", [128, 1], F32)

    for tt in range(n_tiles):
        b = tt % 2
        xs = "da" if b == 0 else "db"
        p.acquire("sync", f"ln_x{b}")
        lv = p.dma("sync", xs, xt[b][:], xin[128 * tt:128 * (tt + 1), :])
        p.wait("vector", xs, lv)
        p.wait("scalar", xs, lv)
        sv = p.inc("vector", lambda e, o=dump1, i=xt[b], s=ssum: e.tensor_scalar(
            o[:], i[:], 1.0, 0.0, op0=ALU.mult, op1=ALU.add, accum_out=s[:]), "ve")
        sq = p.inc("scalar", lambda e, o=dump2, i=xt[b], s=ssq: e.activation(
            o[:], i[:], AF.Square, accum_out=s[:]), "se")
        p.wait("vector", "se", sq)
        p.wait("vector", "ve", sv)
        mv = p.inc("vector", lambda e, m=mean, s=ssum: e.tensor_scalar(
            m[:], s[:], 1.0 / D, None, op0=ALU.mult), "ve")
        p.inc("vector", lambda e, v=var, s=ssq: e.tensor_scalar(
            v[:], s[:], 1.0 / D, None, op0=ALU.mult), "ve")
        p.wait("vector", "ve", mv)
        mqv = p.inc("vector", lambda e, m=msq, a=mean: e.tensor_tensor(
            m[:], a[:], a[:], op=ALU.mult), "ve")
        p.wait("vector", "ve", mqv)
        vv = p.inc("vector", lambda e, v=var, m=msq: e.tensor_tensor(
            v[:], v[:], m[:], op=ALU.subtract), "ve")
        p.wait("scalar", "ve", vv)
        sqv = p.inc("scalar", lambda e, v=var, r=rstd, c=p.cv: e.activation(
            r[:], v[:], AF.Sqrt, bias=c[:, C_LNEPS:C_LNEPS + 1]), "se")
        p.wait("vector", "se", sqv)
        rv = p.inc("vector", lambda e, r=rstd: e.reciprocal(r[:], r[:]), "ve")
        p.acquire("vector", f"ln_h{b}")
        p.wait("vector", "ve", rv)
        av = p.inc("vector", lambda e, h=ht[b], x=xt[b], m=mean, r=rstd: e.tensor_scalar(
            h[:], x[:], m[:], r[:], op0=ALU.subtract, op1=ALU.mult), "ve")
        p.release(f"ln_x{b}", "ve")
        p.wait("scalar", "ve", av)
        ws = "dwa" if b == 0 else "dwb"
        wv = p.dma("scalar", ws, h_tm_d[128 * tt:128 * (tt + 1), :], ht[b][:])
        p.release(f"ln_h{b}", ws)
    ph.close()


def emit_load_h_fm(p, h_tm_d, hf, t0, tcols):
    for kc in range(KC):
        p.dma("sync", "dtot", hf[kc][:, 0:tcols],
              h_tm_d[t0:t0 + tcols, 128 * kc:128 * (kc + 1)], transpose=True)
    return p.cnt["dtot"]



def _emit_ffn_impl(p, xin, xout, gwT, uwT, dwT, gsc, usc, dsc, zero_point=False):
    h_tm_d = p.dram("hL0f", [R, D], BF16)
    hdn_d = p.dram("hdnL0f", [NIC, 128, R], BF16)
    for eng in ("tensor", "vector", "scalar"):
        p.wait_consts(eng)

    fid = getattr(p, "_ffn_id", 0)
    p._ffn_id = fid + 1
    p.phase(f"ffn{fid}_ln")
    emit_ln(p, xin, h_tm_d)

    p.phase(f"ffn{fid}_gateup")
    dfm_d = p.dram("dfm", [KC * 128, R], BF16)

    with contextlib.ExitStack() as ph:
        hf_all = ph.enter_context(p.nc.sbuf_tensor(p.name("hf"), [128, KC * R], BF16))
        hf = [hf_all[:, R * kc:R * (kc + 1)] for kc in range(KC)]
        NWB = 4
        gw = [ph.enter_context(p.nc.sbuf_tensor(p.name("gw"), [128, KC, 128], BF16)) for _ in range(NWB)]
        uw = [ph.enter_context(p.nc.sbuf_tensor(p.name("uw"), [128, KC, 128], BF16)) for _ in range(NWB)]
        sg = [ph.enter_context(p.nc.sbuf_tensor(p.name("sg"), [128, 512], BF16)) for _ in range(3)]
        hdn_t = [ph.enter_context(p.nc.sbuf_tensor(p.name("hdn"), [128, R], BF16)) for _ in range(2)]
        pg = [ph.enter_context(p.nc.psum_tensor(p.name("pg"), [128, 512], F32)) for _ in range(3)]
        pu = [ph.enter_context(p.nc.psum_tensor(p.name("pu"), [128, 512], F32)) for _ in range(3)]

        p.wait("sync", "dwa", p.cnt["dwa"])
        p.wait("sync", "dwb", p.cnt["dwb"])
        hv = emit_load_h_fm(p, h_tm_d, hf, 0, R)
        p.wait("tensor", "dtot", p.cnt["dtot"])

        sbc = None
        if zero_point:
            sbc_all = ph.enter_context(p.nc.sbuf_tensor(p.name("sbc"), [128, R], BF16))
            sbc = [sbc_all[:, 512 * q:512 * (q + 1)] for q in range(4)]
            psb = ph.enter_context(p.nc.psum_tensor(p.name("psb"), [128, 512], F32))
            for q in range(4):
                tv = p.mm_group(psb[:], [(p.ones[:], hf[kc][:, 512 * q:512 * (q + 1)])
                                         for kc in range(KC)])
                p.wait("scalar", "te", tv)
                sv = p.inc("scalar", lambda e, o=sbc[q], i_=psb: e.activation(
                    o, i_[:], AF.Copy, scale=-1.0), "se")
                p.wait("tensor", "se", sv)

        WSEM = ["dc", "dd", "dc2", "dd2"]

        def gu_load(i):
            bi = i % NWB
            ws = WSEM[bi]
            p.acquire("gpsimd", f"gw{bi}")
            p.dma("gpsimd", ws, gw[bi][:],
                  gwT[:, 128 * i:128 * (i + 1)].rearrange("(kc p) j -> p kc j", p=128))
            return p.dma("gpsimd", ws, uw[bi][:],
                         uwT[:, 128 * i:128 * (i + 1)].rearrange("(kc p) j -> p kc j", p=128))

        pw_ = [gu_load(i) for i in range(min(3, NIC))]
        for i in range(NIC):
            bi = i % NWB
            ws = WSEM[bi]
            gv = pw_[i]
            if i + 3 < NIC:
                pw_.append(gu_load(i + 3))
            p.wait("tensor", ws, gv)
            p.acquire("vector", f"hdn{i % 2}")
            for q in range(4):
                bq = (i * 4 + q) % 3
                zp = [(p.ones[:], sbc[q])] if zero_point else []
                p.acquire("tensor", f"pg{bq}")
                tg = p.mm_group(pg[bq][:], [(gw[bi][:, kc, :], hf[kc][:, 512 * q:512 * (q + 1)])
                                            for kc in range(KC)] + zp)
                p.acquire("tensor", f"pu{bq}")
                tu = p.mm_group(pu[bq][:], [(uw[bi][:, kc, :], hf[kc][:, 512 * q:512 * (q + 1)])
                                            for kc in range(KC)] + zp)
                p.wait("scalar", "te", tg)
                p.acquire("scalar", f"sg{bq}")
                ev = p.inc("scalar", lambda e, o=sg[bq], i_=pg[bq], s=gsc[:, i:i + 1]:
                           e.activation(o[:], i_[:], AF.Silu, scale=s), "se")
                p.release(f"pg{bq}", "se")
                p.wait("vector", "te", tu)
                p.wait("vector", "se", ev)
                p.inc("vector", lambda e, o=hdn_t[i % 2][:, 512 * q:512 * (q + 1)], i_=pu[bq],
                      s=usc[:, i:i + 1], g=sg[bq]: e.scalar_tensor_tensor(
                          o, i_[:], s, g[:], op0=ALU.mult, op1=ALU.mult), "ve")
                p.release(f"pu{bq}", "ve")
                p.release(f"sg{bq}", "ve")
            p.release(f"gw{bi}", "te")
            p.wait("scalar", "ve", p.cnt["ve"])
            hs = "dwa" if i % 2 == 0 else "dwb"
            p.dma("scalar", hs, hdn_d[i, :, :], hdn_t[i % 2][:])
            p.release(f"hdn{i % 2}", hs)

    p.phase(f"ffn{fid}_down")
    with contextlib.ExitStack() as ph:
        hh_all = ph.enter_context(p.nc.sbuf_tensor(p.name("hh"), [128, NIC * 1024], BF16))
        hh = [hh_all[:, 1024 * i:1024 * (i + 1)] for i in range(NIC)]
        dw = [ph.enter_context(p.nc.sbuf_tensor(p.name("dw"), [128, NIC, 128], BF16)) for _ in range(4)]
        dd = [ph.enter_context(p.nc.sbuf_tensor(p.name("dd"), [128, 512], BF16)) for _ in range(2)]
        pd = [ph.enter_context(p.nc.psum_tensor(p.name("pd"), [128, 512], F32)) for _ in range(2)]

        sbh_all = ph.enter_context(p.nc.sbuf_tensor(p.name("sbh"), [128, 1024], BF16)) if zero_point else None
        psb2 = ph.enter_context(p.nc.psum_tensor(p.name("psb2"), [128, 512], F32)) if zero_point else None
        for half in range(2):
            p.wait("gpsimd", "dwa", p.cnt["dwa"])
            p.wait("gpsimd", "dwb", p.cnt["dwb"])
            p.acquire("gpsimd", "hh")
            for i in range(NIC):
                p.dma("gpsimd", "dgt", hh[i][:], hdn_d[i, :, 1024 * half:1024 * (half + 1)])
            hv = p.cnt["dgt"]
            p.wait("tensor", "dgt", hv)
            if zero_point:
                sbh = [sbh_all[:, 512 * q:512 * (q + 1)] for q in range(2)]
                p.wait("tensor", "se", p.cnt["se"])
                for q in range(2):
                    tv = p.mm_group(psb2[:], [(p.ones[:], hh[i][:, 512 * q:512 * (q + 1)])
                                              for i in range(NIC)])
                    p.wait("scalar", "te", tv)
                    sv = p.inc("scalar", lambda e, o=sbh[q], i_=psb2: e.activation(
                        o, i_[:], AF.Copy, scale=-1.0), "se")
                    p.wait("tensor", "se", sv)
            def dw_load(d):
                bd = d % 4
                ws = WSEM[bd]
                p.acquire("gpsimd", f"dwb{bd}")
                return p.dma("gpsimd", ws, dw[bd][:],
                             dwT[:, 128 * d:128 * (d + 1)].rearrange("(ic p) j -> p ic j", p=128))

            pdw = [dw_load(d) for d in range(3)]
            for d in range(KC):
                bd = d % 4
                ws = WSEM[bd]
                wv = pdw[d]
                if d + 3 < KC:
                    pdw.append(dw_load(d + 3))
                p.wait("tensor", ws, wv)
                for q in range(2):
                    bq = q % 2
                    zp = [(p.ones[:], sbh[q])] if zero_point else []
                    p.acquire("tensor", f"pd{bq}")
                    tv = p.mm_group(pd[bq][:], [(dw[bd][:, i, :], hh[i][:, 512 * q:512 * (q + 1)])
                                                for i in range(NIC)] + zp)
                    p.wait("scalar", "te", tv)
                    p.acquire("scalar", f"dd{bq}")
                    sv = p.inc("scalar", lambda e, o=dd[bq], i_=pd[bq], s=dsc[:, d:d + 1]:
                               e.activation(o[:], i_[:], AF.Copy, scale=s), "se")
                    p.release(f"pd{bq}", "se")
                    p.wait("scalar", "se", sv)
                    ws = "dwa" if bq == 0 else "dwb"
                    p.dma("scalar", ws, dfm_d[128 * d:128 * (d + 1), 1024 * half + 512 * q:1024 * half + 512 * (q + 1)],
                          dd[bq][:])
                    p.release(f"dd{bq}", ws)
                p.release(f"dwb{bd}", "te")
            p.release("hh", "te")

    p.phase(f"ffn{fid}_res")
    rph = contextlib.ExitStack()
    xr = [rph.enter_context(p.nc.sbuf_tensor(p.name("xr"), [128, D], F32)) for _ in range(2)]
    xo = [rph.enter_context(p.nc.sbuf_tensor(p.name("xo"), [128, D], F32)) for _ in range(2)]
    dtt = [rph.enter_context(p.nc.sbuf_tensor(p.name("dtt"), [128, D], BF16)) for _ in range(2)]
    p.wait("sync", "dwa", p.cnt["dwa"])
    p.wait("sync", "dwb", p.cnt["dwb"])
    for tt in range(NT):
        b = tt % 2
        xs = "da" if b == 0 else "db"
        ds_ = "dta" if b == 0 else "dtb"
        ws = "dwa" if b == 0 else "dwb"
        p.acquire("sync", f"res_x{b}")
        lv = p.dma("sync", xs, xr[b][:], xin[128 * tt:128 * (tt + 1), :])
        p.dma("sync", ds_, dtt[b][:], dfm_d[:, 128 * tt:128 * (tt + 1)], transpose=True)
        dv = p.cnt[ds_]
        p.wait("vector", xs, lv)
        p.wait("vector", ds_, dv)
        p.acquire("vector", f"res_o{b}")
        p.inc("vector", lambda e, o=xo[b], a=xr[b], dl=dtt[b]: e.tensor_tensor(
            o[:], a[:], dl[:], op=ALU.add), "ve")
        p.release(f"res_x{b}", "ve")
        p.wait("scalar", "ve", p.cnt["ve"])
        p.dma("scalar", ws, xout[128 * tt:128 * (tt + 1), :], xo[b][:])
        p.release(f"res_o{b}", ws)
    rph.close()



def emit_ln2_l1(p, xin, X_d, h2_d, lnwb, lnbb):
    ph = contextlib.ExitStack()
    S = lambda b, s, d: ph.enter_context(p.nc.sbuf_tensor(p.name(b), s, d))
    xt = [S("l1x", [128, D], F32) for _ in range(2)]
    Xt = [S("l1X", [128, D], F32) for _ in range(2)]
    ht = [S("l1h", [128, D], BF16) for _ in range(2)]
    tmp = S("l1tmp", [128, D], F32)
    dump1 = S("l1d1", [128, D], BF16)
    dump2 = S("l1d2", [128, D], BF16)
    ssum = S("l1ssum", [128, 1], F32)
    ssq = S("l1ssq", [128, 1], F32)
    mean = S("l1mean", [128, 1], F32)
    var = S("l1var", [128, 1], F32)
    msq = S("l1msq", [128, 1], F32)
    rstd = S("l1rstd", [128, 1], F32)

    def stats(src_tile):
        sv = p.inc("vector", lambda e, o=dump1, i=src_tile, s=ssum: e.tensor_scalar(
            o[:], i[:], 1.0, 0.0, op0=ALU.mult, op1=ALU.add, accum_out=s[:]), "ve")
        sq = p.inc("scalar", lambda e, o=dump2, i=src_tile, s=ssq: e.activation(
            o[:], i[:], AF.Square, accum_out=s[:]), "se")
        p.wait("vector", "se", sq)
        p.wait("vector", "ve", sv)
        mv = p.inc("vector", lambda e, m=mean, s=ssum: e.tensor_scalar(
            m[:], s[:], 1.0 / D, None, op0=ALU.mult), "ve")
        p.inc("vector", lambda e, v=var, s=ssq: e.tensor_scalar(
            v[:], s[:], 1.0 / D, None, op0=ALU.mult), "ve")
        p.wait("vector", "ve", mv)
        mqv = p.inc("vector", lambda e, m=msq, a=mean: e.tensor_tensor(
            m[:], a[:], a[:], op=ALU.mult), "ve")
        p.wait("vector", "ve", mqv)
        vv = p.inc("vector", lambda e, v=var, m=msq: e.tensor_tensor(
            v[:], v[:], m[:], op=ALU.subtract), "ve")
        p.wait("scalar", "ve", vv)
        sqv = p.inc("scalar", lambda e, v=var, r=rstd, c=p.cv: e.activation(
            r[:], v[:], AF.Sqrt, bias=c[:, C_LNEPS:C_LNEPS + 1]), "se")
        p.wait("vector", "se", sqv)
        rv = p.inc("vector", lambda e, r=rstd: e.reciprocal(r[:], r[:]), "ve")
        p.wait("vector", "ve", rv)

    for tt in range(NT):
        b = tt % 2
        xs = "da" if b == 0 else "db"
        ws = "dwa" if b == 0 else "dwb"
        p.acquire("sync", f"l1x{b}")
        lv = p.dma("sync", xs, xt[b][:], xin[128 * tt:128 * (tt + 1), :])
        p.wait("vector", xs, lv)
        p.wait("scalar", xs, lv)
        stats(xt[b])
        p.acquire("vector", f"l1X{b}")
        xv = p.inc("vector", lambda e, o=Xt[b], x=xt[b], m=mean, r=rstd: e.tensor_scalar(
            o[:], x[:], m[:], r[:], op0=ALU.subtract, op1=ALU.mult), "ve")
        p.release(f"l1x{b}", "ve")
        p.wait("scalar", "ve", xv)
        p.dma("scalar", ws, X_d[128 * tt:128 * (tt + 1), :], Xt[b][:])
        p.wait("vector", "ve", xv)
        stats(Xt[b])
        nv = p.inc("vector", lambda e, o=tmp, x=Xt[b], m=mean, r=rstd: e.tensor_scalar(
            o[:], x[:], m[:], r[:], op0=ALU.subtract, op1=ALU.mult), "ve")
        p.release(f"l1X{b}", "ve")
        p.release_add(f"l1X{b}", ws)
        p.wait("vector", "ve", nv)
        wv_ = p.inc("vector", lambda e, o=tmp, w=lnwb: e.tensor_tensor(
            o[:], o[:], w[:], op=ALU.mult), "ve")
        p.wait("vector", "ve", wv_)
        p.acquire("vector", f"l1h{b}")
        hv = p.inc("vector", lambda e, o=ht[b], t=tmp, bb=lnbb: e.tensor_tensor(
            o[:], t[:], bb[:], op=ALU.add), "ve")
        p.wait("scalar", "ve", hv)
        ws2 = "dwc" if b == 0 else "dwd"
        p.dma("scalar", ws2, h2_d[128 * tt:128 * (tt + 1), :], ht[b][:])
        p.release(f"l1h{b}", ws2)
    ph.close()


def _emit_l1_attn_impl(p, xin, xout, WinAp, WoutAp, binc, boutc, lnwb, lnbb,
                 costab, sintab, onesf, replica_groups):
    X_d = p.dram("Xl1", [R, D], F32)
    h2_d = p.dram("h2l1", [R, D], BF16)
    Q_d = p.dram("Ql1", [KC, 128, R], BF16)
    ccin = p.dram("ccin", [128, KC], F32)
    ccout = p.dram("ccout", [128, KC], F32)
    for eng in ("tensor", "vector", "scalar"):
        p.wait_consts(eng)

    p.phase("l1a_ln")
    emit_ln2_l1(p, xin, X_d, h2_d, lnwb, lnbb)

    dfm_d = p.dram("l1dfm", [KC * 128, R], BF16)
    accA = p.sb("accA", [128, KC], F32)
    zv = p.inc("vector", lambda e, o=accA: e.memset(o[:], 0.0), "ve")

    p.phase("l1a_p1")
    with contextlib.ExitStack() as ph:
        S = lambda b, s, d: ph.enter_context(p.nc.sbuf_tensor(p.name(b), s, d))
        hf2 = [S("hf2", [128, 512], BF16) for _ in range(KC)]
        qsb = [S("qsb", [128, 512], BF16) for _ in range(KC)]
        ksb = [S("ksb", [128, 512], BF16) for _ in range(KC)]
        vsb = [S("vsb", [128, 512], BF16) for _ in range(KC)]
        ctb = [S("ctb", [128, 512], BF16) for _ in range(8)]
        stb = [S("stb", [128, 512], BF16) for _ in range(8)]
        wcol = [S("wcol", [128, KC, 128], BF16) for _ in range(4)]
        rsc1 = S("rsc1", [128, 512], F32)
        rsc2 = S("rsc2", [128, 512], F32)
        ksq = [S("ksq", [128, 512], BF16) for _ in range(2)]
        knrm = S("knrm", [128, 512], BF16)
        n2row = S("n2row", [1, 512], F32)
        rdump = S("rdump", [128, 512], F32)
        rcol = S("rcol", [128, 1], F32)
        pqkv = [ph.enter_context(p.nc.psum_tensor(p.name("pqkv"), [128, 512], F32)) for _ in range(2)]
        prh = ph.enter_context(p.nc.psum_tensor(p.name("prh"), [128, 512], F32))
        pn2 = ph.enter_context(p.nc.psum_tensor(p.name("pn2"), [1, 512], F32))
        prb = ph.enter_context(p.nc.psum_tensor(p.name("prb"), [128, 512], F32))

        for q in range(4):
            t0 = 512 * q
            p.wait("sync", "dwa", p.cnt["dwa"])
            p.wait("sync", "dwb", p.cnt["dwb"])
            p.wait("sync", "dwc", p.cnt["dwc"])
            p.wait("sync", "dwd", p.cnt["dwd"])
            for kc in range(KC):
                p.acquire("sync", f"hf2_{kc}")
                p.dma("sync", "dtot", hf2[kc][:], h2_d[t0:t0 + 512, 128 * kc:128 * (kc + 1)],
                      transpose=True)
            for c in range(8):
                p.acquire("sync", f"rt{c}")
                p.dma("sync", "dtot", ctb[c][:], costab[128 * c:128 * (c + 1), t0:t0 + 512])
                p.dma("sync", "dtot", stb[c][:], sintab[128 * c:128 * (c + 1), t0:t0 + 512])
            hv = p.cnt["dtot"]
            p.wait("tensor", "dtot", hv)
            p.wait("vector", "dtot", hv)
            WS4 = ["dc", "dd", "dc2", "dd2"]

            def qkv_load(oc):
                bw = oc % 4
                p.acquire("gpsimd", f"wcol{bw}")
                return p.dma("gpsimd", WS4[bw], wcol[bw][:],
                             WinAp[:, 128 * oc:128 * (oc + 1)].rearrange("(kc p) j -> p kc j", p=128))

            pq_ = [qkv_load(oc) for oc in range(3)]
            for oc in range(3 * KC):
                bw = oc % 4
                wv = pq_[oc]
                if oc + 3 < 3 * KC:
                    pq_.append(qkv_load(oc + 3))
                p.wait("tensor", WS4[bw], wv)
                p.acquire("tensor", f"pqkv{oc % 2}")
                bwp = oc % 2
                tv = p.mm_group(pqkv[bwp][:], [(wcol[bw][:, kc, :], hf2[kc][:]) for kc in range(KC)])
                p.release(f"wcol{bw}", "te")
                dst = (qsb if oc < KC else ksb if oc < 2 * KC else vsb)[oc % KC]
                p.wait("scalar", "te", tv)
                p.acquire("scalar", f"l1t_{oc % KC}_{oc // KC}")
                sv = p.inc("scalar", lambda e, o=dst, i_=pqkv[bwp], bc=binc[:, oc:oc + 1]:
                           e.activation(o[:], i_[:], AF.Identity, bias=bc), "se")
                p.release(f"pqkv{bwp}", "se")
            sev = p.cnt["se"]
            v3 = 0
            for tiles in (qsb, ksb):
                for c in range(8):
                    p.wait("tensor", "se", sev)
                    p.acquire("tensor", "prh")
                    rv = p.inc("tensor", lambda e, o=prh, t=tiles[c], pw=p.pswap:
                               e.matmul(o[:], pw[:], t[:], start=True, stop=True), "te")
                    p.wait("vector", "te", rv)
                    p.wait("vector", "ve", v3)
                    v1 = p.inc("vector", lambda e, o=rsc1, i_=prh, s=stb[c]: e.tensor_tensor(
                        o[:], i_[:], s[:], op=ALU.mult), "ve")
                    p.release("prh", "ve")
                    v2 = p.inc("vector", lambda e, o=rsc2, t=tiles[c], ct=ctb[c]: e.tensor_tensor(
                        o[:], t[:], ct[:], op=ALU.mult), "ve")
                    p.wait("vector", "ve", v2)
                    v3 = p.inc("vector", lambda e, o=tiles[c], a=rsc1, bb=rsc2: e.tensor_tensor(
                        o[:], a[:], bb[:], op=ALU.add), "ve")
            p.wait("scalar", "ve", p.cnt["ve"])
            for kc in range(KC):
                p.dma("scalar", "dwa", Q_d[kc, :, t0:t0 + 512], qsb[kc][:])
            qspill = p.cnt["dwa"]
            vev = p.cnt["ve"]
            p.acquire("tensor", "pn2")
            for c in range(KC):
                bw = c % 2
                p.wait("scalar", "ve", vev)
                p.acquire("scalar", f"ksq{bw}")
                qv = p.inc("scalar", lambda e, o=ksq[bw], i_=ksb[c]: e.activation(
                    o[:], i_[:], AF.Square), "se")
                p.wait("tensor", "se", qv)
                fn = lambda e, o=pn2, l=p.ones[:, 0:1], r=ksq[bw], s=(c == 0), t=(c == KC - 1): \
                    e.matmul(o[:], l, r[:], start=s, stop=t)
                tv = p.inc("tensor", fn, "te")
                p.release(f"ksq{bw}", "te")
            p.wait("vector", "te", p.cnt["te"])
            nv = p.inc("vector", lambda e, o=n2row, i_=pn2: e.tensor_copy(o[:], i_[:]), "ve")
            p.wait("scalar", "ve", nv)
            sv = p.inc("scalar", lambda e, o=n2row: e.activation(o[:], o[:], AF.Sqrt), "se")
            p.wait("vector", "se", sv)
            rv = p.inc("vector", lambda e, o=n2row: e.reciprocal(o[:], o[:]), "ve")
            p.wait("tensor", "ve", rv)
            bv = p.inc("tensor", lambda e, o=prb, l=onesf[:], r=n2row: e.matmul(
                o[:], l, r[:], start=True, stop=True), "te")
            p.wait("vector", "te", bv)
            for c in range(KC):
                k1 = p.inc("vector", lambda e, o=knrm, k=ksb[c], r=prb: e.tensor_tensor(
                    o[:], k[:], r[:], op=ALU.mult), "ve")
                p.wait("vector", "ve", k1)
                k2 = p.inc("vector", lambda e, o=rdump, k=knrm, v=vsb[c]: e.tensor_tensor(
                    o[:], k[:], v[:], op=ALU.mult), "ve")
                p.wait("vector", "ve", k2)
                k2 = p.inc("vector", lambda e, o=rcol, i_=rdump: e.tensor_reduce(
                    o[:], i_[:], axis=AX.X, op=ALU.add), "ve")
                p.wait("vector", "ve", k2)
                k3 = p.inc("vector", lambda e, o=accA[:, c:c + 1], rc=rcol: e.tensor_tensor(
                    o, o, rc[:], op=ALU.add), "ve")
            vv_all = p.cnt["ve"]
            p.release("pn2", "ve")
            for kc in range(KC):
                p.rel[f"hf2_{kc}"] = [("te", p.cnt["te"])]
                p.rel[f"l1t_{kc}_0"] = [("dwa", qspill)]
                p.rel[f"l1t_{kc}_1"] = [("ve", vv_all)]
                p.rel[f"l1t_{kc}_2"] = [("ve", vv_all)]
            for c in range(8):
                p.rel[f"rt{c}"] = [("ve", vv_all)]

    p.wait("gpsimd", "ve", p.cnt["ve"])
    p.dma("gpsimd", "dcc", ccin, accA[:])
    p.wait("gpsimd", "dcc", p.cnt["dcc"])
    p.inc("gpsimd", lambda e, i_=ccin, o_=ccout, rg=replica_groups: e.collective_compute(
        "AllReduce", ALU.add, replica_groups=rg, ins=[i_.opt()], outs=[o_.opt()]), "dcl", 1)
    p.wait("gpsimd", "dcl", p.cnt["dcl"])
    p.dma("gpsimd", "dcc", accA[:], ccout)
    ccv = p.cnt["dcc"]
    p.wait("vector", "dcc", ccv)

    p.phase("l1a_p2")
    p.barrier(sems=("te", "ve", "se", "dwa", "dwb", "dwc", "dwd"))
    with contextlib.ExitStack() as ph:
        S = lambda b, s, d: ph.enter_context(p.nc.sbuf_tensor(p.name(b), s, d))
        ysb = [S("ysb", [128, 512], BF16) for _ in range(KC)]
        wcol = [S("wcol2", [128, KC, 128], BF16) for _ in range(4)]
        dd = [S("l1dd", [128, 512], BF16) for _ in range(2)]
        pw = [ph.enter_context(p.nc.psum_tensor(p.name("pw"), [128, 512], F32)) for _ in range(2)]
        for q in range(4):
            t0 = 512 * q
            p.wait("sync", "dwa", p.cnt["dwa"])
            p.wait("sync", "dwb", p.cnt["dwb"])
            for kc in range(KC):
                p.acquire("sync", f"ysb{kc}")
                p.dma("sync", "dtot", ysb[kc][:], Q_d[kc, :, t0:t0 + 512])
            hv = p.cnt["dtot"]
            p.wait("vector", "dtot", hv)
            for kc in range(KC):
                p.inc("vector", lambda e, o=ysb[kc], a=accA[:, kc:kc + 1]: e.tensor_scalar(
                    o[:], o[:], a, None, op0=ALU.mult), "ve")
            yv = p.cnt["ve"]
            p.wait("tensor", "ve", yv)
            WS4 = ["dc", "dd", "dc2", "dd2"]

            def wout_load(oc):
                bw = oc % 4
                p.acquire("gpsimd", f"wc2{bw}")
                return p.dma("gpsimd", WS4[bw], wcol[bw][:],
                             WoutAp[:, 128 * oc:128 * (oc + 1)].rearrange("(kc p) j -> p kc j", p=128))

            pq2_ = [wout_load(oc) for oc in range(3)]
            for oc in range(KC):
                bw = oc % 4
                wv = pq2_[oc]
                if oc + 3 < KC:
                    pq2_.append(wout_load(oc + 3))
                p.wait("tensor", WS4[bw], wv)
                b2 = oc % 2
                p.acquire("tensor", f"pw{b2}")
                tv = p.mm_group(pw[b2][:], [(wcol[bw][:, kc, :], ysb[kc][:]) for kc in range(KC)])
                p.release(f"wc2{bw}", "te")
                p.wait("scalar", "te", tv)
                p.acquire("scalar", f"l1dd{b2}")
                sv = p.inc("scalar", lambda e, o=dd[b2], i_=pw[b2], bc=boutc[:, oc:oc + 1]:
                           e.activation(o[:], i_[:], AF.Identity, bias=bc), "se")
                p.release(f"pw{b2}", "se")
                p.wait("scalar", "se", sv)
                wsd = "dwa" if b2 == 0 else "dwb"
                p.dma("scalar", wsd, dfm_d[128 * oc:128 * (oc + 1), t0:t0 + 512], dd[b2][:])
                p.release(f"l1dd{b2}", wsd)
            for kc in range(KC):
                p.release(f"ysb{kc}", "te")

    rph = contextlib.ExitStack()
    xr = [rph.enter_context(p.nc.sbuf_tensor(p.name("xr"), [128, D], F32)) for _ in range(2)]
    Xr = [rph.enter_context(p.nc.sbuf_tensor(p.name("Xr"), [128, D], F32)) for _ in range(2)]
    xo = [rph.enter_context(p.nc.sbuf_tensor(p.name("xo"), [128, D], F32)) for _ in range(2)]
    dtt = [rph.enter_context(p.nc.sbuf_tensor(p.name("dtt"), [128, D], BF16)) for _ in range(2)]
    p.wait("sync", "dwa", p.cnt["dwa"])
    p.wait("sync", "dwb", p.cnt["dwb"])
    p.wait("gpsimd", "ve", p.cnt["ve"])
    p.wait("gpsimd", "te", p.cnt["te"])
    p.wait("gpsimd", "se", p.cnt["se"])
    p.wait("gpsimd", "dwa", p.cnt["dwa"])
    p.wait("gpsimd", "dwb", p.cnt["dwb"])
    for tt in range(NT):
        b = tt % 2
        xs = "da" if b == 0 else "db"
        ds_ = "dta" if b == 0 else "dtb"
        Xs = "dc" if b == 0 else "dd"
        ws = "dwa" if b == 0 else "dwb"
        p.acquire("sync", f"r2x{b}")
        lv = p.dma("sync", xs, xr[b][:], xin[128 * tt:128 * (tt + 1), :])
        p.dma("sync", ds_, dtt[b][:], dfm_d[:, 128 * tt:128 * (tt + 1)], transpose=True)
        dv = p.cnt[ds_]
        p.acquire("gpsimd", f"r2X{b}")
        Xv = p.dma("gpsimd", Xs, Xr[b][:], X_d[128 * tt:128 * (tt + 1), :])
        p.wait("vector", xs, lv)
        p.wait("vector", Xs, Xv)
        p.wait("vector", ds_, dv)
        p.acquire("vector", f"r2o{b}")
        a1 = p.inc("vector", lambda e, o=xo[b], a=xr[b], x2=Xr[b]: e.tensor_tensor(
            o[:], a[:], x2[:], op=ALU.add), "ve")
        p.release(f"r2X{b}", "ve")
        p.wait("vector", "ve", a1)
        p.inc("vector", lambda e, o=xo[b], dl=dtt[b]: e.tensor_tensor(
            o[:], o[:], dl[:], op=ALU.add), "ve")
        p.release(f"r2x{b}", "ve")
        p.wait("scalar", "ve", p.cnt["ve"])
        p.dma("scalar", ws, xout[128 * tt:128 * (tt + 1), :], xo[b][:])
        p.release(f"r2o{b}", ws)
    rph.close()



C64 = 64
NCH = R // C64
NG = 4
SCL = 128.0 ** -0.5


def emit_ln_l0(p, xin, xhalo, h_d):
    emit_ln(p, xin, h_d[3:3 + R, :])
    p.barrier(engines=("sync", "vector", "scalar"), sems=("ve", "se", "dwa", "dwb"))
    ph = contextlib.ExitStack()
    S = lambda b, s, d: ph.enter_context(p.nc.sbuf_tensor(p.name(b), s, d))
    hx = S("hlx", [3, D], F32)
    hh = S("hlh", [3, D], BF16)
    hd1 = S("hld1", [3, D], BF16)
    hd2 = S("hld2", [3, D], BF16)
    hsum = S("hlsum", [3, 1], F32)
    hsq = S("hlsq", [3, 1], F32)
    hmean = S("hlmean", [3, 1], F32)
    hvar = S("hlvar", [3, 1], F32)
    hmsq = S("hlmsq", [3, 1], F32)
    hrstd = S("hlrstd", [3, 1], F32)
    lv = p.dma("sync", "da", hx[:], xhalo)
    p.wait("vector", "da", lv)
    p.wait("scalar", "da", lv)
    sv = p.inc("vector", lambda e, o=hd1, i=hx, s=hsum: e.tensor_scalar(
        o[:], i[:], 1.0, 0.0, op0=ALU.mult, op1=ALU.add, accum_out=s[:]), "ve")
    sq = p.inc("scalar", lambda e, o=hd2, i=hx, s=hsq: e.activation(
        o[:], i[:], AF.Square, accum_out=s[:]), "se")
    p.wait("vector", "se", sq)
    p.wait("vector", "ve", sv)
    mv = p.inc("vector", lambda e, m=hmean, s=hsum: e.tensor_scalar(
        m[:], s[:], 1.0 / D, None, op0=ALU.mult), "ve")
    p.inc("vector", lambda e, v=hvar, s=hsq: e.tensor_scalar(
        v[:], s[:], 1.0 / D, None, op0=ALU.mult), "ve")
    p.wait("vector", "ve", mv)
    mqv = p.inc("vector", lambda e, m=hmsq, a=hmean: e.tensor_tensor(
        m[:], a[:], a[:], op=ALU.mult), "ve")
    p.wait("vector", "ve", mqv)
    vv = p.inc("vector", lambda e, v=hvar, m=hmsq: e.tensor_tensor(
        v[:], v[:], m[:], op=ALU.subtract), "ve")
    p.wait("scalar", "ve", vv)
    sqv = p.inc("scalar", lambda e, v=hvar, r=hrstd, c=p.cv: e.activation(
        r[:], v[:], AF.Sqrt, bias=c[0:3, C_LNEPS:C_LNEPS + 1]), "se")
    p.wait("vector", "se", sqv)
    rv = p.inc("vector", lambda e, r=hrstd: e.reciprocal(r[:], r[:]), "ve")
    p.wait("vector", "ve", rv)
    av = p.inc("vector", lambda e, h=hh, x=hx, m=hmean, r=hrstd: e.tensor_scalar(
        h[:], x[:], m[:], r[:], op0=ALU.subtract, op1=ALU.mult), "ve")
    p.wait("scalar", "ve", av)
    p.dma("scalar", "dwa", h_d[0:3, :], hh[:])
    ph.close()


def _emit_l0_attn_impl(p, xin, xhalo, xout, Wq, Wk, Wv, Ww, Wf1, Wf2, Wb, Wg1, Wg2, Wo,
                 cq2d, ck2d, cv2d, bg2b, normwb, trix, masksu, cmask2d, onesf2,
                 replica_groups):
    h_d = p.dram("hL0a", [3 + R, D], BF16)
    qd = p.dram("qdL0", [KC, 128, R], BF16)
    kd = p.dram("kdL0", [KC, 128, R], BF16)
    vd = p.dram("vdL0", [KC, 128, R], BF16)
    wd = p.dram("wdL0", [KC, 128, R], BF16)
    gnd = p.dram("gnd", [R, D], F32)
    gsd = p.dram("gsd", [R, D], BF16)
    Btd = p.dram("Btd", [R, D], BF16)
    Bhd = p.dram("Bhd", [R, D], BF16)
    Kpd = p.dram("Kpd", [R, D], BF16)
    Vpd = p.dram("Vpd", [R, D], BF16)
    Dtd = p.dram("Dtd", [R, D], BF16)
    ecd = p.dram("ecd", [NCH, D], F32)
    ecc = p.dram("ecc", [NCH, 128, 16], F32)
    otd = p.dram("otd", [R, D], BF16)
    nworld = len(replica_groups[0])
    agin = p.dram("agin", [128, 2 * D], BF16)
    agout = p.dram("agout", [nworld * 128, 2 * D], BF16)
    ctin = p.dram("ctin", [1, D], F32)
    ctout = p.dram("ctout", [nworld, D], F32)
    for eng in ("tensor", "vector", "scalar"):
        p.wait_consts(eng)

    p.phase("l0a_ln")
    emit_ln_l0(p, xin, xhalo, h_d)

    beta_sb = p.sb("beta", [128, NT * 16], F32)
    betat = [beta_sb[:, 16 * t:16 * (t + 1)] for t in range(NT)]
    Skb = p.sb("Skb", [128, D], BF16)
    Svb = p.sb("Svb", [128, D], BF16)
    ctot = p.sb("ctot", [1, D], F32)
    hfm_all2 = [p.sb("hfm", [128, KC * 512], BF16) for _ in range(2)]
    hfm2 = [[t[:, 512 * kc:512 * (kc + 1)] for kc in range(KC)] for t in hfm_all2]
    hhalo_all = p.sb("hhalo", [128, KC * 3], BF16)
    hhalo = [hhalo_all[:, 3 * kc:3 * (kc + 1)] for kc in range(KC)]

    with contextlib.ExitStack() as ph:
        S = lambda b, s, d: ph.enter_context(p.nc.sbuf_tensor(p.name(b), s, d))
        htm3 = S("htm3", [3, D], BF16)
        ptp = ph.enter_context(p.nc.psum_tensor(p.name("ptp"), [128, 3], BF16))
        p.wait("sync", "dwa", p.cnt["dwa"])
        p.wait("sync", "dwb", p.cnt["dwb"])
        lv = p.dma("sync", "dtot", htm3[:], h_d[0:3, :])
        p.wait("tensor", "dtot", lv)
        for kc in range(KC):
            p.acquire("tensor", "ptp")
            tv = p.inc("tensor", lambda e, o=ptp, i_=htm3[:, 128 * kc:128 * (kc + 1)],
                       id_=p.ident: e.transpose(o[:], i_, id_[0:3, 0:3]), "te")
            p.wait("vector", "te", tv)
            vv = p.inc("vector", lambda e, o=hhalo[kc], i_=ptp: e.tensor_copy(o, i_[:]), "ve")
            p.release("ptp", "ve")
        p.wait("tensor", "ve", p.cnt["ve"])

    p.phase("l0a_proj")
    vzero = p.inc("vector", lambda e, o=ctot: e.memset(o[:], 0.0), "ve")
    with contextlib.ExitStack() as ph:
        S = lambda b, s, d: ph.enter_context(p.nc.sbuf_tensor(p.name(b), s, d))
        wfull = S("wfull", [128, KC, KC, 128], BF16)
        tails = S("tails", [128, KC * 3], F32)
        acc = [S("acc", [128, 512], F32) for _ in range(4)]
        qf = [S("qf", [128, 512], BF16) for _ in range(4)]
        pp = [ph.enter_context(p.nc.psum_tensor(p.name("pp"), [128, 512], F32)) for _ in range(4)]
        pmini = ph.enter_context(p.nc.psum_tensor(p.name("pmini"), [128, 3], F32))

        for (Wt, convt, dstd) in ((Wq, cq2d, qd), (Wk, ck2d, kd), (Wv, cv2d, vd), (Ww, cv2d, wd)):
            p.wait("gpsimd", "te", p.cnt["te"])
            for oc in range(KC):
                p.dma("gpsimd", "dgt", wfull[:, oc, :, :],
                      Wt[:, 128 * oc:128 * (oc + 1)].rearrange("(kc p) j -> p kc j", p=128))
            wv_ = p.cnt["dgt"]
            p.wait("tensor", "dgt", wv_)
            for oc in range(KC):
                p.acquire("tensor", "pmini")
                tv = p.mm_group(pmini[:], [(wfull[:, oc, kc, :], hhalo[kc][:]) for kc in range(KC)])
                p.wait("vector", "te", tv)
                vv = p.inc("vector", lambda e, o=tails[:, 3 * oc:3 * (oc + 1)], i_=pmini:
                           e.tensor_copy(o, i_[:]), "ve")
                p.release("pmini", "ve")
            def p1_loads(tile):
                bh = tile % 2
                semh = "dta" if bh == 0 else "dtb"
                t0 = 512 * tile
                p.acquire("sync", f"hfmb{bh}")
                for kc in range(KC):
                    p.dma("sync", semh, hfm2[bh][kc][:], h_d[3 + t0:3 + t0 + 512,
                                                             128 * kc:128 * (kc + 1)], transpose=True)
                return p.cnt[semh]

            pendh = p1_loads(0)
            for tile in range(NG):
                t0 = 512 * tile
                bh = tile % 2
                hfm = hfm2[bh]
                semh = "dta" if bh == 0 else "dtb"
                hv = pendh
                if tile + 1 < NG:
                    pendh = p1_loads(tile + 1)
                p.wait("tensor", semh, hv)
                for oc in range(KC):
                    bq = (tile * KC + oc) % 4
                    p.acquire("tensor", f"pp{bq}")
                    tv = p.mm_group(pp[bq][:], [(wfull[:, oc, kc, :], hfm[kc][:]) for kc in range(KC)])
                    p.wait("scalar", "te", tv)
                    p.acquire("scalar", f"accb{bq}")
                    c3 = p.inc("scalar", lambda e, o=acc[bq], i_=pp[bq], s=convt[:, 4 * oc + 3:4 * oc + 4]:
                               e.activation(o[:], i_[:], AF.Identity, scale=s), "se")
                    p.wait("vector", "se", c3)
                    for i in (2, 1, 0):
                        a0 = 3 - i
                        cv_ = p.inc("vector", lambda e, o=acc[bq][:, a0:512], i_=pp[bq][:, 0:512 - a0],
                              s=convt[:, 4 * oc + i:4 * oc + i + 1], ac=acc[bq][:, a0:512]:
                              e.scalar_tensor_tensor(o, i_, s, ac, op0=ALU.mult, op1=ALU.add), "ve")
                        cv_ = p.inc("vector", lambda e, o=acc[bq][:, 0:3 - i], t_=tails[:, 3 * oc + i:3 * oc + 3],
                              s=convt[:, 4 * oc + i:4 * oc + i + 1]:
                              e.scalar_tensor_tensor(o, t_, s, o, op0=ALU.mult, op1=ALU.add), "ve")
                    tnv = p.inc("vector", lambda e, o=tails[:, 3 * oc:3 * (oc + 1)], i_=pp[bq][:, 509:512]:
                                e.tensor_copy(o, i_), "ve")
                    p.release(f"pp{bq}", "ve")
                    p.wait("scalar", "ve", tnv)
                    p.acquire("scalar", f"qf{bq}")
                    fv = p.inc("scalar", lambda e, o=qf[bq], i_=acc[bq]: e.activation(
                        o[:], i_[:], AF.Silu), "se")
                    p.rel[f"accb{bq}"] = [("se", fv)]
                    ws = ["dwa", "dwb", "dwc", "dwd"][bq]
                    p.wait("scalar", "se", fv)
                    p.dma("scalar", ws, dstd[oc, :, t0:t0 + 512], qf[bq][:])
                    p.release(f"qf{bq}", ws)
                    if oc == KC - 1:
                        p.release(f"hfmb{bh}", "te")

    p.phase("l0a_gates")
    p.barrier(sems=("te", "ve", "se", "dwa", "dwb", "dwc", "dwd"))
    with contextlib.ExitStack() as ph:
        S = lambda b, s, d: ph.enter_context(p.nc.sbuf_tensor(p.name(b), s, d))
        wf1s = S("wf1s", [128, KC, 128], BF16)
        wg1s = S("wg1s", [128, KC, 128], BF16)
        wbs = S("wbs", [128, KC, 16], BF16)
        wf2s = S("wf2s", [128, D], BF16)
        wg2s = S("wg2s", [128, D], BF16)
        f1 = S("f1", [128, 512], BF16)
        g1 = S("g1", [128, 512], BF16)
        sc1 = S("sc1", [128, 512], F32)
        sc2 = S("sc2", [128, 512], F32)
        gnt = S("gnt", [128, D], F32)
        gst = S("gst", [128, D], BF16)
        pj = [ph.enter_context(p.nc.psum_tensor(p.name("pj"), [128, 512], F32)) for _ in range(2)]
        pb = ph.enter_context(p.nc.psum_tensor(p.name("pb"), [128, 16], F32))

        p.wait("gpsimd", "te", p.cnt["te"])
        p.dma("gpsimd", "dgt", wf1s[:], Wf1.rearrange("(kc p) j -> p kc j", p=128))
        p.dma("gpsimd", "dgt", wg1s[:], Wg1.rearrange("(kc p) j -> p kc j", p=128))
        p.dma("gpsimd", "dgt", wbs[:], Wb.rearrange("(kc p) j -> p kc j", p=128))
        p.dma("gpsimd", "dgt", wf2s[:], Wf2)
        wv_ = p.dma("gpsimd", "dgt", wg2s[:], Wg2)
        p.wait("tensor", "dgt", wv_)
        def p2_loads(tile):
            bh = tile % 2
            semh = "dta" if bh == 0 else "dtb"
            t0 = 512 * tile
            p.acquire("sync", f"hfmb{bh}")
            for kc in range(KC):
                p.dma("sync", semh, hfm2[bh][kc][:], h_d[3 + t0:3 + t0 + 512,
                                                         128 * kc:128 * (kc + 1)], transpose=True)
            return p.cnt[semh]

        pendh = p2_loads(0)
        for tile in range(NG):
            t0 = 512 * tile
            bh = tile % 2
            hfm = hfm2[bh]
            semh = "dta" if bh == 0 else "dtb"
            hv = pendh
            if tile + 1 < NG:
                pendh = p2_loads(tile + 1)
            p.wait("tensor", semh, hv)
            p.acquire("tensor", "pj0")
            tv = p.mm_group(pj[0][:], [(wf1s[:, kc, :], hfm[kc][:]) for kc in range(KC)])
            p.wait("scalar", "te", tv)
            p.acquire("scalar", "f1")
            fv = p.inc("scalar", lambda e, o=f1, i_=pj[0]: e.activation(o[:], i_[:], AF.Copy), "se")
            p.release("pj0", "se")
            p.acquire("tensor", "pj1")
            tv = p.mm_group(pj[1][:], [(wg1s[:, kc, :], hfm[kc][:]) for kc in range(KC)])
            p.wait("scalar", "te", tv)
            p.acquire("scalar", "g1")
            gv = p.inc("scalar", lambda e, o=g1, i_=pj[1]: e.activation(o[:], i_[:], AF.Copy), "se")
            p.release("pj1", "se")
            for j in range(4):
                tt_i = tile * 4 + j
                p.acquire("scalar", "gn_sp")
                p.acquire("vector", "gn_sp")
                p.acquire("tensor", "pb")
                tv = p.mm_group(pb[:], [(hfm[kc][:, 128 * j:128 * (j + 1)], wbs[:, kc, :])
                                        for kc in range(KC)])
                p.wait("scalar", "te", tv)
                bv = p.inc("scalar", lambda e, o=betat[tt_i], i_=pb: e.activation(
                    o, i_[:], AF.Sigmoid), "se")
                p.release("pb", "se")
                p.wait("tensor", "se", max(fv, gv))
                for q2 in range(4):
                    cs = slice(512 * q2, 512 * (q2 + 1))
                    p.acquire("tensor", "pj0")
                    tv = p.inc("tensor", lambda e, o=pj[0], l=f1[:, 128 * j:128 * (j + 1)],
                               r=wf2s[:, cs]: e.matmul(o[:], l, r, start=True, stop=True), "te")
                    p.wait("scalar", "te", tv)
                    p.acquire("scalar", "sc1")
                    e1 = p.inc("scalar", lambda e, o=sc1, i_=pj[0]: e.activation(
                        o[:], i_[:], AF.Exp, scale=-1.0), "se")
                    p.release("pj0", "se")
                    p.wait("scalar", "se", e1)
                    e2 = p.inc("scalar", lambda e, o=gnt[:, cs], i_=sc1: e.activation(
                        o, i_[:], AF.Ln, bias=1.0), "se")
                    p.release("sc1", "se")
                    p.acquire("tensor", "pj1")
                    tv = p.inc("tensor", lambda e, o=pj[1], l=g1[:, 128 * j:128 * (j + 1)],
                               r=wg2s[:, cs]: e.matmul(o[:], l, r, start=True, stop=True), "te")
                    p.wait("vector", "te", tv)
                    p.acquire("vector", "sc2")
                    v1 = p.inc("vector", lambda e, o=sc2, i_=pj[1], b=bg2b[:, cs]: e.tensor_tensor(
                        o[:], i_[:], b, op=ALU.add), "ve")
                    p.release("pj1", "ve")
                    p.wait("scalar", "ve", v1)
                    s1 = p.inc("scalar", lambda e, o=sc2: e.activation(o[:], o[:], AF.Sigmoid), "se")
                    p.wait("vector", "se", s1)
                    v2 = p.inc("vector", lambda e, o=gst[:, cs], i_=sc2, n=normwb[:, cs]:
                               e.tensor_tensor(o, i_[:], n, op=ALU.mult), "ve")
                    p.release("sc2", "ve")
                sev = p.cnt["se"]
                vev = p.cnt["ve"]
                p.wait("scalar", "se", sev)
                p.wait("scalar", "ve", vev)
                ws = "dwa" if tt_i % 2 == 0 else "dwb"
                p.dma("scalar", ws, gnd[128 * tt_i:128 * (tt_i + 1), :], gnt[:])
                p.dma("scalar", ws, gsd[128 * tt_i:128 * (tt_i + 1), :], gst[:])
                p.release("gn_sp", ws)
                p.rel["f1"] = [("te", p.cnt["te"])]
                p.rel["g1"] = [("te", p.cnt["te"])]
                p.rel["sc1"] = [("se", p.cnt["se"])]
                p.rel["sc2"] = [("ve", p.cnt["ve"])]
                if j == 3:
                    p.release(f"hfmb{bh}", "te")

    p.phase("l0a_sweep1")
    p.barrier(sems=("te", "ve", "se", "dwa", "dwb", "dwc", "dwd"))
    zk = p.inc("vector", lambda e, o=Skb: e.memset(o[:], 0.0), "ve")
    zv2 = p.inc("vector", lambda e, o=Svb: e.memset(o[:], 0.0), "ve")
    with contextlib.ExitStack() as ph:
        S = lambda b, s, d: ph.enter_context(p.nc.sbuf_tensor(p.name(b), s, d))
        ktm2 = [S("ktm", [128, D], BF16) for _ in range(2)]
        vtm2 = [S("vtm", [128, D], BF16) for _ in range(2)]
        wtm2 = [S("wtm", [128, D], BF16) for _ in range(2)]
        gn22 = [S("gn2", [128, D], F32) for _ in range(2)]
        wsq = S("wsq", [128, D], F32)
        sq16 = S("sq16", [128, 16], F32)
        wn = S("wn", [128, D], BF16)
        kp = S("kp", [128, D], BF16)
        vp = S("vp", [128, D], BF16)
        Bt = S("Bt", [128, D], BF16)
        Bh = S("Bh", [128, D], BF16)
        Dt = S("Dt", [128, D], BF16)
        en = S("en", [128, 512], BF16)
        ecrow = S("ecrow", [128, D], F32)
        eccol = S("eccol", [128, 16], F32)
        ctot2 = S("ctot2", [128, D], F32)
        pc_ps = ph.enter_context(p.nc.psum_tensor(p.name("pc"), [128, 512], F32))
        pl_ps = ph.enter_context(p.nc.psum_tensor(p.name("pl"), [128, 512], F32))
        PB = ph.enter_context(p.nc.psum_tensor(p.name("PB"), [128, 512], F32))
        pcol = ph.enter_context(p.nc.psum_tensor(p.name("pcol"), [128, 16], F32))
        psk = ph.enter_context(p.nc.psum_tensor(p.name("psk"), [128, 512], F32))
        psv = ph.enter_context(p.nc.psum_tensor(p.name("psv"), [128, 512], F32))
        zc = p.inc("vector", lambda e, o=ctot2: e.memset(o[:], 0.0), "ve")

        def s1_loads(pr):
            b = pr % 2
            sem = "dta" if b == 0 else "dtb"
            gsem = "dga" if b == 0 else "dgb"
            p.acquire("sync", f"s1buf{b}")
            for kc in range(KC):
                p.dma("sync", sem, ktm2[b][:, 128 * kc:128 * (kc + 1)],
                      kd[kc, :, 128 * pr:128 * pr + 128], transpose=True)
                p.dma("sync", sem, vtm2[b][:, 128 * kc:128 * (kc + 1)],
                      vd[kc, :, 128 * pr:128 * pr + 128], transpose=True)
                p.dma("sync", sem, wtm2[b][:, 128 * kc:128 * (kc + 1)],
                      wd[kc, :, 128 * pr:128 * pr + 128], transpose=True)
            p.acquire("gpsimd", f"s1buf{b}")
            p.dma("gpsimd", gsem, gn22[b][:], gnd[128 * pr:128 * pr + 128, :])
            return p.cnt[sem], p.cnt[gsem]

        pend = s1_loads(0)
        for pr in range(NT):
            tp0 = 128 * pr
            b = pr % 2
            ktm, vtm, wtm, gn2 = ktm2[b], vtm2[b], wtm2[b], gn22[b]
            lv, gv_ = pend
            if pr + 1 < NT:
                pend = s1_loads(pr + 1)
            sem = "dta" if b == 0 else "dtb"
            gsem = "dga" if b == 0 else "dgb"
            for eng in ("tensor", "vector", "scalar"):
                p.wait(eng, sem, lv)
                p.wait(eng, gsem, gv_)
            for hc in range(2):
                ci = 2 * pr + hc
                rs_ = slice(64 * hc, 64 * hc + 64)
                h64 = 64 * hc
                tt_i = pr
                p.acquire("vector", f"sw1sp{hc}")
                p.acquire("scalar", f"sw1sp{hc}")
                wv1 = p.inc("scalar", lambda e, o=wsq[rs_, :], i_=wtm[rs_, :]: e.activation(
                    o, i_, AF.Square), "se")
                p.wait("vector", "se", wv1)
                wv2 = p.inc("vector", lambda e, o=sq16[rs_, :], i_=wsq[rs_, :]: e.tensor_reduce(
                    o, i_.rearrange("p (h j) -> p h j", h=16), axis=AX.X, op=ALU.add), "ve")
                p.wait("scalar", "ve", wv2)
                wv3 = p.inc("scalar", lambda e, o=sq16[rs_, :], c=p.cv[rs_, C_L2EPS:C_L2EPS + 1]:
                    e.activation(o, o, AF.Sqrt, bias=c), "se")
                p.wait("vector", "se", wv3)
                wv4 = p.inc("vector", lambda e, o=sq16[rs_, :]: e.reciprocal(o, o), "ve")
                p.wait("vector", "ve", wv4)
                for h in range(16):
                    hs = slice(128 * h, 128 * (h + 1))
                    p.inc("vector", lambda e, o=wn[rs_, hs], i_=wtm[rs_, hs], s=sq16[rs_, h:h + 1]:
                          e.tensor_scalar(o, i_, s, None, op0=ALU.mult), "ve")
                    p.inc("vector", lambda e, o=kp[rs_, hs], i_=ktm[rs_, hs],
                          s=betat[tt_i][rs_, h:h + 1]: e.tensor_scalar(o, i_, s, None, op0=ALU.mult), "ve")
                    p.inc("vector", lambda e, o=vp[rs_, hs], i_=vtm[rs_, hs],
                          s=betat[tt_i][rs_, h:h + 1]: e.tensor_scalar(o, i_, s, None, op0=ALU.mult), "ve")
                wv5 = p.cnt["ve"]
                p.wait("tensor", "se", p.cnt["se"])
                p.acquire("tensor", "pcol")
                for h in range(16):
                    fn = lambda e, o=pcol[:, h:h + 1], l=gn2[rs_, 128 * h:128 * (h + 1)], \
                        r=p.cv[rs_, C_ONE:C_ONE + 1]: e.matmul(o, l, r, start=True, stop=True)
                    cv_ = p.inc("tensor", fn, "te")
                p.wait("scalar", "te", p.cnt["te"])
                p.acquire("scalar", "eccolbuf")
                ecv = p.inc("scalar", lambda e, o=eccol, i_=pcol: e.activation(
                    o[:], i_[:], AF.Exp, scale=-1.0), "se")
                p.release("pcol", "se")
                p.wait("scalar", "se", ecv)
                p.dma("scalar", "dwa" if ci % 2 == 0 else "dwb", ecc[ci, :, :], eccol[:])
                for q2 in range(4):
                    cs = slice(512 * q2, 512 * (q2 + 1))
                    p.acquire("tensor", "pc_ps")
                    t1 = p.inc("tensor", lambda e, o=pc_ps[rs_, :], l=trix[rs_, 0:64], r=gn2[rs_, cs]:
                               e.matmul(o, l, r, start=True, stop=True), "te")
                    p.acquire("tensor", "pl_ps")
                    t2 = p.inc("tensor", lambda e, o=pl_ps[h64:h64 + 1, :], l=p.cv[rs_, C_ONE:C_ONE + 1],
                               r=gn2[rs_, cs]: e.matmul(o, l, r, start=True, stop=True), "te")
                    p.wait("scalar", "te", t2)
                    s1 = p.inc("scalar", lambda e, o=Dt[rs_, cs], i_=pc_ps[rs_, :]: e.activation(
                        o, i_, AF.Exp, scale=-1.0), "se")
                    p.acquire("scalar", "en")
                    s2 = p.inc("scalar", lambda e, o=en[rs_, :], i_=pc_ps[rs_, :]: e.activation(
                        o, i_, AF.Exp), "se")
                    p.release("pc_ps", "se")
                    s3 = p.inc("scalar", lambda e, o=ecrow[h64:h64 + 1, cs], i_=pl_ps[h64:h64 + 1, :]:
                               e.activation(o, i_, AF.Exp, scale=-1.0), "se")
                    p.wait("vector", "te", t2)
                    p.wait("vector", "ve", wv5)
                    ct1 = p.inc("vector", lambda e, o=ctot2[h64:h64 + 1, cs], i_=pl_ps[h64:h64 + 1, :]:
                                e.tensor_tensor(o, o, i_, op=ALU.add), "ve")
                    p.release("pl_ps", "ve")
                    p.wait("vector", "se", s2)
                    bt1 = p.inc("vector", lambda e, o=Bt[rs_, cs], a=wn[rs_, cs], b=en[rs_, :]:
                                e.tensor_tensor(o, a, b, op=ALU.mult), "ve")
                    p.release("en", "ve")
                    p.wait("tensor", "se", s3)
                    p.acquire("tensor", "PB")
                    t3 = p.inc("tensor", lambda e, o=PB[:], l=onesf2[h64:h64 + 1, 0:128],
                               r=ecrow[h64:h64 + 1, cs]: e.matmul(o[:], l, r, start=True, stop=True), "te")
                    p.wait("vector", "te", t3)
                    p.wait("vector", "ve", bt1)
                    bh1 = p.inc("vector", lambda e, o=Bh[rs_, cs], a=Bt[rs_, cs], b=PB[rs_, :]:
                                e.tensor_tensor(o, b, a, op=ALU.mult), "ve")
                    p.wait("tensor", "ve", bh1)
                    p.acquire("tensor", "psk")
                    p.acquire("tensor", "psv")
                    for hh in range(4):
                        h = 4 * q2 + hh
                        hs = slice(128 * h, 128 * (h + 1))
                        os = slice(128 * hh, 128 * (hh + 1))
                        fn = lambda e, o=psk[:, os], l=kp[rs_, hs], r=Bh[rs_, hs]: \
                            e.matmul(o, l, r, start=True, stop=True)
                        p.inc("tensor", fn, "te")
                        fn = lambda e, o=psv[:, os], l=Bh[rs_, hs], r=vp[rs_, hs]: \
                            e.matmul(o, l, r, start=True, stop=True)
                        p.inc("tensor", fn, "te")
                    tsv = p.cnt["te"]
                    p.wait("vector", "te", tsv)
                    sk1 = p.inc("vector", lambda e, o=Skb[:, cs], b=PB[:]: e.tensor_tensor(
                        o, b, o, op=ALU.mult), "ve")
                    p.release("PB", "ve")
                    p.wait("vector", "ve", sk1)
                    sk2 = p.inc("vector", lambda e, o=Skb[:, cs], b=psk[:]: e.tensor_tensor(
                        o, b, o, op=ALU.add), "ve")
                    p.wait("vector", "ve", sk2)
                    p.release("psk", "ve")
                    for hh in range(4):
                        h = 4 * q2 + hh
                        hs = slice(128 * h, 128 * (h + 1))
                        os = slice(128 * hh, 128 * (hh + 1))
                        vv_ = p.inc("vector", lambda e, o=Svb[:, hs], s=eccol[:, h:h + 1],
                                    b=psv[:, os]: e.scalar_tensor_tensor(
                                        o, o, s, b, op0=ALU.mult, op1=ALU.add), "ve")
                        p.wait("vector", "ve", vv_)
                    p.release("psv", "ve")
                    p.release_add("eccolbuf", "ve")
                vv_all = p.cnt["ve"]
                se_all = p.cnt["se"]
                p.wait("scalar", "ve", vv_all)
                p.wait("scalar", "se", se_all)
                ws = "dwa" if ci % 2 == 0 else "dwb"
                p.dma("scalar", ws, Btd[64 * ci:64 * ci + 64, :], Bt[rs_, :])
                p.dma("scalar", ws, Bhd[64 * ci:64 * ci + 64, :], Bh[rs_, :])
                p.dma("scalar", ws, Kpd[64 * ci:64 * ci + 64, :], kp[rs_, :])
                p.dma("scalar", ws, Vpd[64 * ci:64 * ci + 64, :], vp[rs_, :])
                p.dma("scalar", ws, Dtd[64 * ci:64 * ci + 64, :], Dt[rs_, :])
                p.dma("scalar", ws, ecd[ci:ci + 1, :], ecrow[h64:h64 + 1, :])
                p.release(f"sw1sp{hc}", ws)
                p.release("eccolbuf", ws)
                p.release_add("eccolbuf", "ve")
                if hc == 1:
                    p.release(f"s1buf{b}", "te")
                    p.release_add(f"s1buf{b}", "ve")
                    p.release_add(f"s1buf{b}", "se")
        p.wait("tensor", "ve", p.cnt["ve"])
        for q2 in range(4):
            cs = slice(512 * q2, 512 * (q2 + 1))
            p.acquire("tensor", "PB")
            tv = p.inc("tensor", lambda e, o=PB[0:1, :], l=onesf2[:, 0:1], r=ctot2[:, cs]:
                       e.matmul(o, l, r, start=True, stop=True), "te")
            p.wait("vector", "te", tv)
            cv_ = p.inc("vector", lambda e, o=ctot[0:1, cs], i_=PB[0:1, :]: e.tensor_copy(
                o, i_), "ve")
            p.release("PB", "ve")
            p.wait("vector", "ve", cv_)

    p.phase("l0a_gather")
    p.barrier(sems=("te", "ve", "se", "dwa", "dwb", "dts"))
    p.wait("gpsimd", "ve", p.cnt["ve"])
    p.dma("gpsimd", "dcc", agin[:, 0:D], Skb[:])
    p.dma("gpsimd", "dcc", agin[:, D:2 * D], Svb[:])
    agv = p.dma("gpsimd", "dcc", ctin, ctot[:])
    p.wait("gpsimd", "dcc", p.cnt["dcc"])
    p.inc("gpsimd", lambda e, i_=agin, o_=agout, rg=replica_groups: e.collective_compute(
        "AllGather", ALU.bypass, replica_groups=rg, ins=[i_.opt()], outs=[o_.opt()]), "dcl", 1)
    p.inc("gpsimd", lambda e, i_=ctin, o_=ctout, rg=replica_groups: e.collective_compute(
        "AllGather", ALU.bypass, replica_groups=rg, ins=[i_.opt()], outs=[o_.opt()]), "dcl", 1)
    p.wait("gpsimd", "dcl", p.cnt["dcl"])
    with contextlib.ExitStack() as ph:
        S = lambda b, s, d: ph.enter_context(p.nc.sbuf_tensor(p.name(b), s, d))
        Lsb = S("Lsb", [128, 2 * D], BF16)
        Lms = S("Lms", [128, 512], BF16)
        ctj = S("ctj", [1, D], F32)
        erj = S("erj", [1, D], F32)
        ecjc = S("ecjc", [128, 16], F32)
        Sf = S("Sf", [128, 2 * D], F32)
        PE2 = ph.enter_context(p.nc.psum_tensor(p.name("PE2"), [128, 512], F32))
        pco2 = ph.enter_context(p.nc.psum_tensor(p.name("pco2"), [128, 16], F32))
        zf = p.inc("vector", lambda e, o=Sf: e.memset(o[:], 0.0), "ve")
        p.wait("vector", "ve", zf)
        for j in range(nworld - 1):
            p.acquire("gpsimd", "Lsb")
            lj = p.dma("gpsimd", "dcc", Lsb[:], agout[128 * j:128 * (j + 1), :])
            cj = p.dma("gpsimd", "dcc", ctj[:], ctout[j:j + 1, :])
            p.wait("vector", "dcc", p.cnt["dcc"])
            p.wait("scalar", "dcc", p.cnt["dcc"])
            m1 = p.inc("vector", lambda e, o=ctj, s=cmask2d[0:1, j:j + 1]: e.tensor_scalar(
                o[:], o[:], s, None, op0=ALU.mult), "ve")
            p.wait("scalar", "ve", m1)
            m2 = p.inc("scalar", lambda e, o=erj, i_=ctj: e.activation(
                o[:], i_[:], AF.Exp, scale=-1.0), "se")
            p.wait("tensor", "se", m2)
            for h in range(16):
                tv = p.inc("tensor", lambda e, o=pco2[:, h:h + 1], i_=erj[0:1, 128 * h:128 * (h + 1)],
                           id_=p.cv[0:1, C_ONE:C_ONE + 1]: e.transpose(o, i_, id_), "te")
            p.wait("vector", "te", p.cnt["te"])
            m3 = p.inc("vector", lambda e, o=ecjc, i_=pco2: e.tensor_copy(o[:], i_[:]), "ve")
            p.wait("vector", "ve", m3)
            for q2 in range(4):
                cs = slice(512 * q2, 512 * (q2 + 1))
                p.acquire("tensor", "PE2")
                tv = p.inc("tensor", lambda e, o=PE2[:], l=onesf2[0:1, 0:128], r=erj[0:1, cs]:
                           e.matmul(o[:], l, r, start=True, stop=True), "te")
                p.wait("vector", "te", tv)
                f1_ = p.inc("vector", lambda e, o=Sf[:, cs], b=PE2[:]: e.tensor_tensor(
                    o, o, b, op=ALU.mult), "ve")
                p.release("PE2", "ve")
                p.wait("vector", "ve", f1_)
                f2a = p.inc("vector", lambda e, o=Lms, a=Lsb[:, cs], s=cmask2d[:, j:j + 1]:
                            e.tensor_scalar(o[:], a, s, None, op0=ALU.mult), "ve")
                p.wait("vector", "ve", f2a)
                f2_ = p.inc("vector", lambda e, o=Sf[:, cs], b=Lms: e.tensor_tensor(
                    o, o, b[:], op=ALU.add), "ve")
                p.wait("vector", "ve", f2_)
            for h in range(16):
                hs = slice(D + 128 * h, D + 128 * (h + 1))
                f3 = p.inc("vector", lambda e, o=Sf[:, hs], s=ecjc[:, h:h + 1]: e.tensor_scalar(
                    o, o, s, None, op0=ALU.mult), "ve")
                p.wait("vector", "ve", f3)
                f4a = p.inc("vector", lambda e, o=Lms[:, 0:128], a=Lsb[:, hs], s=cmask2d[:, j:j + 1]:
                           e.tensor_scalar(o, a, s, None, op0=ALU.mult), "ve")
                p.wait("vector", "ve", f4a)
                f4 = p.inc("vector", lambda e, o=Sf[:, hs], b=Lms[:, 0:128]: e.tensor_tensor(
                    o, o, b, op=ALU.add), "ve")
                p.wait("vector", "ve", f4)
            p.release("Lsb", "ve")
        p.wait("vector", "dcc", agv)
        for q2 in range(4):
            cs = slice(512 * q2, 512 * (q2 + 1))
            cs2 = slice(D + 512 * q2, D + 512 * (q2 + 1))
            p.inc("vector", lambda e, o=Skb[:, cs], i_=Sf[:, cs]: e.tensor_copy(o, i_), "ve")
            p.inc("vector", lambda e, o=Svb[:, cs], i_=Sf[:, cs2]: e.tensor_copy(o, i_), "ve")

    p.phase("l0a_sweep2")
    p.barrier(sems=("te", "ve", "se"))
    with contextlib.ExitStack() as ph:
        S = lambda b, s, d: ph.enter_context(p.nc.sbuf_tensor(p.name(b), s, d))
        qfm2 = [S("qfm", [128, KC * 128], BF16) for _ in range(2)]
        kfm2 = [S("kfm", [128, KC * 128], BF16) for _ in range(2)]
        Bt22 = [S("Bt2", [128, D], BF16) for _ in range(2)]
        Bh22 = [S("Bh2", [128, D], BF16) for _ in range(2)]
        Kp22 = [S("Kp2", [128, D], BF16) for _ in range(2)]
        Vp22 = [S("Vp2", [128, D], BF16) for _ in range(2)]
        Dt22 = [S("Dt2", [128, D], BF16) for _ in range(2)]
        gs22 = [S("gs2", [128, D], BF16) for _ in range(2)]
        ecr22 = [S("ecr2", [128, D], F32) for _ in range(2)]
        ecc22 = [S("ecc2", [128, 2 * 16], F32) for _ in range(2)]
        amall = S("amall", [128, 1024], BF16)
        cmall = S("cmall", [128, 1024], BF16)
        amball = S("amball", [128, D], BF16)
        atfall = S("atfall", [128, 1024], BF16)
        btfall = S("btfall", [128, 1024], BF16)
        oraw = S("oraw", [128, D], F32)
        ssb = S("ssb", [128, D], F32)
        esb = S("esb", [128, D], F32)
        nmx = S("nmx", [128, 16], F32)
        rsm = S("rsm", [128, 16], F32)
        orr = S("orr", [128, 16], F32)
        ebst = S("ebst", [128, D], BF16)
        otm = S("otm", [128, D], BF16)
        pX = [ph.enter_context(p.nc.psum_tensor(p.name("pX"), [128, 128], F32)) for _ in range(2)]
        pS = ph.enter_context(p.nc.psum_tensor(p.name("pS"), [128, D], F32))
        ptr = [ph.enter_context(p.nc.psum_tensor(p.name("ptr"), [128, 64], BF16)) for _ in range(2)]

        def s2_loads(pr):
            b = pr % 2
            tp0 = 128 * pr
            sem = "dta" if b == 0 else "dtb"
            ssem = "dsa" if b == 0 else "dsb"
            gsem = "dga" if b == 0 else "dgb"
            p.acquire("sync", f"s2buf{b}")
            for kc in range(KC):
                p.dma("sync", sem, qfm2[b][:, 128 * kc:128 * (kc + 1)], qd[kc, :, tp0:tp0 + 128])
                p.dma("sync", sem, kfm2[b][:, 128 * kc:128 * (kc + 1)], kd[kc, :, tp0:tp0 + 128])
            p.acquire("scalar", f"s2buf{b}")
            p.dma("scalar", ssem, Bt22[b][:], Btd[tp0:tp0 + 128, :])
            p.dma("scalar", ssem, Bh22[b][:], Bhd[tp0:tp0 + 128, :])
            p.dma("scalar", ssem, Kp22[b][:], Kpd[tp0:tp0 + 128, :])
            p.acquire("gpsimd", f"s2buf{b}")
            p.dma("gpsimd", gsem, Vp22[b][:], Vpd[tp0:tp0 + 128, :])
            p.dma("gpsimd", gsem, Dt22[b][:], Dtd[tp0:tp0 + 128, :])
            p.dma("gpsimd", gsem, gs22[b][:], gsd[tp0:tp0 + 128, :])
            p.dma("gpsimd", gsem, ecr22[b][0:1, :], ecd[2 * pr:2 * pr + 1, :])
            p.dma("gpsimd", gsem, ecr22[b][64:65, :], ecd[2 * pr + 1:2 * pr + 2, :])
            p.dma("gpsimd", gsem, ecc22[b][:, 0:16], ecc[2 * pr, :, :])
            p.dma("gpsimd", gsem, ecc22[b][:, 16:32], ecc[2 * pr + 1, :, :])
            return p.cnt[sem], p.cnt[ssem], p.cnt[gsem]

        pend2 = s2_loads(0)
        for pr in range(NT):
            tp0 = 128 * pr
            b = pr % 2
            qfm, kfm = qfm2[b], kfm2[b]
            Bt2, Bh2, Kp2, Vp2 = Bt22[b], Bh22[b], Kp22[b], Vp22[b]
            Dt2, gs2, ecr2, ecc2 = Dt22[b], gs22[b], ecr22[b], ecc22[b]
            lv, sv_, gv_ = pend2
            if pr + 1 < NT:
                pend2 = s2_loads(pr + 1)
            sem = "dta" if b == 0 else "dtb"
            ssem = "dsa" if b == 0 else "dsb"
            gsem = "dga" if b == 0 else "dgb"
            for eng in ("tensor", "vector", "scalar"):
                p.wait(eng, sem, lv)
                p.wait(eng, ssem, sv_)
                p.wait(eng, gsem, gv_)
            p.acquire("vector", "otm_sp")
            for hc in range(2):
                ci = 2 * pr + hc
                rs_ = slice(64 * hc, 64 * hc + 64)
                h64 = 64 * hc
                tcol = slice(64 * hc, 64 * hc + 64)
                p.acquire("tensor", "pS_full")
                p.acquire("vector", "Am_all")
                for h in range(16):
                    bx = h % 2
                    p.acquire("tensor", f"pX{bx}")
                    t1 = p.inc("tensor", lambda e, o=pX[bx][rs_, 0:64], l=kfm[:, 128 * h:128 * h + 128][:, tcol],
                               r=qfm[:, 128 * h:128 * h + 128][:, tcol]: e.matmul(
                                   o, l, r, start=True, stop=True), "te")
                    p.wait("vector", "te", t1)
                    p.inc("vector", lambda e, o=amall[rs_, 64 * h:64 * (h + 1)], i_=pX[bx][rs_, 0:64],
                          s=betat[pr][rs_, h:h + 1], m=masksu[rs_, :]: e.scalar_tensor_tensor(
                              o, i_, s, m, op0=ALU.mult, op1=ALU.mult), "ve")
                    p.release(f"pX{bx}", "ve")
                amv = p.cnt["ve"]
                p.wait("tensor", "ve", amv)
                for h in range(16):
                    hs = slice(128 * h, 128 * (h + 1))
                    p.inc("tensor", lambda e, o=pS[rs_, hs], l=qfm[:, 128 * h:128 * h + 128][:, tcol],
                          r=Skb[:, hs]: e.matmul(o, l, r, start=True, stop=False), "te")
                    p.inc("tensor", lambda e, o=pS[rs_, hs], l=amall[rs_, 64 * h:64 * (h + 1)],
                          r=Bt2[rs_, hs]: e.matmul(o, l, r, start=False, stop=True), "te")
                tS = p.cnt["te"]
                p.rel["Am_all"] = [("te", tS)]
                p.wait("vector", "te", tS)
                sv1 = p.inc("vector", lambda e, o=ssb[rs_, :], i_=pS[rs_, :], d=Dt2[rs_, :]:
                            e.scalar_tensor_tensor(o, i_, SCL, d, op0=ALU.mult, op1=ALU.mult), "ve")
                p.release("pS_full", "ve")
                p.wait("vector", "ve", sv1)
                sv2 = p.inc("vector", lambda e, o=nmx[rs_, :], i_=ssb[rs_, :]: e.tensor_reduce(
                    o, i_.rearrange("p (h j) -> p h j", h=16), axis=AX.X, op=ALU.max, negate=True), "ve")
                p.wait("vector", "ve", sv2)
                for h in range(16):
                    hs = slice(128 * h, 128 * (h + 1))
                    p.inc("vector", lambda e, o=esb[rs_, hs], i_=ssb[rs_, hs], s=nmx[rs_, h:h + 1]:
                          e.tensor_scalar(o, i_, s, None, op0=ALU.add), "ve")
                sv3 = p.cnt["ve"]
                p.wait("scalar", "ve", sv3)
                sv4 = p.inc("scalar", lambda e, o=esb[rs_, :]: e.activation(o, o, AF.Exp), "se")
                p.wait("vector", "se", sv4)
                sv5 = p.inc("vector", lambda e, o=rsm[rs_, :], i_=esb[rs_, :]: e.tensor_reduce(
                    o, i_.rearrange("p (h j) -> p h j", h=16), axis=AX.X, op=ALU.add), "ve")
                p.wait("vector", "ve", sv5)
                sv6 = p.inc("vector", lambda e, o=rsm[rs_, :]: e.reciprocal(o, o), "ve")
                p.wait("vector", "ve", sv6)
                p.acquire("vector", "amball")
                for h in range(16):
                    hs = slice(128 * h, 128 * (h + 1))
                    p.inc("vector", lambda e, o=amball[rs_, hs], i_=esb[rs_, hs],
                          s=rsm[rs_, h:h + 1], d=Dt2[rs_, hs]: e.scalar_tensor_tensor(
                              o, i_, s, d, op0=ALU.mult, op1=ALU.mult), "ve")
                c1v = p.cnt["ve"]
                p.wait("tensor", "ve", c1v)
                for h in range(16):
                    hs = slice(128 * h, 128 * (h + 1))
                    bx = h % 2
                    p.acquire("tensor", f"ptr{bx}")
                    tva = p.inc("tensor", lambda e, o=ptr[bx][:, 0:64], i_=amball[rs_, hs],
                                id_=p.ident[rs_, rs_]: e.transpose(o, i_, id_), "te")
                    p.wait("vector", "te", tva)
                    p.inc("vector", lambda e, o=atfall[:, 64 * h:64 * (h + 1)], i_=ptr[bx][:, 0:64]:
                          e.tensor_copy(o, i_), "ve")
                    p.release(f"ptr{bx}", "ve")
                p.rel["amball"] = [("te", p.cnt["te"])]
                for h in range(16):
                    hs = slice(128 * h, 128 * (h + 1))
                    bx = h % 2
                    p.acquire("tensor", f"ptr{bx}")
                    tvb = p.inc("tensor", lambda e, o=ptr[bx][:, 0:64], i_=Bt2[rs_, hs],
                                id_=p.ident[rs_, rs_]: e.transpose(o, i_, id_), "te")
                    p.wait("vector", "te", tvb)
                    p.inc("vector", lambda e, o=btfall[:, 64 * h:64 * (h + 1)], i_=ptr[bx][:, 0:64]:
                          e.tensor_copy(o, i_), "ve")
                    p.release(f"ptr{bx}", "ve")
                tfv = p.cnt["ve"]
                p.wait("tensor", "ve", tfv)
                p.acquire("vector", "cmall")
                for h in range(16):
                    bx = h % 2
                    p.acquire("tensor", f"pX{bx}")
                    tv = p.inc("tensor", lambda e, o=pX[bx][rs_, 0:64], l=btfall[:, 64 * h:64 * (h + 1)],
                               r=atfall[:, 64 * h:64 * (h + 1)]: e.matmul(o, l, r, start=True, stop=True), "te")
                    p.wait("vector", "te", tv)
                    p.inc("vector", lambda e, o=cmall[rs_, 64 * h:64 * (h + 1)], i_=pX[bx][rs_, 0:64],
                          m=masksu[rs_, :]: e.tensor_tensor(o, i_, m, op=ALU.mult), "ve")
                    p.release(f"pX{bx}", "ve")
                cmv = p.cnt["ve"]
                p.wait("tensor", "ve", cmv)
                p.acquire("vector", "orawall")
                for h in range(16):
                    hs = slice(128 * h, 128 * (h + 1))
                    bx = h % 2
                    p.acquire("tensor", f"pX{bx}")
                    p.inc("tensor", lambda e, o=pX[bx][rs_, :], l=atfall[:, 64 * h:64 * (h + 1)],
                          r=Svb[:, hs]: e.matmul(o, l, r, start=True, stop=False), "te")
                    to = p.inc("tensor", lambda e, o=pX[bx][rs_, :], l=cmall[rs_, 64 * h:64 * (h + 1)],
                               r=Vp2[rs_, hs]: e.matmul(o, l, r, start=False, stop=True), "te")
                    p.wait("vector", "te", to)
                    p.inc("vector", lambda e, o=oraw[rs_, hs], i_=pX[bx][rs_, :]: e.tensor_copy(
                        o, i_), "ve")
                    p.release(f"pX{bx}", "ve")
                orv = p.cnt["ve"]
                p.rel["cmall"] = [("te", p.cnt["te"])]
                p.rel["atfall"] = [("te", p.cnt["te"])]
                p.rel["btfall"] = [("te", p.cnt["te"])]
                p.wait("scalar", "ve", orv)
                sq = p.inc("scalar", lambda e, o=ssb[rs_, :], i_=oraw[rs_, :]: e.activation(
                    o, i_, AF.Square), "se")
                p.wait("vector", "se", sq)
                rv_ = p.inc("vector", lambda e, o=orr[rs_, :], i_=ssb[rs_, :]: e.tensor_reduce(
                    o, i_.rearrange("p (h j) -> p h j", h=16), axis=AX.X, op=ALU.add), "ve")
                p.wait("scalar", "ve", rv_)
                o3 = p.inc("scalar", lambda e, o=orr[rs_, :], c=p.cv[rs_, C_LNEPS:C_LNEPS + 1]:
                    e.activation(o, o, AF.Sqrt, scale=1.0 / 128.0, bias=c), "se")
                p.wait("vector", "se", o3)
                o4 = p.inc("vector", lambda e, o=orr[rs_, :]: e.reciprocal(o, o), "ve")
                p.wait("vector", "ve", o4)
                for h in range(16):
                    hs = slice(128 * h, 128 * (h + 1))
                    p.inc("vector", lambda e, o=otm[rs_, hs], i_=oraw[rs_, hs], s=orr[rs_, h:h + 1],
                          g=gs2[rs_, hs]: e.scalar_tensor_tensor(
                              o, i_, s, g, op0=ALU.mult, op1=ALU.mult), "ve")
                p.rel["orawall"] = [("ve", p.cnt["ve"])]
                p.acquire("tensor", "pS_full")
                p.acquire("vector", "ebst")
                for q2 in range(4):
                    cs = slice(512 * q2, 512 * (q2 + 1))
                    tv = p.inc("tensor", lambda e, o=pS[:, cs],
                               l=onesf2[h64:h64 + 1, 0:128], r=ecr2[h64:h64 + 1, cs]:
                               e.matmul(o, l, r, start=True, stop=True), "te")
                    p.wait("vector", "te", tv)
                    p.inc("vector", lambda e, o=ebst[:, cs], i_=pS[:, cs]: e.tensor_copy(o, i_), "ve")
                ebv = p.cnt["ve"]
                p.release("pS_full", "ve")
                p.wait("vector", "ve", ebv)
                for h in range(16):
                    hs = slice(128 * h, 128 * (h + 1))
                    p.inc("vector", lambda e, o=Skb[:, hs], b=ebst[:, hs]: e.tensor_tensor(
                        o, b, o, op=ALU.mult), "ve")
                dkv = p.cnt["ve"]
                p.rel["ebst"] = [("ve", dkv)]
                p.wait("tensor", "ve", dkv)
                for h in range(16):
                    hs = slice(128 * h, 128 * (h + 1))
                    bx = h % 2
                    p.acquire("tensor", f"pX{bx}")
                    tk = p.inc("tensor", lambda e, o=pX[bx][:, :], l=Kp2[rs_, hs], r=Bh2[rs_, hs]:
                               e.matmul(o, l, r, start=True, stop=True), "te")
                    p.wait("vector", "te", tk)
                    p.inc("vector", lambda e, o=Skb[:, hs], b=pX[bx][:, :]: e.tensor_tensor(
                        o, b, o, op=ALU.add), "ve")
                    p.release(f"pX{bx}", "ve")
                skv = p.cnt["ve"]
                p.wait("tensor", "ve", skv)
                for h in range(16):
                    hs = slice(128 * h, 128 * (h + 1))
                    bx = h % 2
                    p.acquire("tensor", f"pX{bx}")
                    tv2 = p.inc("tensor", lambda e, o=pX[bx][:, :], l=Bh2[rs_, hs], r=Vp2[rs_, hs]:
                                e.matmul(o, l, r, start=True, stop=True), "te")
                    p.wait("vector", "te", tv2)
                    p.inc("vector", lambda e, o=Svb[:, hs], s=ecc2[:, 16 * hc + h:16 * hc + h + 1],
                          b=pX[bx][:, :]: e.scalar_tensor_tensor(
                              o, o, s, b, op0=ALU.mult, op1=ALU.add), "ve")
                    p.release(f"pX{bx}", "ve")
                p.wait("vector", "ve", p.cnt["ve"])
            ov = p.cnt["ve"]
            p.wait("scalar", "ve", ov)
            ws = "dwa" if pr % 2 == 0 else "dwb"
            p.dma("scalar", ws, otd[tp0:tp0 + 128, :], otm[:])
            p.release("otm_sp", ws)
            p.release(f"s2buf{b}", "te")
            p.release_add(f"s2buf{b}", "ve")
            p.release_add(f"s2buf{b}", "se")

    p.phase("l0a_wo")
    p.barrier(sems=("te", "ve", "se", "dwa", "dwb", "dts", "dgt"))
    dfm_d = p.dram("l0dfm", [KC * 128, R], BF16)
    with contextlib.ExitStack() as ph:
        S = lambda b, s, d: ph.enter_context(p.nc.sbuf_tensor(p.name(b), s, d))
        ofm = [S("ofm", [128, 512], BF16) for _ in range(KC)]
        wcol = [S("wcolo", [128, KC, 128], BF16) for _ in range(4)]
        dd = [S("l0dd", [128, 512], BF16) for _ in range(2)]
        pw = [ph.enter_context(p.nc.psum_tensor(p.name("pwo"), [128, 512], F32)) for _ in range(2)]
        for q in range(4):
            t0 = 512 * q
            p.wait("sync", "dwa", p.cnt["dwa"])
            p.wait("sync", "dwb", p.cnt["dwb"])
            p.wait("sync", "te", p.cnt["te"])
            for kc in range(KC):
                p.dma("sync", "dtot", ofm[kc][:], otd[t0:t0 + 512, 128 * kc:128 * (kc + 1)],
                      transpose=True)
            hv = p.cnt["dtot"]
            p.wait("tensor", "dtot", hv)
            WS4o = ["dc", "dd", "dc2", "dd2"]

            def wo_load(oc):
                bw = oc % 4
                p.acquire("gpsimd", f"wco{bw}")
                return p.dma("gpsimd", WS4o[bw], wcol[bw][:],
                             Wo[:, 128 * oc:128 * (oc + 1)].rearrange("(kc p) j -> p kc j", p=128))

            pqo_ = [wo_load(oc) for oc in range(3)]
            for oc in range(KC):
                bw = oc % 4
                wv_ = pqo_[oc]
                if oc + 3 < KC:
                    pqo_.append(wo_load(oc + 3))
                p.wait("tensor", WS4o[bw], wv_)
                b2 = oc % 2
                p.acquire("tensor", f"pwo{b2}")
                tv = p.mm_group(pw[b2][:], [(wcol[bw][:, kc, :], ofm[kc][:]) for kc in range(KC)])
                p.release(f"wco{bw}", "te")
                p.wait("scalar", "te", tv)
                p.acquire("scalar", f"l0dd{b2}")
                sv = p.inc("scalar", lambda e, o=dd[b2], i_=pw[b2]: e.activation(
                    o[:], i_[:], AF.Copy), "se")
                p.release(f"pwo{b2}", "se")
                p.wait("scalar", "se", sv)
                wsd = "dwa" if b2 == 0 else "dwb"
                p.dma("scalar", wsd, dfm_d[128 * oc:128 * (oc + 1), t0:t0 + 512], dd[b2][:])
                p.release(f"l0dd{b2}", wsd)

    rph = contextlib.ExitStack()
    xr = [rph.enter_context(p.nc.sbuf_tensor(p.name("xr0"), [128, D], F32)) for _ in range(2)]
    xo = [rph.enter_context(p.nc.sbuf_tensor(p.name("xo0"), [128, D], F32)) for _ in range(2)]
    dtt = [rph.enter_context(p.nc.sbuf_tensor(p.name("dtt0"), [128, D], BF16)) for _ in range(2)]
    p.wait("sync", "dwa", p.cnt["dwa"])
    p.wait("sync", "dwb", p.cnt["dwb"])
    for tt in range(NT):
        b = tt % 2
        xs = "da" if b == 0 else "db"
        ds_ = "dta" if b == 0 else "dtb"
        ws = "dwa" if b == 0 else "dwb"
        p.acquire("sync", f"r0x{b}")
        lv = p.dma("sync", xs, xr[b][:], xin[128 * tt:128 * (tt + 1), :])
        p.dma("sync", ds_, dtt[b][:], dfm_d[:, 128 * tt:128 * (tt + 1)], transpose=True)
        dv = p.cnt[ds_]
        p.wait("vector", xs, lv)
        p.wait("vector", ds_, dv)
        p.acquire("vector", f"r0o{b}")
        p.inc("vector", lambda e, o=xo[b], a=xr[b], dl=dtt[b]: e.tensor_tensor(
            o[:], a[:], dl[:], op=ALU.add), "ve")
        p.release(f"r0x{b}", "ve")
        p.wait("scalar", "ve", p.cnt["ve"])
        p.dma("scalar", ws, xout[128 * tt:128 * (tt + 1), :], xo[b][:])
        p.release(f"r0o{b}", ws)
    rph.close()




def emit_ffn(p, *a, **kw):
    with p.block_scope():
        _emit_ffn_impl(p, *a, **kw)


def emit_l1_attn(p, *a, **kw):
    with p.block_scope():
        _emit_l1_attn_impl(p, *a, **kw)


def emit_l0_attn(p, *a, **kw):
    with p.block_scope():
        _emit_l0_attn_impl(p, *a, **kw)


def finish(p):
    for s in list(p.cnt):
        if p.cnt[s] > 0:
            p.wait("sync", s, p.cnt[s])



def host_consts():
    return np.eye(128, dtype=ml_dtypes.bfloat16)


def host_ones():
    return np.ones((128, 128), dtype=ml_dtypes.bfloat16)


def host_cvec():
    cv = np.zeros((128, NCONST), np.float32)
    cv[:, C_LNEPS] = 1e-5
    cv[:, C_L2EPS] = 1e-6
    cv[:, C_ONE] = 1.0
    cv[:, C_SCALE] = 128.0 ** -0.5
    return cv


def host_pswap():
    P = np.zeros((128, 128), np.float32)
    for i in range(64):
        P[2 * i + 1, 2 * i] = -1.0
        P[2 * i, 2 * i + 1] = 1.0
    return P.astype(ml_dtypes.bfloat16)


def host_rope_tables(core_id):
    rot = 1024
    inv = 1.0 / (10000.0 ** (np.arange(0, rot, 2, dtype=np.float64) / rot))
    pos = np.arange(core_id * R, (core_id + 1) * R, dtype=np.float64)
    fr = pos[:, None] * inv[None, :]
    fr = np.repeat(fr, 2, axis=-1)
    cos = np.cos(fr).T.astype(ml_dtypes.bfloat16)
    sin = np.sin(fr).T.astype(ml_dtypes.bfloat16)
    return np.ascontiguousarray(cos), np.ascontiguousarray(sin)


def host_onesf():
    return np.ones((1, 128), np.float32)


def bcast_row(v):
    return np.ascontiguousarray(np.broadcast_to(np.asarray(v, np.float32), (128, v.shape[0])))




def host_trix():
    t = np.zeros((128, 64), np.float32)
    for s in range(64):
        t[s, s:] = 1.0
        t[64 + s, s:] = 1.0
    return t


def host_masksu():
    return host_trix().astype(ml_dtypes.bfloat16)


def host_onesf2():
    return np.ones((128, 128), np.float32)


def host_conv2d(c):
    return np.ascontiguousarray(c.reshape(KC, 128, 4).transpose(1, 0, 2).reshape(128, KC * 4))


def host_cmask(core_id):
    m = np.zeros((128, 8), np.float32)
    m[:, :core_id] = 1.0
    return m


def scale_2d(s):
    c = s.shape[0] // 128
    return np.ascontiguousarray(s.reshape(c, 128).T)



_GROUPS = [list(range(NCORES))]


def _ein(nc, n, sh, dt=F32):
    return nc.dram_tensor(n, sh, dt, kind="ExternalInput").ap()


def _build_l0attn_prog():
    p = P()
    nc = p.nc
    xin = _ein(nc, "xin", [R, D])
    xhalo = _ein(nc, "xhalo", [3, D])
    Wq, Wk, Wv, Ww = (_ein(nc, n, [D, D]) for n in ("Wq", "Wk", "Wv", "Ww"))
    Wf1, Wf2 = _ein(nc, "Wf1", [D, 128]), _ein(nc, "Wf2", [128, D])
    Wb = _ein(nc, "Wb", [D, 16])
    Wg1, Wg2 = _ein(nc, "Wg1", [D, 128]), _ein(nc, "Wg2", [128, D])
    Wo = _ein(nc, "Wo", [D, D])
    cqd, ckd, cvd = (_ein(nc, n, [128, 64]) for n in ("cqd", "ckd", "cvd"))
    bg2d, nwd = _ein(nc, "bg2d", [128, D]), _ein(nc, "nwd", [128, D])
    trixd = _ein(nc, "trixd", [128, 64])
    maskd = _ein(nc, "maskd", [128, 64], BF16)
    cmd = _ein(nc, "cmd", [128, 8])
    onf2 = _ein(nc, "onf2", [128, 128])
    idc = _ein(nc, "idc", [128, 128], BF16)
    onc = _ein(nc, "onc", [128, 128], BF16)
    cvec = _ein(nc, "cvec", [128, NCONST])
    xout = nc.dram_tensor("xout", [R, D], F32, kind="ExternalOutput").ap()
    emit_setup(p, cvec, idc, onc)
    cq2 = p.const2sb(cqd, [128, 64], F32, "cq2")
    ck2 = p.const2sb(ckd, [128, 64], F32, "ck2")
    cv2 = p.const2sb(cvd, [128, 64], F32, "cv2")
    bg2b = p.const2sb(bg2d, [128, D], F32, "bg2b")
    nwb = p.const2sb(nwd, [128, D], F32, "nwb")
    trix = p.const2sb(trixd, [128, 64], F32, "trix")
    masksu = p.const2sb(maskd, [128, 64], BF16, "masksu")
    cmask = p.const2sb(cmd, [128, 8], F32, "cmask")
    onesf2 = p.const2sb(onf2, [128, 128], F32, "onesf2")
    emit_l0_attn(p, xin, xhalo, xout, Wq, Wk, Wv, Ww, Wf1, Wf2, Wb, Wg1, Wg2, Wo,
                 cq2, ck2, cv2, bg2b, nwb, trix, masksu, cmask, onesf2, _GROUPS)
    finish(p)
    return p.build()


def _build_ffn_prog(int_weights):
    p = P()
    nc = p.nc
    wdt = I32 if int_weights else F32
    xin = _ein(nc, "xin", [R, D])
    gwT = _ein(nc, "gwT", [D, FFN_I], wdt)
    uwT = _ein(nc, "uwT", [D, FFN_I], wdt)
    dwT = _ein(nc, "dwT", [FFN_I, D], wdt)
    gs = _ein(nc, "gs", [128, NIC])
    us = _ein(nc, "us", [128, NIC])
    ds = _ein(nc, "ds", [128, KC])
    idc = _ein(nc, "idc", [128, 128], BF16)
    onc = _ein(nc, "onc", [128, 128], BF16)
    cvec = _ein(nc, "cvec", [128, NCONST])
    xout = nc.dram_tensor("xout", [R, D], F32, kind="ExternalOutput").ap()
    emit_setup(p, cvec, idc, onc)
    gsc = p.const2sb(gs, [128, NIC], F32, "gsc")
    usc = p.const2sb(us, [128, NIC], F32, "usc")
    dsc = p.const2sb(ds, [128, KC], F32, "dsc")
    emit_ffn(p, xin, xout, gwT, uwT, dwT, gsc, usc, dsc, zero_point=int_weights)
    finish(p)
    return p.build()


def _build_l1attn_prog():
    p = P()
    nc = p.nc
    xin = _ein(nc, "xin", [R, D])
    Win = _ein(nc, "Win", [D, 3 * D])
    Wout = _ein(nc, "Wout", [D, D])
    binc_d = _ein(nc, "binc", [128, 48])
    boutc_d = _ein(nc, "boutc", [128, KC])
    lnwb_d = _ein(nc, "lnwb", [128, D])
    lnbb_d = _ein(nc, "lnbb", [128, D])
    cost_d = _ein(nc, "cost", [1024, R], BF16)
    sint_d = _ein(nc, "sint", [1024, R], BF16)
    idc = _ein(nc, "idc", [128, 128], BF16)
    onc = _ein(nc, "onc", [128, 128], BF16)
    psw = _ein(nc, "psw", [128, 128], BF16)
    onf = _ein(nc, "onf", [1, 128])
    cvec = _ein(nc, "cvec", [128, NCONST])
    xout = nc.dram_tensor("xout", [R, D], F32, kind="ExternalOutput").ap()
    emit_setup(p, cvec, idc, onc, psw, onf)
    binc = p.const2sb(binc_d, [128, 48], F32, "binc")
    boutc = p.const2sb(boutc_d, [128, KC], F32, "boutc")
    lnwb = p.const2sb(lnwb_d, [128, D], F32, "lnwb")
    lnbb = p.const2sb(lnbb_d, [128, D], F32, "lnbb")
    emit_l1_attn(p, xin, xout, Win, Wout, binc, boutc, lnwb, lnbb,
                 cost_d, sint_d, p.onesf, _GROUPS)
    finish(p)
    return p.build()


def _build_fused_prog():
    p = P()
    nc = p.nc
    xin = _ein(nc, "xin", [R, D])
    xhalo = _ein(nc, "xhalo", [3, D])
    Wq, Wk, Wv, Ww = (_ein(nc, n, [D, D]) for n in ("Wq", "Wk", "Wv", "Ww"))
    Wf1, Wf2 = _ein(nc, "Wf1", [D, 128]), _ein(nc, "Wf2", [128, D])
    Wb = _ein(nc, "Wb", [D, 16])
    Wg1, Wg2 = _ein(nc, "Wg1", [D, 128]), _ein(nc, "Wg2", [128, D])
    Wo = _ein(nc, "Wo", [D, D])
    cqd, ckd, cvd = (_ein(nc, n, [128, 64]) for n in ("cqd", "ckd", "cvd"))
    bg2d, nwd = _ein(nc, "bg2d", [128, D]), _ein(nc, "nwd", [128, D])
    trixd = _ein(nc, "trixd", [128, 64])
    maskd = _ein(nc, "maskd", [128, 64], BF16)
    cmd = _ein(nc, "cmd", [128, 8])
    onf2 = _ein(nc, "onf2", [128, 128])
    f0g = _ein(nc, "f0g", [D, FFN_I])
    f0u = _ein(nc, "f0u", [D, FFN_I])
    f0d = _ein(nc, "f0d", [FFN_I, D])
    f0gs, f0us = _ein(nc, "f0gs", [128, NIC]), _ein(nc, "f0us", [128, NIC])
    f0ds = _ein(nc, "f0ds", [128, KC])
    Win = _ein(nc, "Win", [D, 3 * D])
    Wout = _ein(nc, "Wout", [D, D])
    binc_d = _ein(nc, "binc", [128, 48])
    boutc_d = _ein(nc, "boutc", [128, KC])
    lnwb_d, lnbb_d = _ein(nc, "lnwb", [128, D]), _ein(nc, "lnbb", [128, D])
    cost_d = _ein(nc, "cost", [1024, R], BF16)
    sint_d = _ein(nc, "sint", [1024, R], BF16)
    psw = _ein(nc, "psw", [128, 128], BF16)
    onf = _ein(nc, "onf", [1, 128])
    f1g = _ein(nc, "f1g", [D, FFN_I], I32)
    f1u = _ein(nc, "f1u", [D, FFN_I], I32)
    f1d = _ein(nc, "f1d", [FFN_I, D], I32)
    f1gs, f1us = _ein(nc, "f1gs", [128, NIC]), _ein(nc, "f1us", [128, NIC])
    f1ds = _ein(nc, "f1ds", [128, KC])
    idc = _ein(nc, "idc", [128, 128], BF16)
    onc = _ein(nc, "onc", [128, 128], BF16)
    cvec = _ein(nc, "cvec", [128, NCONST])
    xout = nc.dram_tensor("xout", [R, D], F32, kind="ExternalOutput").ap()
    x1 = p.dram("x1", [R, D], F32)
    x2 = p.dram("x2", [R, D], F32)
    x3 = p.dram("x3", [R, D], F32)

    emit_setup(p, cvec, idc, onc, psw, onf)

    with p.block_scope():
        cq2 = p.const2sb_blk(cqd, [128, 64], F32, "cq2", "dc1")
        ck2 = p.const2sb_blk(ckd, [128, 64], F32, "ck2", "dc1")
        cv2 = p.const2sb_blk(cvd, [128, 64], F32, "cv2", "dc1")
        bg2b = p.const2sb_blk(bg2d, [128, D], F32, "bg2b", "dc1")
        nwb = p.const2sb_blk(nwd, [128, D], F32, "nwb", "dc1")
        trix = p.const2sb_blk(trixd, [128, 64], F32, "trix", "dc1")
        masksu = p.const2sb_blk(maskd, [128, 64], BF16, "masksu", "dc1")
        cmask = p.const2sb_blk(cmd, [128, 8], F32, "cmask", "dc1")
        onesf2 = p.const2sb_blk(onf2, [128, 128], F32, "onesf2", "dc1")
        p.wait_blk_consts("dc1")
        _emit_l0_attn_impl(p, xin, xhalo, x1, Wq, Wk, Wv, Ww, Wf1, Wf2, Wb, Wg1, Wg2,
                           Wo, cq2, ck2, cv2, bg2b, nwb, trix, masksu, cmask, onesf2,
                           _GROUPS)
    with p.block_scope():
        f0gsc = p.const2sb_blk(f0gs, [128, NIC], F32, "f0gsc", "dc2")
        f0usc = p.const2sb_blk(f0us, [128, NIC], F32, "f0usc", "dc2")
        f0dsc = p.const2sb_blk(f0ds, [128, KC], F32, "f0dsc", "dc2")
        p.wait_blk_consts("dc2")
        _emit_ffn_impl(p, x1, x2, f0g, f0u, f0d, f0gsc, f0usc, f0dsc)
    with p.block_scope():
        binc = p.const2sb_blk(binc_d, [128, 48], F32, "binc", "dc3")
        boutc = p.const2sb_blk(boutc_d, [128, KC], F32, "boutc", "dc3")
        lnwb = p.const2sb_blk(lnwb_d, [128, D], F32, "lnwb", "dc3")
        lnbb = p.const2sb_blk(lnbb_d, [128, D], F32, "lnbb", "dc3")
        p.wait_blk_consts("dc3")
        _emit_l1_attn_impl(p, x2, x3, Win, Wout, binc, boutc, lnwb, lnbb,
                           cost_d, sint_d, p.onesf, _GROUPS)
    with p.block_scope():
        f1gsc = p.const2sb_blk(f1gs, [128, NIC], F32, "f1gsc", "dc4")
        f1usc = p.const2sb_blk(f1us, [128, NIC], F32, "f1usc", "dc4")
        f1dsc = p.const2sb_blk(f1ds, [128, KC], F32, "f1dsc", "dc4")
        p.wait_blk_consts("dc4")
        _emit_ffn_impl(p, x3, xout, f1g, f1u, f1d, f1gsc, f1usc, f1dsc, zero_point=True)
    finish(p)
    return p.build()


def _fused_maps(inputs, x):
    common = dict(idc=host_consts(), onc=host_ones(), cvec=host_cvec())
    g0 = np.ascontiguousarray(np.asarray(inputs["L0_ffn_gate_w"]).T)
    u0 = np.ascontiguousarray(np.asarray(inputs["L0_ffn_up_w"]).T)
    d0 = np.ascontiguousarray(np.asarray(inputs["L0_ffn_down_w"]).T)
    g1 = np.ascontiguousarray(np.asarray(inputs["L1_ffn_wq_gate"]).T)
    u1 = np.ascontiguousarray(np.asarray(inputs["L1_ffn_wq_up"]).T)
    d1 = np.ascontiguousarray(np.asarray(inputs["L1_ffn_wq_down"]).T)
    maps = []
    for c in range(NCORES):
        xh = x[c * R - 3:c * R] if c > 0 else np.zeros((3, D), np.float32)
        cos, sin = host_rope_tables(c)
        maps.append(dict(
            xin=x[c * R:(c + 1) * R], xhalo=np.ascontiguousarray(xh),
            Wq=inputs["L0_attn_Wq"], Wk=inputs["L0_attn_Wk"], Wv=inputs["L0_attn_Wv"],
            Ww=inputs["L0_attn_Ww"], Wf1=inputs["L0_attn_Wf1"], Wf2=inputs["L0_attn_Wf2"],
            Wb=inputs["L0_attn_Wb"], Wg1=inputs["L0_attn_Wg1"], Wg2=inputs["L0_attn_Wg2"],
            Wo=inputs["L0_attn_Wo"],
            cqd=host_conv2d(np.asarray(inputs["L0_attn_cq"])),
            ckd=host_conv2d(np.asarray(inputs["L0_attn_ck"])),
            cvd=host_conv2d(np.asarray(inputs["L0_attn_cv"])),
            bg2d=bcast_row(np.asarray(inputs["L0_attn_bg2"])),
            nwd=bcast_row(np.tile(np.asarray(inputs["L0_attn_norm_w"]), 16)),
            trixd=host_trix(), maskd=host_masksu(), cmd=host_cmask(c),
            onf2=host_onesf2(),
            f0g=g0, f0u=u0, f0d=d0,
            f0gs=scale_2d(np.asarray(inputs["L0_ffn_gate_s"])),
            f0us=scale_2d(np.asarray(inputs["L0_ffn_up_s"])),
            f0ds=scale_2d(np.asarray(inputs["L0_ffn_down_s"])),
            Win=inputs["L1_attn_W_in"], Wout=inputs["L1_attn_W_out"],
            binc=scale_2d(np.asarray(inputs["L1_attn_b_in"])),
            boutc=scale_2d(np.asarray(inputs["L1_attn_b_out"])),
            lnwb=bcast_row(np.asarray(inputs["L1_attn_ln_w"])),
            lnbb=bcast_row(np.asarray(inputs["L1_attn_ln_b"])),
            cost=cos, sint=sin, psw=host_pswap(), onf=host_onesf(),
            f1g=g1, f1u=u1, f1d=d1,
            f1gs=scale_2d(np.asarray(inputs["L1_ffn_s_gate"])),
            f1us=scale_2d(np.asarray(inputs["L1_ffn_s_up"])),
            f1ds=scale_2d(np.asarray(inputs["L1_ffn_s_down"])),
            **common))
    return maps


FUSED = True


_PROGS = {}


def _prog(key, builder):
    if key not in _PROGS:
        _PROGS[key] = builder()
    return _PROGS[key]


def kernel(**inputs):
    x = np.ascontiguousarray(np.asarray(inputs["L0_attn_hidden_states"],
                                        np.float32).reshape(NCORES * R, D))
    if FUSED:
        ncf = _prog("fused", _build_fused_prog)
        res = run_bass_kernel_spmd(ncf, _fused_maps(inputs, x), core_ids=list(range(NCORES)))
        x4 = np.concatenate([res.results[c]["xout"] for c in range(NCORES)], axis=0)
        return x4.reshape(1, NCORES * R, D)
    shards = lambda arr: [arr[c * R:(c + 1) * R] for c in range(NCORES)]
    common = dict(idc=host_consts(), onc=host_ones(), cvec=host_cvec())

    nc1 = _prog("l0a", _build_l0attn_prog)
    maps = []
    for c in range(NCORES):
        xh = x[c * R - 3:c * R] if c > 0 else np.zeros((3, D), np.float32)
        maps.append(dict(
            xin=x[c * R:(c + 1) * R], xhalo=np.ascontiguousarray(xh),
            Wq=inputs["L0_attn_Wq"], Wk=inputs["L0_attn_Wk"], Wv=inputs["L0_attn_Wv"],
            Ww=inputs["L0_attn_Ww"], Wf1=inputs["L0_attn_Wf1"], Wf2=inputs["L0_attn_Wf2"],
            Wb=inputs["L0_attn_Wb"], Wg1=inputs["L0_attn_Wg1"], Wg2=inputs["L0_attn_Wg2"],
            Wo=inputs["L0_attn_Wo"],
            cqd=host_conv2d(np.asarray(inputs["L0_attn_cq"])),
            ckd=host_conv2d(np.asarray(inputs["L0_attn_ck"])),
            cvd=host_conv2d(np.asarray(inputs["L0_attn_cv"])),
            bg2d=bcast_row(np.asarray(inputs["L0_attn_bg2"])),
            nwd=bcast_row(np.tile(np.asarray(inputs["L0_attn_norm_w"]), 16)),
            trixd=host_trix(), maskd=host_masksu(), cmd=host_cmask(c),
            onf2=host_onesf2(), **common))
    res = run_bass_kernel_spmd(nc1, maps, core_ids=list(range(NCORES)))
    x1 = np.concatenate([res.results[c]["xout"] for c in range(NCORES)], axis=0)

    nc2 = _prog("l0f", lambda: _build_ffn_prog(False))
    gwT = np.ascontiguousarray(np.asarray(inputs["L0_ffn_gate_w"]).T)
    uwT = np.ascontiguousarray(np.asarray(inputs["L0_ffn_up_w"]).T)
    dwT = np.ascontiguousarray(np.asarray(inputs["L0_ffn_down_w"]).T)
    maps = [dict(xin=s, gwT=gwT, uwT=uwT, dwT=dwT,
                 gs=scale_2d(np.asarray(inputs["L0_ffn_gate_s"])),
                 us=scale_2d(np.asarray(inputs["L0_ffn_up_s"])),
                 ds=scale_2d(np.asarray(inputs["L0_ffn_down_s"])), **common)
            for s in shards(x1)]
    res = run_bass_kernel_spmd(nc2, maps, core_ids=list(range(NCORES)))
    x2 = np.concatenate([res.results[c]["xout"] for c in range(NCORES)], axis=0)

    nc3 = _prog("l1a", _build_l1attn_prog)
    maps = []
    for c, s in enumerate(shards(x2)):
        cos, sin = host_rope_tables(c)
        maps.append(dict(
            xin=s, Win=inputs["L1_attn_W_in"], Wout=inputs["L1_attn_W_out"],
            binc=scale_2d(np.asarray(inputs["L1_attn_b_in"])),
            boutc=scale_2d(np.asarray(inputs["L1_attn_b_out"])),
            lnwb=bcast_row(np.asarray(inputs["L1_attn_ln_w"])),
            lnbb=bcast_row(np.asarray(inputs["L1_attn_ln_b"])),
            cost=cos, sint=sin, psw=host_pswap(), onf=host_onesf(), **common))
    res = run_bass_kernel_spmd(nc3, maps, core_ids=list(range(NCORES)))
    x3 = np.concatenate([res.results[c]["xout"] for c in range(NCORES)], axis=0)

    nc4 = _prog("l1f", lambda: _build_ffn_prog(True))
    gwT = np.ascontiguousarray(np.asarray(inputs["L1_ffn_wq_gate"]).T)
    uwT = np.ascontiguousarray(np.asarray(inputs["L1_ffn_wq_up"]).T)
    dwT = np.ascontiguousarray(np.asarray(inputs["L1_ffn_wq_down"]).T)
    maps = [dict(xin=s, gwT=gwT, uwT=uwT, dwT=dwT,
                 gs=scale_2d(np.asarray(inputs["L1_ffn_s_gate"])),
                 us=scale_2d(np.asarray(inputs["L1_ffn_s_up"])),
                 ds=scale_2d(np.asarray(inputs["L1_ffn_s_down"])), **common)
            for s in shards(x3)]
    res = run_bass_kernel_spmd(nc4, maps, core_ids=list(range(NCORES)))
    x4 = np.concatenate([res.results[c]["xout"] for c in range(NCORES)], axis=0)
    return x4.reshape(1, NCORES * R, D)
